# version 07
# baseline (speedup 1.0000x reference)
import numpy as np
import ml_dtypes
import concourse.bass as bass
import concourse.mybir as mybir
from concourse.bass_utils import run_bass_kernel_spmd
from concourse.expressions import RuntimeValue
from contextlib import ExitStack

F32 = mybir.dt.float32
BF16 = mybir.dt.bfloat16
I32 = mybir.dt.int32
AF = mybir.ActivationFunctionType
ALU = mybir.AluOpType
AX = mybir.AxisListType

B, S, C = 4, 4096, 2048
T = 2048
HS, H, NE = 2048, 1024, 8
N5 = 512
CT = C // 128
NT = T // N5
SNT = S // N5
NQM = S // N5
NKT = S // 128
HT, HTS = H // 128, HS // 128
ISQ = float(1.0 / np.sqrt(128.0))

UNITS_A = [dict(q=[0, 1, 2, 3], k=4, v=5), dict(q=[6, 7, 8, 9], k=10, v=11)]
F_A = 12 * 128
UNITS_B = [dict(q=[3 * h], k=3 * h + 1, v=3 * h + 2) for h in range(8)]
F_B = 24 * 128
ET = 8

PH_LN1, PH_A1, PH_R1, PH_F1, PH_R2, PH_A2, PH_R3, PH_F2 = 1, 2, 3, 4, 5, 6, 7, 8
NPH = 8
POLL_BOUND = 3000000


class Sems:
    def __init__(self, nc):
        self.nc = nc
        self.h = {}
        self.v = {}
        self.nofence = set()

    def sem(self, name):
        if name not in self.h:
            self.h[name] = self.nc.alloc_semaphore(name=name)
            self.v.setdefault(name, 0)
        return self.h[name]

    def inc(self, ins, name, k):
        ins.then_inc(self.sem(name), k)
        self.v[name] += k
        return self.v[name]

    def val(self, name):
        self.sem(name)
        return self.v[name]


class Ctx:
    def __init__(self, nc):
        self.nc = nc
        self.sm = Sems(nc)
        self.mark = {}
        self.g = nc.gpsimd
        self.sy = nc.sync
        self.t = nc.tensor
        self.v = nc.vector
        self.s = nc.scalar

    def wait(self, eng, key):
        if key in self.mark:
            name, val = self.mark[key]
            if val > 0:
                eng.wait_ge(self.sm.sem(name), val)

    def setmark(self, key, name):
        self.mark[key] = (name, self.sm.val(name))

    def fence(self):
        for eng in (self.t, self.v, self.s, self.sy, self.g):
            for name, val in sorted(self.sm.v.items()):
                if val > 0 and name not in self.sm.nofence:
                    eng.wait_ge(self.sm.sem(name), val)
        self.mark.clear()


def emit_resid_ln(cx, es, srcs, x_out, xb_sb_dst, sh_xb_dst=None, xbf_out=None,
                  do_ln=True, tag=""):
    nc, sm, t, v, s, sy, g = cx.nc, cx.sm, cx.t, cx.v, cx.s, cx.sy, cx.g
    nsrc = len(srcs)
    with ExitStack() as ph:
        row = ph.enter_context(nc.sbuf_tensor("r_row" + tag, [128, 2 * nsrc * T], F32))
        xnew = ph.enter_context(nc.sbuf_tensor("r_xnew" + tag, [128, 2 * T], F32))
        sqr = ph.enter_context(nc.sbuf_tensor("r_sqr" + tag, [128, T], F32))
        accA = ph.enter_context(nc.sbuf_tensor("r_accA" + tag, [128, T], F32))
        accB = ph.enter_context(nc.sbuf_tensor("r_accB" + tag, [128, T], F32))

        rb = ph.enter_context(nc.sbuf_tensor("r_rb" + tag, [128, 2 * N5], F32))
        mrb = ph.enter_context(nc.sbuf_tensor("r_mrb" + tag, [128, 2 * N5], F32))
        xf1 = ph.enter_context(nc.sbuf_tensor("r_xf1" + tag, [128, N5], F32))
        xbrow = ph.enter_context(nc.sbuf_tensor("r_xbrow" + tag, [128, 2 * T], BF16))
        xfrow = ph.enter_context(nc.sbuf_tensor("r_xfrow" + tag, [128, 2 * T], F32))
        pln = ph.enter_context(nc.psum_tensor("r_pln" + tag, [1, N5], F32))
        pbc = ph.enter_context(nc.psum_tensor("r_pbc" + tag, [128, 2 * N5], F32))

        ld = "rl_ld"

        for ct in range(CT):
            par = ct % 2
            cx.wait(sy, "rl_rowfree_%d" % par)
            for si, src in enumerate(srcs):
                sm.inc(sy.dma_start(row[:, (par * nsrc + si) * T:(par * nsrc + si + 1) * T], src[ct]),
                       ld + str(par), 16)
            v.wait_ge(sm.sem(ld + str(par)), sm.val(ld + str(par)))
            cx.wait(v, "rl_xnfree_%d" % par)
            base = par * nsrc * T
            if nsrc == 1:
                i = v.tensor_copy(xnew[:, par * T:(par + 1) * T], row[:, base:base + T])
            elif nsrc == 2:
                i = v.tensor_tensor(xnew[:, par * T:(par + 1) * T], row[:, base:base + T],
                                    row[:, base + T:base + 2 * T], ALU.add)
            else:
                vv = sm.inc(v.tensor_tensor(xnew[:, par * T:(par + 1) * T], row[:, base:base + T],
                                            row[:, base + T:base + 2 * T], ALU.add), "rl_vv", 1)
                v.wait_ge(sm.sem("rl_vv"), vv)
                i = v.tensor_tensor(xnew[:, par * T:(par + 1) * T], xnew[:, par * T:(par + 1) * T],
                                    row[:, base + 2 * T:base + 3 * T], ALU.add)
            vv = sm.inc(i, "rl_vv", 1)
            cx.setmark("rl_rowfree_%d" % par, ld + str(par))
            v.wait_ge(sm.sem("rl_vv"), vv)
            sy.wait_ge(sm.sem("rl_vv"), vv)
            sm.inc(sy.dma_start(x_out[ct], xnew[:, par * T:(par + 1) * T]), "rl_st%d" % par, 16)
            cx.setmark("rl_rowfree_%d" % par, "rl_vv")
            self_mark_val = sm.v["rl_vv"]
            if do_ln:
                s.wait_ge(sm.sem("rl_vv"), vv)
                cx.wait(s, "rl_sqfree")
                svv = sm.inc(s.activation(sqr[:, :], xnew[:, par * T:(par + 1) * T], AF.Square), "rl_sv", 1)
                if ct == 0:
                    i = v.tensor_copy(accA[:, :], xnew[:, par * T:(par + 1) * T])
                else:
                    i = v.tensor_tensor(accA[:, :], accA[:, :], xnew[:, par * T:(par + 1) * T], ALU.add)
                vv2 = sm.inc(i, "rl_vv", 1)
                v.wait_ge(sm.sem("rl_sv"), svv)
                v.wait_ge(sm.sem("rl_vv"), vv2)
                if ct == 0:
                    i = v.tensor_copy(accB[:, :], sqr[:, :])
                else:
                    i = v.tensor_tensor(accB[:, :], accB[:, :], sqr[:, :], ALU.add)
                vv3 = sm.inc(i, "rl_vv", 1)
                cx.setmark("rl_sqfree", "rl_vv")
                v.wait_ge(sm.sem("rl_vv"), vv3)
            cx.setmark("rl_xnfree_%d" % par, "rl_st%d" % par)

        if not do_ln:
            sy.wait_ge(sm.sem("rl_st0"), sm.val("rl_st0"))
            sy.wait_ge(sm.sem("rl_st1"), sm.val("rl_st1"))
            return

        def edge(ins):
            vv = sm.inc(ins, "rl_vv", 1)
            v.wait_ge(sm.sem("rl_vv"), vv)

        for acc, dst in ((accA, sqr), (accB, None)):
            for nt in range(NT):
                cx.wait(t, "rl_plnfree")
                t.wait_ge(sm.sem("rl_vv"), sm.val("rl_vv"))
                t.wait_ge(sm.sem("rl_sv"), sm.val("rl_sv"))
                i = t.matmul(pln[0:1, :], cx.onesf[:, :], acc[:, nt * N5:(nt + 1) * N5],
                             start=True, stop=True)
                mmv = sm.inc(i, "rl_mm", 1)
                v.wait_ge(sm.sem("rl_mm"), mmv)
                tgt = sqr[0:1, nt * N5:(nt + 1) * N5] if acc is accA else accB[0:1, nt * N5:(nt + 1) * N5]
                edge(v.tensor_copy(tgt, pln[0:1, :]))
                cx.setmark("rl_plnfree", "rl_vv")
        c1 = float(1.0 / C)
        edge(v.tensor_scalar(sqr[0:1, 0:T], sqr[0:1, 0:T], c1, None, ALU.mult))
        edge(v.tensor_tensor(accA[0:1, :], sqr[0:1, 0:T], sqr[0:1, 0:T], ALU.mult))
        edge(v.scalar_tensor_tensor(accB[0:1, :], accB[0:1, :], c1, accA[0:1, :],
                                    ALU.mult, ALU.subtract))
        edge(v.tensor_scalar(accB[0:1, :], accB[0:1, :], 1e-5, None, ALU.add))
        s.wait_ge(sm.sem("rl_vv"), sm.val("rl_vv"))
        svv = sm.inc(s.activation(accB[0:1, :], accB[0:1, :], AF.Ln), "rl_sv", 1)
        s.wait_ge(sm.sem("rl_sv"), svv)
        svv = sm.inc(s.activation(accB[0:1, :], accB[0:1, :], AF.Exp, scale=-0.5), "rl_sv", 1)
        v.wait_ge(sm.sem("rl_sv"), svv)
        edge(v.tensor_tensor(accA[0:1, :], sqr[0:1, 0:T], accB[0:1, :], ALU.mult))
        t.wait_ge(sm.sem("rl_vv"), sm.val("rl_vv"))

        sy.wait_ge(sm.sem("rl_st0"), sm.val("rl_st0"))
        sy.wait_ge(sm.sem("rl_st1"), sm.val("rl_st1"))
        for nt in range(NT):
            bpar = nt % 2
            cx.wait(t, "rl_bcfree_%d" % bpar)
            i = t.matmul(pbc[:, bpar * N5:(bpar + 1) * N5], cx.ones[0:1, :],
                         accB[0:1, nt * N5:(nt + 1) * N5], start=True, stop=True)
            mmv = sm.inc(i, "rl_mm", 1)
            v.wait_ge(sm.sem("rl_mm"), mmv)
            edge(v.tensor_copy(rb[:, bpar * N5:(bpar + 1) * N5], pbc[:, bpar * N5:(bpar + 1) * N5]))
            t.wait_ge(sm.sem("rl_vv"), sm.val("rl_vv"))
            i = t.matmul(pbc[:, bpar * N5:(bpar + 1) * N5], cx.ones[0:1, :],
                         accA[0:1, nt * N5:(nt + 1) * N5], start=True, stop=True)
            mmv = sm.inc(i, "rl_mm", 1)
            v.wait_ge(sm.sem("rl_mm"), mmv)
            edge(v.tensor_copy(mrb[:, bpar * N5:(bpar + 1) * N5], pbc[:, bpar * N5:(bpar + 1) * N5]))
            cx.setmark("rl_bcfree_%d" % bpar, "rl_vv")
            for ct in range(CT):
                par = ct % 2
                cx.wait(sy, "rl_xffree_%d" % par)
                sm.inc(sy.dma_start(xfrow[:, par * T + nt * N5:par * T + (nt + 1) * N5],
                                    x_out[ct][:, nt * N5:(nt + 1) * N5]), "rl_xf%d" % par, 16)
                v.wait_ge(sm.sem("rl_xf%d" % par), sm.val("rl_xf%d" % par))
                vv = sm.inc(v.tensor_tensor(xf1[:, :], xfrow[:, par * T + nt * N5:par * T + (nt + 1) * N5],
                                            rb[:, bpar * N5:(bpar + 1) * N5], ALU.mult), "rl_vv", 1)
                v.wait_ge(sm.sem("rl_vv"), vv)
                cx.wait(v, "rl_xbfree_%d" % par)
                if xbf_out is not None:
                    vv = sm.inc(v.tensor_tensor(xfrow[:, par * T + nt * N5:par * T + (nt + 1) * N5],
                                                xf1[:, :], mrb[:, bpar * N5:(bpar + 1) * N5],
                                                ALU.subtract), "rl_vv", 1)
                    v.wait_ge(sm.sem("rl_vv"), vv)
                    sy.wait_ge(sm.sem("rl_vv"), vv)
                    sm.inc(sy.dma_start(xbf_out[ct][:, nt * N5:(nt + 1) * N5],
                                        xfrow[:, par * T + nt * N5:par * T + (nt + 1) * N5]),
                           "rl_xbf%d" % par, 16)
                    cx.setmark("rl_xffree_%d" % par, "rl_xbf%d" % par)
                    i = v.tensor_copy(xbrow[:, par * T + nt * N5:par * T + (nt + 1) * N5],
                                      xfrow[:, par * T + nt * N5:par * T + (nt + 1) * N5])
                else:
                    cx.setmark("rl_xffree_%d" % par, "rl_vv")
                    i = v.tensor_tensor(xbrow[:, par * T + nt * N5:par * T + (nt + 1) * N5],
                                        xf1[:, :], mrb[:, bpar * N5:(bpar + 1) * N5], ALU.subtract)
                vv = sm.inc(i, "rl_vv", 1)
                if xb_sb_dst is not None:
                    v.wait_ge(sm.sem("rl_vv"), vv)
                    vv = sm.inc(v.tensor_copy(xb_sb_dst[:, ct * T + nt * N5:ct * T + (nt + 1) * N5],
                                              xbrow[:, par * T + nt * N5:par * T + (nt + 1) * N5]),
                                "rl_vv", 1)
                if sh_xb_dst is not None:
                    sy.wait_ge(sm.sem("rl_vv"), vv)
                    sm.inc(sy.dma_start(sh_xb_dst(ct, nt),
                                        xbrow[:, par * T + nt * N5:par * T + (nt + 1) * N5]),
                           "rl_shxb%d" % par, 16)
                    cx.setmark("rl_xbfree_%d" % par, "rl_shxb%d" % par)
                else:
                    cx.setmark("rl_xbfree_%d" % par, "rl_vv")


def emit_attn(cx, es, units, FF, xb_src3, wqkv_h, wo_h, qkv_h, shp_dst2, tag=""):
    nc, sm, t, v, s, sy, g = cx.nc, cx.sm, cx.t, cx.v, cx.s, cx.sy, cx.g
    OF = FF // 128
    NCH = OF * 128 // N5
    with ExitStack() as ph:
        arena = ph.enter_context(nc.sbuf_tensor("a_arena" + tag, [128, CT * S], BF16))
        wch = ph.enter_context(nc.sbuf_tensor("a_wch" + tag, [128, 2 * CT * N5], BF16))
        qrow = ph.enter_context(nc.sbuf_tensor("a_qrow" + tag, [128, 2 * S], BF16))
        sq = ph.enter_context(nc.sbuf_tensor("a_sq" + tag, [128, 4 * N5], BF16))
        nred = ph.enter_context(nc.sbuf_tensor("a_nred" + tag, [1, 8], F32))

        nmax = ph.enter_context(nc.sbuf_tensor("a_nmax" + tag, [1, 32], F32))
        csb = ph.enter_context(nc.sbuf_tensor("a_c" + tag, [1, 4], F32))
        negc = ph.enter_context(nc.sbuf_tensor("a_negc" + tag, [128, 1], F32))
        pch = ph.enter_context(nc.sbuf_tensor("a_pch" + tag, [128, 3 * N5], BF16))
        linv = ph.enter_context(nc.sbuf_tensor("a_linv" + tag, [1, 2 * N5], F32))
        lb = ph.enter_context(nc.sbuf_tensor("a_lb" + tag, [128, N5], F32))
        stg = ph.enter_context(nc.sbuf_tensor("a_stg" + tag, [128, 4 * N5], F32))
        pS = ph.enter_context(nc.psum_tensor("a_pS" + tag, [128, 2 * N5], F32))
        pAV = ph.enter_context(nc.psum_tensor("a_pAV" + tag, [128, 2 * N5], F32))
        pl = ph.enter_context(nc.psum_tensor("a_pl" + tag, [1, 2 * N5], F32))
        pb = ph.enter_context(nc.psum_tensor("a_pb" + tag, [128, N5], F32))
        pvt = ph.enter_context(nc.psum_tensor("a_pvt" + tag, [128, 128], BF16))

        A_AO = 0
        A_KT = ET * S
        A_VT = A_KT + S
        A_VTK = A_VT + S
        A_QT = A_VTK + S
        A_WO = A_QT + S

        qkv3 = qkv_h.rearrange("(of p) t -> of p t", p=128)

        sm.inc(g.dma_start(arena[:, 0:CT * S].rearrange("p (ct t) -> p ct t", ct=CT),
                           xb_src3.rearrange("ct p t -> p ct t")), "a_xbld", 16)
        for eng in (t, v, s):
            eng.wait_ge(sm.sem("a_xbld"), sm.val("a_xbld"))

        for ch in range(NCH):
            par = ch % 2
            wofs = par * CT * N5
            cx.wait(g, "a_wchfree_%d" % par)
            wname = "a_wld%d" % par
            sm.inc(g.dma_start(
                wch[:, wofs:wofs + CT * N5].rearrange("p (ct f) -> p ct f", ct=CT),
                wqkv_h.rearrange("(ct p) f -> p ct f", p=128)[:, :, ch * N5:(ch + 1) * N5]), wname, 16)
            t.wait_ge(sm.sem(wname), sm.val(wname))
            for fo in range(4):
                of = ch * 4 + fo
                qpar = of % 2
                cx.wait(s, "a_qrowfree_%d" % qpar)
                for nt in range(SNT):
                    ppar = nt % 2
                    cx.wait(t, "a_psfree_%d" % ppar)
                    for ct in range(CT):
                        i = t.matmul(
                            pS[:, ppar * N5:(ppar + 1) * N5],
                            wch[:, wofs + ct * N5 + fo * 128: wofs + ct * N5 + fo * 128 + 128],
                            arena[:, ct * S + nt * N5: ct * S + (nt + 1) * N5],
                            start=(ct == 0), stop=(ct == CT - 1))
                    mmv = sm.inc(i, "a_mm", 1)
                    s.wait_ge(sm.sem("a_mm"), mmv)
                    sv = sm.inc(s.activation(qrow[:, qpar * S + nt * N5: qpar * S + (nt + 1) * N5],
                                             pS[:, ppar * N5:(ppar + 1) * N5], AF.Copy), "a_sv", 1)
                    cx.setmark("a_psfree_%d" % ppar, "a_sv")
                    if nt % 4 == 0:
                        cx.wait(s, "a_sqfree")
                    sv2 = sm.inc(s.activation(sq[:, (nt % 4) * N5:((nt % 4) + 1) * N5],
                                              pS[:, ppar * N5:(ppar + 1) * N5], AF.Square), "a_sv", 1)
                    cx.setmark("a_psfree_%d" % ppar, "a_sv")
                    if nt % 4 == 3:
                        t.wait_ge(sm.sem("a_sv"), sv2)
                        half = nt // 4
                        for q4 in range(4):
                            plpar = q4 % 2
                            cx.wait(t, "a_pnofree_%d" % plpar)
                            i = t.matmul(pl[0:1, plpar * N5:(plpar + 1) * N5], cx.onesc[:, :],
                                         sq[:, q4 * N5:(q4 + 1) * N5], start=True, stop=True)
                            mmv = sm.inc(i, "a_mm", 1)
                            cx.setmark("a_sqfree", "a_mm")
                            v.wait_ge(sm.sem("a_mm"), mmv)
                            vv = sm.inc(v.tensor_copy(stg[0:1, q4 * N5:(q4 + 1) * N5],
                                                      pl[0:1, plpar * N5:(plpar + 1) * N5]), "a_vv", 1)
                            cx.setmark("a_pnofree_%d" % plpar, "a_vv")
                        v.wait_ge(sm.sem("a_vv"), sm.val("a_vv"))
                        vv = sm.inc(v.reduce_max(nred[0:1, 0:4],
                                                 stg[0:1, 0:4 * N5].rearrange("o (a b) -> o a b", a=4), axis=AX.X), "a_vv", 1)
                        v.wait_ge(sm.sem("a_vv"), vv)
                        if half == 0:
                            vv = sm.inc(v.reduce_max(nmax[0:1, of:of + 1], nred[0:1, 0:4], axis=AX.X), "a_vv", 1)
                        else:
                            vv = sm.inc(v.reduce_max(nred[0:1, 4:5], nred[0:1, 0:4], axis=AX.X), "a_vv", 1)
                            v.wait_ge(sm.sem("a_vv"), vv)
                            vv = sm.inc(v.tensor_tensor(nmax[0:1, of:of + 1], nmax[0:1, of:of + 1],
                                                        nred[0:1, 4:5], ALU.max), "a_vv", 1)
                        v.wait_ge(sm.sem("a_vv"), vv)
                sy.wait_ge(sm.sem("a_sv"), sm.val("a_sv"))
                sm.inc(sy.dma_start(qkv3[of], qrow[:, qpar * S:(qpar + 1) * S]), "a_qst%d" % qpar, 16)
                cx.setmark("a_qrowfree_%d" % qpar, "a_qst%d" % qpar)
            cx.setmark("a_wchfree_%d" % par, "a_mm")

        sy.wait_ge(sm.sem("a_qst0"), sm.val("a_qst0"))
        sy.wait_ge(sm.sem("a_qst1"), sm.val("a_qst1"))
        g.wait_ge(sm.sem("a_mm"), sm.val("a_mm"))
        sm.inc(g.dma_start(arena[:, A_WO:A_WO + ET * CT * 128].rearrange("p (et ct c) -> p et ct c", et=ET, ct=CT),
                           wo_h.rearrange("(et p) (ct c) -> p et ct c", p=128, c=128)), "a_wold", 16)
        sy.wait_ge(sm.sem("a_mm"), sm.val("a_mm"))
        v.wait_ge(sm.sem("a_mm"), sm.val("a_mm"))

        for ui, u in enumerate(units):
            cx.wait(sy, "a_kvfree")
            sm.inc(sy.dma_start(arena[:, A_KT:A_KT + S], qkv3[u["k"]]), "a_kvld", 16)
            sm.inc(sy.dma_start(arena[:, A_VT:A_VT + S], qkv3[u["v"]]), "a_kvld", 16)
            t.wait_ge(sm.sem("a_kvld"), sm.val("a_kvld"))
            for kt in range(NKT):
                cx.wait(t, "a_pvtfree")
                i = t.transpose(pvt[:, :], arena[:, A_VT + kt * 128:A_VT + (kt + 1) * 128], cx.identb[:, :])
                mmv = sm.inc(i, "a_mm", 1)
                v.wait_ge(sm.sem("a_mm"), mmv)
                vv = sm.inc(v.tensor_copy(arena[:, A_VTK + kt * 128:A_VTK + (kt + 1) * 128], pvt[:, :]),
                            "a_vtev", 1)
                cx.setmark("a_pvtfree", "a_vtev")
            t.wait_ge(sm.sem("a_vtev"), sm.val("a_vtev"))
            for qi, qof in enumerate(u["q"]):
                head_et = ui * len(u["q"]) + qi
                cx.wait(sy, "a_qfree")
                qv = sm.inc(sy.dma_start(arena[:, A_QT:A_QT + S], qkv3[qof]), "a_qld", 16)
                t.wait_ge(sm.sem("a_qld"), qv)

                def edge(ins):
                    vv = sm.inc(ins, "a_rt", 1)
                    v.wait_ge(sm.sem("a_rt"), vv)
                edge(v.tensor_tensor(csb[0:1, 0:1], nmax[0:1, qof:qof + 1],
                                     nmax[0:1, u["k"]:u["k"] + 1], ALU.mult))
                s.wait_ge(sm.sem("a_rt"), sm.val("a_rt"))
                sv = sm.inc(s.activation(csb[0:1, 1:2], csb[0:1, 0:1], AF.Sqrt, scale=1.0 / 128.0), "a_sv", 1)
                v.wait_ge(sm.sem("a_sv"), sv)
                edge(v.tensor_scalar(csb[0:1, 2:3], csb[0:1, 1:2], -1.0, None, ALU.mult))
                t.wait_ge(sm.sem("a_rt"), sm.val("a_rt"))
                cx.wait(t, "a_pbfree")
                i = t.matmul(pb[:, 0:1], cx.ones[0:1, :], csb[0:1, 2:3], start=True, stop=True)
                mmv = sm.inc(i, "a_mm", 1)
                v.wait_ge(sm.sem("a_mm"), mmv)
                vv = sm.inc(v.tensor_copy(negc[:, :], pb[:, 0:1]), "a_nc", 1)
                cx.setmark("a_pbfree", "a_nc")
                s.wait_ge(sm.sem("a_nc"), sm.val("a_nc"))
                pending_fin = None

                def emit_fin(fin):
                    qmf, avparf, head_etf, rtv = fin
                    t.wait_ge(sm.sem("a_rt"), rtv)
                    cx.wait(t, "a_pbfree")
                    i = t.matmul(pb[:, 0:N5], cx.ones[0:1, :], linv[0:1, avparf * N5:(avparf + 1) * N5],
                                 start=True, stop=True)
                    mmv = sm.inc(i, "a_mm", 1)
                    v.wait_ge(sm.sem("a_mm"), mmv)
                    vv = sm.inc(v.tensor_copy(lb[:, :], pb[:, 0:N5]), "a_rt", 1)
                    cx.setmark("a_pbfree", "a_rt")
                    v.wait_ge(sm.sem("a_rt"), vv)
                    i = v.tensor_tensor(arena[:, A_AO + head_etf * S + qmf * N5: A_AO + head_etf * S + (qmf + 1) * N5],
                                        pAV[:, avparf * N5:(avparf + 1) * N5], lb[:, :], ALU.mult)
                    sm.inc(i, "a_aoev", 1)
                    cx.setmark("a_pavfree_%d" % avparf, "a_aoev")

                for qm in range(NQM):
                    nkt = 4 * (qm + 1)
                    avpar = qm % 2
                    cx.wait(t, "a_pavfree_%d" % avpar)
                    cx.wait(t, "a_plfree_%d" % avpar)
                    pending_av = []

                    def emit_av(av):
                        ktf, sparf, ppar2f, svf, firstf, lastf = av
                        t.wait_ge(sm.sem("a_sv"), svf)
                        i = t.matmul(pl[0:1, avpar * N5:(avpar + 1) * N5], cx.onesc[:, :],
                                     pch[:, ppar2f * N5:(ppar2f + 1) * N5],
                                     start=firstf, stop=lastf)
                        sm.inc(i, "a_mm", 1)
                        i = t.matmul(pAV[:, avpar * N5:(avpar + 1) * N5],
                                     arena[:, A_VTK + ktf * 128:A_VTK + (ktf + 1) * 128],
                                     pch[:, ppar2f * N5:(ppar2f + 1) * N5],
                                     start=firstf, stop=lastf)
                        mmv = sm.inc(i, "a_mm", 1)
                        cx.setmark("a_pchfree_%d" % ppar2f, "a_mm")

                    for kt in range(nkt):
                        spar = kt % 2
                        cx.wait(t, "a_psfree_%d" % spar)
                        i = t.matmul(pS[:, spar * N5:(spar + 1) * N5],
                                     arena[:, A_KT + kt * 128:A_KT + (kt + 1) * 128],
                                     arena[:, A_QT + qm * N5:A_QT + (qm + 1) * N5],
                                     start=True, stop=True)
                        mmv = sm.inc(i, "a_mm", 1)
                        if kt == 0 and pending_fin is not None:
                            emit_fin(pending_fin)
                            pending_fin = None
                        if len(pending_av) >= 2:
                            emit_av(pending_av.pop(0))
                        o = kt * 128 - qm * N5
                        if o >= 0:
                            mi = o // 128
                            v.wait_ge(sm.sem("a_mm"), mmv)
                            vv = sm.inc(v.tensor_tensor(pS[:, spar * N5:(spar + 1) * N5],
                                                        pS[:, spar * N5:(spar + 1) * N5],
                                                        cx.masks[:, mi * N5:(mi + 1) * N5], ALU.add), "a_msk", 1)
                            s.wait_ge(sm.sem("a_msk"), vv)
                        else:
                            s.wait_ge(sm.sem("a_mm"), mmv)
                        ppar2 = kt % 3
                        cx.wait(s, "a_pchfree_%d" % ppar2)
                        sv = sm.inc(s.activation(pch[:, ppar2 * N5:(ppar2 + 1) * N5],
                                                 pS[:, spar * N5:(spar + 1) * N5], AF.Exp,
                                                 bias=negc[:, 0:1], scale=ISQ), "a_sv", 1)
                        cx.setmark("a_psfree_%d" % spar, "a_sv")
                        pending_av.append((kt, spar, ppar2, sv, kt == 0, kt == nkt - 1))
                    for av in pending_av:
                        emit_av(av)
                    v.wait_ge(sm.sem("a_mm"), sm.val("a_mm"))
                    vv = sm.inc(v.tensor_copy(linv[0:1, avpar * N5:(avpar + 1) * N5],
                                              pl[0:1, avpar * N5:(avpar + 1) * N5]), "a_rt", 1)
                    cx.setmark("a_plfree_%d" % avpar, "a_rt")
                    v.wait_ge(sm.sem("a_rt"), vv)
                    vv = sm.inc(v.reciprocal(linv[0:1, avpar * N5:(avpar + 1) * N5],
                                             linv[0:1, avpar * N5:(avpar + 1) * N5]), "a_rt", 1)
                    pending_fin = (qm, avpar, head_et, vv)
                emit_fin(pending_fin)
                cx.setmark("a_qfree", "a_mm")
            cx.setmark("a_kvfree", "a_mm")

        t.wait_ge(sm.sem("a_aoev"), sm.val("a_aoev"))
        t.wait_ge(sm.sem("a_wold"), sm.val("a_wold"))
        for nt in range(SNT):
            for ch2 in range(8):
                slot = ch2 % 2
                for ct in range(ch2 * 2, ch2 * 2 + 2):
                    ppar = ct % 2
                    cx.wait(t, "a_psfree_%d" % ppar)
                    for et in range(ET):
                        i = t.matmul(pS[:, ppar * N5:(ppar + 1) * N5],
                                     arena[:, A_WO + (et * CT + ct) * 128:A_WO + (et * CT + ct) * 128 + 128],
                                     arena[:, A_AO + et * S + nt * N5:A_AO + et * S + (nt + 1) * N5],
                                     start=(et == 0), stop=(et == ET - 1))
                    mmv = sm.inc(i, "a_mm", 1)
                    v.wait_ge(sm.sem("a_mm"), mmv)
                    if ct % 2 == 0:
                        cx.wait(v, "a_stgfree_%d" % slot)
                    i = v.tensor_copy(stg[:, (slot * 2 + ct % 2) * N5:(slot * 2 + ct % 2 + 1) * N5],
                                      pS[:, ppar * N5:(ppar + 1) * N5])
                    sm.inc(i, "a_yev", 1)
                    cx.setmark("a_psfree_%d" % ppar, "a_yev")
                sy.wait_ge(sm.sem("a_yev"), sm.val("a_yev"))
                i = sy.dma_start(shp_dst2(ch2, nt),
                                 stg[:, slot * 2 * N5:(slot + 1) * 2 * N5].rearrange("p (a t) -> p a t", a=2))
                sm.inc(i, "a_yst", 16)
                cx.setmark("a_stgfree_%d" % slot, "a_yst")
        sy.wait_ge(sm.sem("a_yst"), sm.val("a_yst"))


def emit_ffn(cx, es, xb_sb, xbf_h, wio, y_h, scr, tag=""):
    nc, sm, t, v, s, sy, g = cx.nc, cx.sm, cx.t, cx.v, cx.s, cx.sy, cx.g
    gw_sb, = scr["gw_sb"],
    s1_h, hh_h, sho_h = scr["s1_hbm"], scr["h_hbm"], scr["shout_hbm"]
    shg_h, shu_h, shd_h = wio["shg_T"], wio["shu_T"], wio["shd_T"]
    wg_h, wu_h, wd_h = wio["wg_T"], wio["wu_T"], wio["wd_T"]
    se_sb = scr["se_sb"]
    y3 = y_h.rearrange("(ct p) t -> p ct t", p=128)
    sho3 = sho_h.rearrange("(ct p) t -> p ct t", p=128)
    s1_3 = s1_h.rearrange("(ht p) t -> ht p t", p=128)
    NTT = T // 128

    with ExitStack() as ph:
        scrr = ph.enter_context(nc.sbuf_tensor("f_scr" + tag, [128, NTT * NE], F32))
        m1 = ph.enter_context(nc.sbuf_tensor("f_m1" + tag, [128, NTT], F32))
        m2 = ph.enter_context(nc.sbuf_tensor("f_m2" + tag, [128, NTT], F32))
        tmp8 = ph.enter_context(nc.sbuf_tensor("f_tmp8" + tag, [128, NE], F32))
        ind8 = ph.enter_context(nc.sbuf_tensor("f_ind8" + tag, [128, NE], F32))
        rs = ph.enter_context(nc.sbuf_tensor("f_rs" + tag, [128, NTT], F32))
        comb = ph.enter_context(nc.sbuf_tensor("f_comb" + tag, [128, NTT * NE], F32))
        combT = ph.enter_context(nc.sbuf_tensor("f_combT" + tag, [8, T], F32))

        sgb = ph.enter_context(nc.sbuf_tensor("f_sgb" + tag, [128, N5], F32))
        cball = ph.enter_context(nc.sbuf_tensor("f_cball" + tag, [128, NT * N5], F32))
        wk = ph.enter_context(nc.sbuf_tensor("f_wk" + tag, [128, CT * N5 * 5], BF16))
        wga = wk[:, 0:2 * CT * N5]
        wub = wk[:, 2 * CT * N5:4 * CT * N5]
        hrow = ph.enter_context(nc.sbuf_tensor("f_hrow" + tag, [128, 2 * T], BF16))
        t1 = ph.enter_context(nc.sbuf_tensor("f_t1" + tag, [128, N5], BF16))
        t2 = ph.enter_context(nc.sbuf_tensor("f_t2" + tag, [128, N5], F32))
        arena = wk
        stage = ph.enter_context(nc.sbuf_tensor("f_stage" + tag, [128, 8 * N5], F32))
        shon = ph.enter_context(nc.sbuf_tensor("f_shon" + tag, [128, 8 * N5], BF16))
        pg = ph.enter_context(nc.psum_tensor("f_pg" + tag, [128, 2 * N5], F32))
        pu = ph.enter_context(nc.psum_tensor("f_pu" + tag, [128, 2 * N5], F32))
        pd = ph.enter_context(nc.psum_tensor("f_pd" + tag, [128, 2 * N5], F32))
        pb = ph.enter_context(nc.psum_tensor("f_pb" + tag, [128, N5], F32))
        pt = ph.enter_context(nc.psum_tensor("f_pt" + tag, [128, 128], F32))

        SHD, S1C = 0, CT * N5 * 4
        WD, HNT = 0, CT * N5 * 4

        xbf3 = xbf_h.rearrange("(ct p) t -> ct p t", p=128)
        for ct in range(CT):
            par = ct % 2
            cx.wait(sy, "f_xffree_%d" % par)
            xfn = "f_xfld%d" % par
            sm.inc(sy.dma_start(stage[:, par * T:par * T + T], xbf3[ct]), xfn, 16)
            t.wait_ge(sm.sem(xfn), sm.val(xfn))
            for tt in range(NTT):
                i = t.matmul(pb[:, tt * NE:(tt + 1) * NE],
                             stage[:, par * T + tt * 128: par * T + tt * 128 + 128],
                             gw_sb[:, ct * NE:(ct + 1) * NE],
                             start=(ct == 0 and tt == 0), stop=(ct == CT - 1 and tt == NTT - 1))
            mmv = sm.inc(i, "f_mm", 1)
            cx.setmark("f_xffree_%d" % par, "f_mm")
        v.wait_ge(sm.sem("f_mm"), sm.val("f_mm"))
        ev = sm.inc(v.tensor_copy(scrr[:, 0:NTT * NE], pb[:, 0:NTT * NE]), "f_ev", 1)
        t.wait_ge(sm.sem("f_ev"), ev)
        cx.setmark("f_stagefree", "f_mm")

        for nt in range(NT):
            for ct in range(CT):
                i = t.matmul(pb[0:1, 0:N5], se_sb[:, ct:ct + 1],
                             xb_sb[:, ct * T + nt * N5: ct * T + (nt + 1) * N5],
                             start=(ct == 0), stop=(ct == CT - 1))
            mmv = sm.inc(i, "f_mm", 1)
            s.wait_ge(sm.sem("f_mm"), mmv)
            sv = sm.inc(s.activation(cball[0:1, nt * N5:(nt + 1) * N5], pb[0:1, 0:N5], AF.Sigmoid), "f_sv", 1)
            t.wait_ge(sm.sem("f_sv"), sv)

        v.wait_ge(sm.sem("f_ev"), sm.val("f_ev"))
        for tt in range(NTT):
            sl = scrr[:, tt * NE:(tt + 1) * NE]
            def edge(ins):
                vv = sm.inc(ins, "f_rt", 1)
                v.wait_ge(sm.sem("f_rt"), vv)
            edge(v.reduce_max(m1[:, tt:tt + 1], sl, axis=AX.X))
            edge(v.tensor_scalar(m1[:, tt:tt + 1], m1[:, tt:tt + 1], -1.0, None, ALU.mult))
            edge(v.tensor_scalar(tmp8[:, :], sl, m1[:, tt:tt + 1], None, ALU.add))
            edge(v.tensor_scalar(ind8[:, :], tmp8[:, :], 0.0, None, ALU.is_ge))
            edge(v.scalar_tensor_tensor(t2[:, 0:NE], ind8[:, :], -1e30, tmp8[:, :], ALU.mult, ALU.add))
            edge(v.reduce_max(m2[:, tt:tt + 1], t2[:, 0:NE], axis=AX.X))
            edge(v.tensor_scalar(ind8[:, :], tmp8[:, :], m2[:, tt:tt + 1], None, ALU.is_ge))
            vv = sm.inc(v.tensor_copy(t2[:, 0:NE], tmp8[:, :]), "f_vv", 1)
            s.wait_ge(sm.sem("f_vv"), vv)
            sv = sm.inc(s.activation(tmp8[:, :], t2[:, 0:NE], AF.Exp), "f_sv", 1)
            v.wait_ge(sm.sem("f_sv"), sv)
            edge(v.tensor_tensor(tmp8[:, :], tmp8[:, :], ind8[:, :], ALU.mult))
            edge(v.reduce_sum(rs[:, tt:tt + 1], tmp8[:, :], axis=AX.X))
            edge(v.reciprocal(rs[:, tt:tt + 1], rs[:, tt:tt + 1]))
            i = v.tensor_scalar(comb[:, tt * NE:(tt + 1) * NE], tmp8[:, :], rs[:, tt:tt + 1], None, ALU.mult)
            vv = sm.inc(i, "f_vv", 1)
            t.wait_ge(sm.sem("f_vv"), vv)
            cx.wait(t, "f_ptfree")
            i = t.transpose(pt[0:NE, 0:128], comb[:, tt * NE:(tt + 1) * NE], cx.identf[:, :])
            mmv = sm.inc(i, "f_mm", 1)
            v.wait_ge(sm.sem("f_mm"), mmv)
            sm.inc(v.tensor_copy(combT[0:NE, tt * 128:(tt + 1) * 128], pt[0:NE, 0:128]), "f_ptev", 1)
            cx.setmark("f_ptfree", "f_ptev")

        def gateup(n_htile, wsrc_g, wsrc_u, hout, use_cb):
            nch = n_htile // 4
            for ch in range(nch):
                par = ch % 2
                wofs = par * CT * N5
                cx.wait(g, "f_wbfree_%d" % par)
                wname = "f_wld%d" % par
                sm.inc(g.dma_start(wga[:, wofs:wofs + CT * N5].rearrange("p (ct h) -> p ct h", ct=CT), wsrc_g(ch)), wname, 16)
                sm.inc(g.dma_start(wub[:, wofs:wofs + CT * N5].rearrange("p (ct h) -> p ct h", ct=CT), wsrc_u(ch)), wname, 16)
                t.wait_ge(sm.sem(wname), sm.val(wname))
                for ht in range(4):
                    hpar = (ch * 4 + ht) % 2
                    cx.wait(v, "f_hrfree_%d" % hpar)
                    for nt in range(NT):
                        ppar = nt % 2
                        cx.wait(t, "f_gufree_%d" % ppar)
                        for ct in range(CT):
                            t.matmul(pg[:, ppar * N5:(ppar + 1) * N5],
                                     wga[:, wofs + ct * N5 + ht * 128: wofs + ct * N5 + ht * 128 + 128],
                                     xb_sb[:, ct * T + nt * N5: ct * T + (nt + 1) * N5],
                                     start=(ct == 0), stop=(ct == CT - 1))
                        for ct in range(CT):
                            i = t.matmul(pu[:, ppar * N5:(ppar + 1) * N5],
                                         wub[:, wofs + ct * N5 + ht * 128: wofs + ct * N5 + ht * 128 + 128],
                                         xb_sb[:, ct * T + nt * N5: ct * T + (nt + 1) * N5],
                                         start=(ct == 0), stop=(ct == CT - 1))
                        mmv = sm.inc(i, "f_mm", 1)
                        s.wait_ge(sm.sem("f_mm"), mmv)
                        sv = sm.inc(s.activation(t1[:, :], pg[:, ppar * N5:(ppar + 1) * N5], AF.Silu), "f_sv", 1)
                        v.wait_ge(sm.sem("f_sv"), sv)
                        if not use_cb:
                            i = v.tensor_tensor(hrow[:, hpar * T + nt * N5: hpar * T + (nt + 1) * N5],
                                                t1[:, :], pu[:, ppar * N5:(ppar + 1) * N5], ALU.mult)
                        else:
                            vv = sm.inc(v.tensor_tensor(t2[:, :], t1[:, :], pu[:, ppar * N5:(ppar + 1) * N5], ALU.mult), "f_rt", 1)
                            v.wait_ge(sm.sem("f_rt"), vv)
                            i = v.tensor_tensor(hrow[:, hpar * T + nt * N5: hpar * T + (nt + 1) * N5],
                                                t2[:, :], cball[:, nt * N5:(nt + 1) * N5], ALU.mult)
                        sm.inc(i, "f_guev", 1)
                        cx.setmark("f_gufree_%d" % ppar, "f_guev")
                    sy.wait_ge(sm.sem("f_guev"), sm.val("f_guev"))
                    hname = "f_hst%d" % hpar
                    sm.inc(sy.dma_start(hout(ch * 4 + ht), hrow[:, hpar * T:(hpar + 1) * T]), hname, 16)
                    cx.setmark("f_hrfree_%d" % hpar, hname)
                cx.setmark("f_wbfree_%d" % par, "f_mm")

        gateup(HTS,
               lambda ch: shg_h.rearrange("(ct p) h -> p ct h", p=128)[:, :, ch * N5:(ch + 1) * N5],
               lambda ch: shu_h.rearrange("(ct p) h -> p ct h", p=128)[:, :, ch * N5:(ch + 1) * N5],
               lambda ht: s1_3[ht, :, :], False)

        g.wait_ge(sm.sem("f_mm"), sm.val("f_mm"))
        g.wait_ge(sm.sem("f_hst0"), sm.val("f_hst0"))
        g.wait_ge(sm.sem("f_hst1"), sm.val("f_hst1"))
        sm.inc(g.dma_start(arena[:, SHD:SHD + HTS * CT * 128].rearrange("p (ht ct c) -> p ht ct c", ht=HTS, ct=CT),
                           shd_h.rearrange("(ht p) (ct c) -> p ht ct c", p=128, c=128)), "f_wldA", 16)
        t.wait_ge(sm.sem("f_wldA"), sm.val("f_wldA"))
        sy.wait_ge(sm.sem("f_hst0"), sm.val("f_hst0"))
        sy.wait_ge(sm.sem("f_hst1"), sm.val("f_hst1"))
        for nt in range(NT):
            cx.wait(sy, "f_s1free")
            sm.inc(sy.dma_start(arena[:, S1C:S1C + HTS * N5].rearrange("p (ht t) -> p ht t", ht=HTS),
                                s1_h.rearrange("(ht p) t -> p ht t", p=128)[:, :, nt * N5:(nt + 1) * N5]), "f_s1ld", 16)
            cx.wait(t, "f_pbfree")
            i = t.matmul(pb[:, 0:N5], cx.ones[0:1, :], cball[0:1, nt * N5:(nt + 1) * N5], start=True, stop=True)
            mmv = sm.inc(i, "f_mm", 1)
            v.wait_ge(sm.sem("f_mm"), mmv)
            bcv = sm.inc(v.tensor_copy(sgb[:, :], pb[:, 0:N5]), "f_bc", 1)
            cx.setmark("f_pbfree", "f_bc")
            t.wait_ge(sm.sem("f_s1ld"), sm.val("f_s1ld"))
            for ct in range(CT):
                ppar = ct % 2
                cx.wait(t, "f_dnfree_%d" % ppar)
                if ct == 0:
                    t.wait_ge(sm.sem("f_bc"), bcv)
                for ht in range(HTS):
                    i = t.matmul(pd[:, ppar * N5:(ppar + 1) * N5],
                                 arena[:, SHD + (ht * CT + ct) * 128:SHD + (ht * CT + ct) * 128 + 128],
                                 arena[:, S1C + ht * N5: S1C + (ht + 1) * N5],
                                 start=(ht == 0), stop=(ht == HTS - 1))
                mmv = sm.inc(i, "f_mm", 1)
                v.wait_ge(sm.sem("f_mm"), mmv)
                v.wait_ge(sm.sem("f_bc"), bcv)
                slot2 = (ct // 4) % 2
                if ct % 4 == 0:
                    cx.wait(v, "f_shonfree_%d" % slot2)
                i = v.tensor_tensor(shon[:, (ct % 8) * N5:((ct % 8) + 1) * N5],
                                    pd[:, ppar * N5:(ppar + 1) * N5], sgb[:, :], ALU.mult)
                sm.inc(i, "f_dn", 1)
                cx.setmark("f_dnfree_%d" % ppar, "f_dn")
                if ct % 4 == 3:
                    sy.wait_ge(sm.sem("f_dn"), sm.val("f_dn"))
                    q = ct // 4
                    sname = "f_shst%d" % slot2
                    sm.inc(sy.dma_start(sho3[:, q * 4:(q + 1) * 4, nt * N5:(nt + 1) * N5],
                                        shon[:, (slot2 * 4 % 8) * N5:(slot2 * 4 % 8 + 4) * N5].rearrange("p (ct t) -> p ct t", ct=4)), sname, 16)
                    cx.setmark("f_shonfree_%d" % slot2, sname)
            cx.setmark("f_s1free", "f_dn")
        cx.setmark("f_wbfree_0", "f_mm")
        cx.setmark("f_wbfree_1", "f_mm")
        cx.setmark("f_hrfree_0", "f_mm")
        cx.setmark("f_hrfree_1", "f_mm")
        cx.setmark("f_cballfree", "f_mm")

        for e in range(NE):
            sy.wait_ge(sm.sem("f_ptev"), sm.val("f_ptev"))
            cx.wait(sy, "f_cbrowfree")
            crv = sm.inc(sy.dma_start(stage[0:1, 0:T], combT[e:e + 1, :]), "f_crld", 16)
            t.wait_ge(sm.sem("f_crld"), crv)
            for nt in range(NT):
                cx.wait(t, "f_pbfree")
                i = t.matmul(pb[:, 0:N5], cx.ones[0:1, :], stage[0:1, nt * N5:(nt + 1) * N5], start=True, stop=True)
                mmv = sm.inc(i, "f_mm", 1)
                v.wait_ge(sm.sem("f_mm"), mmv)
                cx.wait(v, "f_cballfree")
                sm.inc(v.tensor_copy(cball[:, nt * N5:(nt + 1) * N5], pb[:, 0:N5]), "f_cbev", 1)
                cx.setmark("f_pbfree", "f_cbev")
            gateup(HT,
                   lambda ch, e=e: wg_h[e].rearrange("(ct p) h -> p ct h", p=128)[:, :, ch * N5:(ch + 1) * N5],
                   lambda ch, e=e: wu_h[e].rearrange("(ct p) h -> p ct h", p=128)[:, :, ch * N5:(ch + 1) * N5],
                   lambda ht, e=e: hh_h[e].rearrange("(ht p) t -> ht p t", p=128)[ht, :, :], True)
            cx.setmark("f_cballfree", "f_guev")
            cx.setmark("f_cbrowfree", "f_mm")

        NG = 8
        EPG = 1
        WDHALF = HT * CT * 128
        sy.wait_ge(sm.sem("f_hst0"), sm.val("f_hst0"))
        sy.wait_ge(sm.sem("f_hst1"), sm.val("f_hst1"))
        g.wait_ge(sm.sem("f_mm"), sm.val("f_mm"))
        for gi in range(NG):
            gpar = gi % 2
            if gi >= 2:
                cx.wait(g, "f_bigfree_%d" % gpar)
            wldn = "f_wldB%d" % gpar
            sm.inc(g.dma_start(
                arena[:, WD + gpar * WDHALF:WD + (gpar + 1) * WDHALF].rearrange("p (ht ct c) -> p ht ct c", ht=HT, ct=CT),
                wd_h[gi].rearrange("(ht p) (ct c) -> p ht ct c", p=128, c=128)), wldn, 16)
            t.wait_ge(sm.sem(wldn), sm.val(wldn))
            for nt in range(NT):
                hpar = (gi * NT + nt) % 2
                idx = gi * NT + nt
                if idx == 0:
                    sm.inc(sy.dma_start(
                        arena[:, HNT: HNT + HT * N5].rearrange("p (ht t) -> p ht t", ht=HT),
                        hh_h[0].rearrange("(ht p) t -> p ht t", p=128)[:, :, 0:N5]), "f_hld0", 16)
                nxt = idx + 1
                if nxt < NG * NT:
                    ngi, nnt = divmod(nxt, NT)
                    npar = nxt % 2
                    cx.wait(sy, "f_hntfree_%d" % npar)
                    sm.inc(sy.dma_start(
                        arena[:, HNT + npar * HT * N5: HNT + (npar + 1) * HT * N5].rearrange("p (ht t) -> p ht t", ht=HT),
                        hh_h[ngi].rearrange("(ht p) t -> p ht t", p=128)[:, :, nnt * N5:(nnt + 1) * N5]),
                        "f_hld%d" % npar, 16)
                t.wait_ge(sm.sem("f_hld%d" % hpar), sm.val("f_hld%d" % hpar))
                if gi == NG - 1:
                    cx.wait(sy, "f_shonfree_0")
                    cx.wait(sy, "f_shonfree_1")
                    sm.inc(sy.dma_start(shon[:, :].rearrange("p (ct t) -> p ct t", ct=8),
                                        sho3[:, 0:8, nt * N5:(nt + 1) * N5]), "f_shold", 16)
                    v.wait_ge(sm.sem("f_shold"), sm.val("f_shold"))
                for ct in range(CT):
                    ppar = ct % 2
                    cx.wait(t, "f_dnfree_%d" % ppar)
                    for ht in range(HT):
                        i = t.matmul(pd[:, ppar * N5:(ppar + 1) * N5],
                                     arena[:, WD + gpar * WDHALF + ht * CT * 128 + ct * 128:WD + gpar * WDHALF + ht * CT * 128 + ct * 128 + 128],
                                     arena[:, HNT + hpar * HT * N5 + ht * N5: HNT + hpar * HT * N5 + (ht + 1) * N5],
                                     start=(ht == 0), stop=(ht == HT - 1))
                    mmv = sm.inc(i, "f_mm", 1)
                    v.wait_ge(sm.sem("f_mm"), mmv)
                    if gi == NG - 1 and ct == 8:
                        sy.wait_ge(sm.sem("f_yev"), sm.val("f_yev"))
                        sm.inc(sy.dma_start(shon[:, :].rearrange("p (ct t) -> p ct t", ct=8),
                                            sho3[:, 8:16, nt * N5:(nt + 1) * N5]), "f_shold", 16)
                        v.wait_ge(sm.sem("f_shold"), sm.val("f_shold"))
                    slot = (ct // 4) % 2
                    if ct % 4 == 0:
                        cx.wait(v, "f_stagefree_%d" % slot)
                    sl_st = stage[:, (ct % 8) * N5:((ct % 8) + 1) * N5]
                    if gi == NG - 1:
                        i = v.scalar_tensor_tensor(sl_st, shon[:, (ct % 8) * N5:((ct % 8) + 1) * N5], 1.0,
                                                   pd[:, ppar * N5:(ppar + 1) * N5], ALU.mult, ALU.add)
                    else:
                        i = v.tensor_copy(sl_st, pd[:, ppar * N5:(ppar + 1) * N5])
                    sm.inc(i, "f_yev", 1)
                    cx.setmark("f_dnfree_%d" % ppar, "f_yev")
                    if ct % 4 == 3:
                        g.wait_ge(sm.sem("f_yev"), sm.val("f_yev"))
                        q = ct // 4
                        yname = "f_yst%d" % slot
                        dst = y3[:, q * 4:(q + 1) * 4, nt * N5:(nt + 1) * N5]
                        src = stage[:, (slot * 4 % 8) * N5:(slot * 4 % 8 + 4) * N5].rearrange("p (ct t) -> p ct t", ct=4)
                        if gi == 0:
                            i = g.dma_start(dst, src)
                        else:
                            i = g.dma_start(dst, src, accum_op=ALU.add)
                        sm.inc(i, yname, 16)
                        cx.setmark("f_stagefree_%d" % slot, yname)
                        if gi == NG - 1 and ct == CT - 1:
                            cx.setmark("f_shonfree_0", yname)
                            cx.setmark("f_shonfree_1", yname)
                cx.setmark("f_hntfree_%d" % hpar, "f_mm")
            cx.setmark("f_bigfree_%d" % gpar, "f_mm")
        g.wait_ge(sm.sem("f_yst0"), sm.val("f_yst0"))
        g.wait_ge(sm.sem("f_yst1"), sm.val("f_yst1"))


def emit_flag_write(cx, sh_flag, fvals_sb, myslot_v, phase):
    sm, sy = cx.sm, cx.sy
    sm.inc(sy.dma_start(sh_flag.ap()[bass.ds(myslot_v, 1), :], fvals_sb[phase - 1:phase, :]),
           "flg", 16)
    sy.wait_ge(sm.sem("flg"), sm.val("flg"))


def emit_poll(cx, sh_flag, flagr_sb, fval_want_reg, paslot_v, regs):
    sm, sy = cx.sm, cx.sy
    it_r, nr_r, fv_r, tmp_r = regs
    sm.nofence.add("po")
    sy.reg_mov(it_r, 0)
    sy.reg_mov(nr_r, 1)
    with sy.While(nr_r):
        sy.dma_start(flagr_sb[0:1, :], sh_flag.ap()[bass.ds(paslot_v, 1), :]).then_inc(sm.sem("po"), 16)
        sy.wait_ge(sm.sem("po"), 16)
        sy.sem_clear(sm.sem("po"))
        sy.reg_load(fv_r, flagr_sb[0:1, 0:1])
        sy.reg_add(it_r, it_r, 1)
        sy.reg_sub(fv_r, fv_r, fval_want_reg)
        sy.reg_alu(nr_r, fv_r, 0, op=ALU.is_lt)
        sy.reg_alu(tmp_r, fv_r, NPH, op=ALU.is_gt)
        sy.reg_alu(nr_r, nr_r, tmp_r, op=ALU.logical_or)
        sy.reg_alu(tmp_r, it_r, POLL_BOUND, op=ALU.is_lt)
        sy.reg_alu(nr_r, nr_r, tmp_r, op=ALU.logical_and)
    sm.inc(sy.nop(), "pollok", 1)
    for eng in (cx.t, cx.v, cx.s, cx.g):
        eng.wait_ge(sm.sem("pollok"), sm.val("pollok"))


def build_program(detect_races=True, upto=9):
    nc = bass.Bass("TRN2", target_bir_lowering=False, detect_race_conditions=detect_races)
    io = {}

    def ein(name, shape, dtype=F32):
        io[name] = nc.dram_tensor(name, shape, dtype, kind="ExternalInput").ap()
        return io[name]

    x_in = ein("x_in", [C, T])
    wqkv_a = ein("wqkv_a", [C, F_A])
    wo_a = ein("wo_a", [1024, C])
    wqkv_b = ein("wqkv_b", [C, F_B])
    wo_b = ein("wo_b", [1024, C])
    gw = ein("gate_wT", [C, NE])
    sew = ein("se_T", [128, CT])
    shg = ein("shg_T", [C, HS])
    shu = ein("shu_T", [C, HS])
    shd = ein("shd_T", [HS, C])
    wg = ein("wg_T", [NE, C, H])
    wu = ein("wu_T", [NE, C, H])
    wd = ein("wd_T", [NE, H, C])
    ones_h = ein("ones128", [1, 128])
    identb_h = ein("identb", [128, 128], BF16)
    identf_h = ein("identf", [128, 128])
    masks_h = ein("masks", [4 * 128, N5], BF16)
    fvals_h = ein("fvals", [NPH, 16], I32)
    route_h = ein("route", [1, 8], I32)

    y_out = nc.dram_tensor("y_out", [C, T], F32, kind="ExternalOutput").ap()

    qkv_hbm = nc.dram_tensor("qkv_hbm", [F_B, S], BF16).ap()
    xbloc_hbm = nc.dram_tensor("xbloc_hbm", [C, T], BF16).ap()
    ypart_hbm = nc.dram_tensor("ypart_hbm", [C, S], F32).ap()
    ppin_hbm = nc.dram_tensor("ppin_hbm", [C, T], F32).ap()
    mpin_hbm = nc.dram_tensor("mpin_hbm", [C, T], F32).ap()
    x_sc = [nc.dram_tensor("x_sc%d" % i, [C, T], F32).ap() for i in range(2)]
    xbf_hbm = nc.dram_tensor("xbf_hbm", [C, T], F32).ap()
    yffn_hbm = nc.dram_tensor("yffn_hbm", [C, T], F32).ap()
    s1_hbm = nc.dram_tensor("s1_hbm", [HS, T], BF16).ap()
    h_hbm = nc.dram_tensor("h_hbm", [NE, H, T], BF16).ap()
    shout_hbm = nc.dram_tensor("shout_hbm", [C, T], BF16).ap()

    sh_xb = nc.dram_tensor("sh_xb", [C, S], BF16, addr_space="Shared")
    sh_part = nc.dram_tensor("sh_part", [2, C, T], F32, addr_space="Shared")
    sh_flag = nc.dram_tensor("sh_flag", [2, 16], I32, addr_space="Shared")

    with ExitStack() as top:
        cx = Ctx(nc)
        sm = cx.sm
        t, v, s, sy, g = cx.t, cx.v, cx.s, cx.sy, cx.g

        cx.ones = top.enter_context(nc.sbuf_tensor("k_ones", [1, 128], F32))
        cx.onesf = top.enter_context(nc.sbuf_tensor("k_onesf", [128, 1], F32))
        cx.onesc = top.enter_context(nc.sbuf_tensor("k_onesc", [128, 1], BF16))
        cx.identb = top.enter_context(nc.sbuf_tensor("k_identb", [128, 128], BF16))
        cx.identf = top.enter_context(nc.sbuf_tensor("k_identf", [128, 128], F32))
        cx.masks = top.enter_context(nc.sbuf_tensor("k_masks", [128, 4 * N5], BF16))
        gw_sb = top.enter_context(nc.sbuf_tensor("k_gw", [128, CT * NE], F32))
        se_sb = top.enter_context(nc.sbuf_tensor("k_se", [128, CT], BF16))
        fvals_sb = top.enter_context(nc.sbuf_tensor("k_fvals", [NPH, 16], I32))
        flagr_sb = top.enter_context(nc.sbuf_tensor("k_flagr", [1, 16], I32))
        route_sb = top.enter_context(nc.sbuf_tensor("k_route", [1, 8], I32))
        blk = top.enter_context(nc.Block())

        @blk.sync
        def _(_sy):
            sm.inc(sy.dma_start(cx.ones[:, :], ones_h), "k_ld", 16)
            sm.inc(g.dma_start(cx.onesf[:, :], ones_h.rearrange("o p -> p o")), "k_ld2", 16)
            sm.inc(g.dma_start(cx.onesc[:, :], ones_h.rearrange("o p -> p o")), "k_ld2", 16)
            sm.inc(sy.dma_start(cx.identb[:, :], identb_h), "k_ld", 16)
            sm.inc(sy.dma_start(cx.identf[:, :], identf_h), "k_ld", 16)
            sm.inc(sy.dma_start(cx.masks[:, :].rearrange("p (m t) -> p m t", m=4),
                                masks_h.rearrange("(m p) t -> p m t", p=128)), "k_ld", 16)
            sm.inc(sy.dma_start(gw_sb[:, :].rearrange("p (ct e) -> p ct e", ct=CT),
                                gw.rearrange("(ct p) e -> ct p e", p=128).rearrange("ct p e -> p ct e")), "k_ld", 16)
            sm.inc(g.dma_start(se_sb[:, :], sew), "k_ld2", 16)
            sm.inc(sy.dma_start(fvals_sb[:, :], fvals_h), "k_ld", 16)
            sm.inc(sy.dma_start(route_sb[:, :], route_h), "k_ld", 16)
            cx.fence()

            my_r = sy.alloc_register("my_r")
            pa_r = sy.alloc_register("pa_r")
            col_r = sy.alloc_register("col_r")
            it_r = sy.alloc_register("it_r")
            nr_r = sy.alloc_register("nr_r")
            fv_r = sy.alloc_register("fv_r")
            tmp_r = sy.alloc_register("tmp_r")
            want_r = sy.alloc_register("want_r")
            sy.reg_load(my_r, route_sb[0:1, 0:1])
            sy.reg_load(pa_r, route_sb[0:1, 1:2])
            sy.reg_load(col_r, route_sb[0:1, 2:3])
            my_v = RuntimeValue(my_r, min_val=0, max_val=1)
            pa_v = RuntimeValue(pa_r, min_val=0, max_val=1)
            col_v = RuntimeValue(col_r, min_val=0, max_val=T)
            gmy_r = g.alloc_register("gmy_r")
            gpa_r = g.alloc_register("gpa_r")
            gcol_r = g.alloc_register("gcol_r")
            gpcol_r = g.alloc_register("gpcol_r")
            g.wait_ge(sm.sem("k_ld"), sm.val("k_ld"))
            g.reg_load(gmy_r, route_sb[0:1, 0:1])
            g.reg_load(gpa_r, route_sb[0:1, 1:2])
            g.reg_load(gcol_r, route_sb[0:1, 2:3])
            g.reg_load(gpcol_r, route_sb[0:1, 3:4])
            gmy_v = RuntimeValue(gmy_r, min_val=0, max_val=1)
            gpa_v = RuntimeValue(gpa_r, min_val=0, max_val=1)
            gcol_v = RuntimeValue(gcol_r, min_val=0, max_val=T)
            gpcol_v = RuntimeValue(gpcol_r, min_val=0, max_val=T)

            x_in3 = x_in.rearrange("(ct p) t -> ct p t", p=128)
            x_sc3 = [x.rearrange("(ct p) t -> ct p t", p=128) for x in x_sc]
            xbf3 = xbf_hbm.rearrange("(ct p) t -> ct p t", p=128)
            yffn3 = yffn_hbm.rearrange("(ct p) t -> ct p t", p=128)
            yout3 = y_out.rearrange("(ct p) t -> ct p t", p=128)
            sh_xb4 = sh_xb.ap().rearrange("(ct p) t -> ct p t", p=128)

            class Rows:
                def __init__(self, fn):
                    self.fn = fn
                def __getitem__(self, ct):
                    return self.fn(ct)

            xbloc3 = xbloc_hbm.rearrange("(ct p) t -> ct p t", p=128)

            def sh_xb_dst(ct, nt):
                return xbloc3[ct][:, nt * N5:(nt + 1) * N5]

            ypart3 = ypart_hbm.rearrange("(ct p) t -> p ct t", p=128)
            ppin3 = ppin_hbm.rearrange("(ct p) t -> ct p t", p=128)
            mpin3 = mpin_hbm.rearrange("(ct p) t -> ct p t", p=128)

            def part_src(w):
                return Rows(lambda ct: (ppin3 if w == 0 else mpin3)[ct])

            def shp_dst2(ch2, nt):
                return ypart3[:, ch2 * 2:(ch2 + 1) * 2, nt * N5:(nt + 1) * N5]

            def share_xb_copy():
                g.wait_ge(sm.sem("rl_shxb0"), sm.val("rl_shxb0"))
                g.wait_ge(sm.sem("rl_shxb1"), sm.val("rl_shxb1"))
                sm.inc(g.dma_start(sh_xb.ap()[:, bass.ds(gcol_v, T)], xbloc_hbm[:, :]), "dyncp", 16)

            def share_ypart_copy():
                g.wait_ge(sm.sem("a_yst"), sm.val("a_yst"))
                sm.inc(g.dma_start(sh_part.ap()[bass.ds(gmy_v, 1), :, :],
                                   ypart_hbm[:, bass.ds(gpcol_v, T)]), "dyncp", 16)

            def fetch_partials():
                sm.inc(g.dma_start(ppin_hbm[:, :], sh_part.ap()[bass.ds(gpa_v, 1), :, :]), "dyncp", 16)
                sm.inc(g.dma_start(mpin_hbm[:, :], ypart_hbm[:, bass.ds(gcol_v, T)]), "dyncp", 16)
                sy.wait_ge(sm.sem("dyncp"), sm.val("dyncp"))

            wio = dict(shg_T=shg, shu_T=shu, shd_T=shd, wg_T=wg, wu_T=wu, wd_T=wd)
            scr = dict(gw_sb=gw_sb, se_sb=se_sb, s1_hbm=s1_hbm, h_hbm=h_hbm, shout_hbm=shout_hbm)

            def poll(phase):
                sy.reg_load(want_r, fvals_sb[phase - 1:phase, 0:1])
                emit_poll(cx, sh_flag, flagr_sb, want_r, pa_v, (it_r, nr_r, fv_r, tmp_r))

            def post(phase):
                cx.fence()
                emit_flag_write(cx, sh_flag, fvals_sb, my_v, phase)

            emit_resid_ln(cx, top, [x_in3], x_sc3[0], None, sh_xb_dst, None, do_ln=True, tag="l1")
            share_xb_copy()
            post(PH_LN1)
            cur_x = x_sc3[0]

            if upto >= 2:
                poll(PH_LN1)
                emit_attn(cx, top, UNITS_A, F_A, sh_xb4, wqkv_a, wo_a, qkv_hbm, shp_dst2, tag="a1")
                share_ypart_copy()
                post(PH_A1)

            if upto >= 3:
                poll(PH_A1)
                fetch_partials()
                with nc.sbuf_tensor("k_xbffn1", [128, CT * T], BF16) as xb_ffn:
                    emit_resid_ln(cx, top, [x_in3, part_src(0), part_src(1)], x_sc3[0],
                                  xb_ffn, None, xbf3, do_ln=True, tag="r1")
                    cx.fence()
                    if upto >= 4:
                        emit_ffn(cx, top, xb_ffn, xbf_hbm, wio, yffn_hbm, scr, tag="f1")
                    cx.fence()

            if upto >= 5:
                emit_resid_ln(cx, top, [x_sc3[0], yffn3], x_sc3[1], None, sh_xb_dst, None,
                              do_ln=True, tag="r2")
                share_xb_copy()
                post(PH_R2)
                cur_x = x_sc3[1]

            if upto >= 6:
                poll(PH_R2)
                emit_attn(cx, top, UNITS_B, F_B, sh_xb4, wqkv_b, wo_b, qkv_hbm, shp_dst2, tag="a2")
                share_ypart_copy()
                post(PH_A2)

            if upto >= 7:
                poll(PH_A2)
                fetch_partials()
                with nc.sbuf_tensor("k_xbffn2", [128, CT * T], BF16) as xb_ffn:
                    emit_resid_ln(cx, top, [x_sc3[1], part_src(0), part_src(1)], x_sc3[0],
                                  xb_ffn, None, xbf3, do_ln=True, tag="r3")
                    cx.fence()
                    cur_x = x_sc3[0]
                    if upto >= 8:
                        emit_ffn(cx, top, xb_ffn, xbf_hbm, wio, yffn_hbm, scr, tag="f2")
                    cx.fence()

            if upto >= 9:
                emit_resid_ln(cx, top, [x_sc3[0], yffn3], yout3, None, None, None,
                              do_ln=False, tag="r4")
            else:
                emit_resid_ln(cx, top, [cur_x], yout3, None, None, None,
                              do_ln=False, tag="r4")
            cx.fence()

    return nc


_NC_CACHE = {}


def kernel(**inputs):
    x = np.asarray(inputs["L0_attn_a_x"])
    wqkv_a_full = np.asarray(inputs["L0_attn_a_weight_qkv"])
    wproj_a = np.asarray(inputs["L0_attn_a_weight_proj"])
    wqkv_b_full = np.asarray(inputs["L0_attn_b_w_qkv"])
    wo_b_full = np.asarray(inputs["L0_attn_b_w_o"])
    gate_w = np.asarray(inputs["L0_ffn_gate_w"])
    w_gate = np.asarray(inputs["L0_ffn_w_gate"])
    w_up = np.asarray(inputs["L0_ffn_w_up"])
    w_down = np.asarray(inputs["L0_ffn_w_down"])
    sh_gate = np.asarray(inputs["L0_ffn_sh_gate"])
    sh_up = np.asarray(inputs["L0_ffn_sh_up"])
    sh_down = np.asarray(inputs["L0_ffn_sh_down"])
    se_gate_w = np.asarray(inputs["L0_ffn_se_gate_w"])

    if "nc" not in _NC_CACHE:
        _NC_CACHE["nc"] = build_program()
    nc = _NC_CACHE["nc"]

    masks = np.zeros((4, 128, N5), dtype=np.float32)
    for mi, o in enumerate((0, 128, 256, 384)):
        for p in range(128):
            masks[mi, p, :] = np.where(np.arange(N5) >= p + o, 0.0, -1e30)
    nonce = int(np.random.randint(1 << 20, 1 << 30))
    fvals = np.zeros((NPH, 16), dtype=np.int32)
    for k in range(NPH):
        fvals[k, 0] = nonce + k + 1

    common = {
        "gate_wT": np.ascontiguousarray(gate_w.T),
        "se_T": np.ascontiguousarray(se_gate_w.reshape(CT, 128).T),
        "shg_T": np.ascontiguousarray(sh_gate.T),
        "shu_T": np.ascontiguousarray(sh_up.T),
        "shd_T": np.ascontiguousarray(sh_down.T),
        "wg_T": np.ascontiguousarray(np.transpose(w_gate, (0, 2, 1))),
        "wu_T": np.ascontiguousarray(np.transpose(w_up, (0, 2, 1))),
        "wd_T": np.ascontiguousarray(np.transpose(w_down, (0, 2, 1))),
        "ones128": np.ones((1, 128), dtype=np.float32),
        "identb": np.eye(128, dtype=ml_dtypes.bfloat16),
        "identf": np.eye(128, dtype=np.float32),
        "masks": masks.reshape(4 * 128, N5).astype(ml_dtypes.bfloat16),
        "fvals": fvals,
    }

    wpa_T = wproj_a.T
    wob_T = wo_b_full.T

    in_maps = []
    for c in range(8):
        b, m = c >> 1, c & 1
        ra = np.r_[2 * m * 768:(2 * m + 2) * 768]
        rb = np.concatenate([np.r_[h * 128:(h + 1) * 128,
                                   2048 + h * 128:2048 + (h + 1) * 128,
                                   4096 + h * 128:4096 + (h + 1) * 128]
                             for h in range(8 * m, 8 * m + 8)])
        im = dict(common)
        im["x_in"] = np.ascontiguousarray(x[b, m * T:(m + 1) * T, :].T)
        im["wqkv_a"] = np.ascontiguousarray(wqkv_a_full[ra].T)
        im["wo_a"] = np.ascontiguousarray(wpa_T[m * 1024:(m + 1) * 1024])
        im["wqkv_b"] = np.ascontiguousarray(wqkv_b_full[rb].T)
        im["wo_b"] = np.ascontiguousarray(wob_T[m * 1024:(m + 1) * 1024])
        im["route"] = np.array([[m, 1 - m, m * T, (1 - m) * T, 0, 0, 0, 0]], dtype=np.int32)
        in_maps.append(im)

    res = run_bass_kernel_spmd(nc, in_maps, core_ids=list(range(8)))

    out = np.empty((B, S, C), dtype=np.float32)
    for c in range(8):
        b, m = c >> 1, c & 1
        out[b, m * T:(m + 1) * T, :] = res.results[c]["y_out"].T
    return out


# version 08 (final)
# speedup vs baseline: 1.0356x; 1.0356x over previous
import numpy as np
import ml_dtypes
import concourse.bass as bass
import concourse.mybir as mybir
from concourse.bass_utils import run_bass_kernel_spmd
from concourse.expressions import RuntimeValue
from contextlib import ExitStack

F32 = mybir.dt.float32
BF16 = mybir.dt.bfloat16
I32 = mybir.dt.int32
AF = mybir.ActivationFunctionType
ALU = mybir.AluOpType
AX = mybir.AxisListType

B, S, C = 4, 4096, 2048
T = 2048
HS, H, NE = 2048, 1024, 8
N5 = 512
CT = C // 128
NT = T // N5
SNT = S // N5
NQM = S // N5
NKT = S // 128
HT, HTS = H // 128, HS // 128
ISQ = float(1.0 / np.sqrt(128.0))

UNITS_A = [dict(q=[0, 1, 2, 3], k=4, v=5), dict(q=[6, 7, 8, 9], k=10, v=11)]
F_A = 12 * 128
UNITS_B = [dict(q=[3 * h], k=3 * h + 1, v=3 * h + 2) for h in range(8)]
F_B = 24 * 128
ET = 8

PH_LN1, PH_A1, PH_R1, PH_F1, PH_R2, PH_A2, PH_R3, PH_F2 = 1, 2, 3, 4, 5, 6, 7, 8
NPH = 8
POLL_BOUND = 3000000


class Sems:
    def __init__(self, nc):
        self.nc = nc
        self.h = {}
        self.v = {}
        self.nofence = set()

    def sem(self, name):
        if name not in self.h:
            self.h[name] = self.nc.alloc_semaphore(name=name)
            self.v.setdefault(name, 0)
        return self.h[name]

    def inc(self, ins, name, k):
        ins.then_inc(self.sem(name), k)
        self.v[name] += k
        return self.v[name]

    def val(self, name):
        self.sem(name)
        return self.v[name]


class Ctx:
    def __init__(self, nc):
        self.nc = nc
        self.sm = Sems(nc)
        self.mark = {}
        self.g = nc.gpsimd
        self.sy = nc.sync
        self.t = nc.tensor
        self.v = nc.vector
        self.s = nc.scalar

    def wait(self, eng, key):
        if key in self.mark:
            name, val = self.mark[key]
            if val > 0:
                eng.wait_ge(self.sm.sem(name), val)

    def setmark(self, key, name):
        self.mark[key] = (name, self.sm.val(name))

    def fence(self):
        for eng in (self.t, self.v, self.s, self.sy, self.g):
            for name, val in sorted(self.sm.v.items()):
                if val > 0 and name not in self.sm.nofence:
                    eng.wait_ge(self.sm.sem(name), val)
        self.mark.clear()


def emit_resid_ln(cx, es, srcs, x_out, xb_sb_dst, sh_xb_dst=None, xbf_out=None,
                  do_ln=True, tag=""):
    nc, sm, t, v, s, sy, g = cx.nc, cx.sm, cx.t, cx.v, cx.s, cx.sy, cx.g
    nsrc = len(srcs)
    with ExitStack() as ph:
        row = ph.enter_context(nc.sbuf_tensor("r_row" + tag, [128, 2 * nsrc * T], F32))
        xnew = ph.enter_context(nc.sbuf_tensor("r_xnew" + tag, [128, 2 * T], F32))
        sqr = ph.enter_context(nc.sbuf_tensor("r_sqr" + tag, [128, T], F32))
        accA = ph.enter_context(nc.sbuf_tensor("r_accA" + tag, [128, T], F32))
        accB = ph.enter_context(nc.sbuf_tensor("r_accB" + tag, [128, T], F32))

        rb = ph.enter_context(nc.sbuf_tensor("r_rb" + tag, [128, 2 * N5], F32))
        mrb = ph.enter_context(nc.sbuf_tensor("r_mrb" + tag, [128, 2 * N5], F32))
        xf1 = ph.enter_context(nc.sbuf_tensor("r_xf1" + tag, [128, N5], F32))
        xbrow = ph.enter_context(nc.sbuf_tensor("r_xbrow" + tag, [128, 2 * T], BF16))
        xfrow = ph.enter_context(nc.sbuf_tensor("r_xfrow" + tag, [128, 2 * T], F32))
        pln = ph.enter_context(nc.psum_tensor("r_pln" + tag, [1, N5], F32))
        pbc = ph.enter_context(nc.psum_tensor("r_pbc" + tag, [128, 2 * N5], F32))

        ld = "rl_ld"

        for ct in range(CT):
            par = ct % 2
            cx.wait(sy, "rl_rowfree_%d" % par)
            for si, src in enumerate(srcs):
                sm.inc(sy.dma_start(row[:, (par * nsrc + si) * T:(par * nsrc + si + 1) * T], src[ct]),
                       ld + str(par), 16)
            v.wait_ge(sm.sem(ld + str(par)), sm.val(ld + str(par)))
            cx.wait(v, "rl_xnfree_%d" % par)
            base = par * nsrc * T
            if nsrc == 1:
                i = v.tensor_copy(xnew[:, par * T:(par + 1) * T], row[:, base:base + T])
            elif nsrc == 2:
                i = v.tensor_tensor(xnew[:, par * T:(par + 1) * T], row[:, base:base + T],
                                    row[:, base + T:base + 2 * T], ALU.add)
            else:
                vv = sm.inc(v.tensor_tensor(xnew[:, par * T:(par + 1) * T], row[:, base:base + T],
                                            row[:, base + T:base + 2 * T], ALU.add), "rl_vv", 1)
                v.wait_ge(sm.sem("rl_vv"), vv)
                i = v.tensor_tensor(xnew[:, par * T:(par + 1) * T], xnew[:, par * T:(par + 1) * T],
                                    row[:, base + 2 * T:base + 3 * T], ALU.add)
            vv = sm.inc(i, "rl_vv", 1)
            cx.setmark("rl_rowfree_%d" % par, ld + str(par))
            v.wait_ge(sm.sem("rl_vv"), vv)
            sy.wait_ge(sm.sem("rl_vv"), vv)
            sm.inc(sy.dma_start(x_out[ct], xnew[:, par * T:(par + 1) * T]), "rl_st%d" % par, 16)
            cx.setmark("rl_rowfree_%d" % par, "rl_vv")
            self_mark_val = sm.v["rl_vv"]
            if do_ln:
                s.wait_ge(sm.sem("rl_vv"), vv)
                cx.wait(s, "rl_sqfree")
                svv = sm.inc(s.activation(sqr[:, :], xnew[:, par * T:(par + 1) * T], AF.Square), "rl_sv", 1)
                if ct == 0:
                    i = v.tensor_copy(accA[:, :], xnew[:, par * T:(par + 1) * T])
                else:
                    i = v.tensor_tensor(accA[:, :], accA[:, :], xnew[:, par * T:(par + 1) * T], ALU.add)
                vv2 = sm.inc(i, "rl_vv", 1)
                v.wait_ge(sm.sem("rl_sv"), svv)
                v.wait_ge(sm.sem("rl_vv"), vv2)
                if ct == 0:
                    i = v.tensor_copy(accB[:, :], sqr[:, :])
                else:
                    i = v.tensor_tensor(accB[:, :], accB[:, :], sqr[:, :], ALU.add)
                vv3 = sm.inc(i, "rl_vv", 1)
                cx.setmark("rl_sqfree", "rl_vv")
                v.wait_ge(sm.sem("rl_vv"), vv3)
            cx.setmark("rl_xnfree_%d" % par, "rl_st%d" % par)

        if not do_ln:
            sy.wait_ge(sm.sem("rl_st0"), sm.val("rl_st0"))
            sy.wait_ge(sm.sem("rl_st1"), sm.val("rl_st1"))
            return

        def edge(ins):
            vv = sm.inc(ins, "rl_vv", 1)
            v.wait_ge(sm.sem("rl_vv"), vv)

        for acc, dst in ((accA, sqr), (accB, None)):
            for nt in range(NT):
                cx.wait(t, "rl_plnfree")
                t.wait_ge(sm.sem("rl_vv"), sm.val("rl_vv"))
                t.wait_ge(sm.sem("rl_sv"), sm.val("rl_sv"))
                i = t.matmul(pln[0:1, :], cx.onesf[:, :], acc[:, nt * N5:(nt + 1) * N5],
                             start=True, stop=True)
                mmv = sm.inc(i, "rl_mm", 1)
                v.wait_ge(sm.sem("rl_mm"), mmv)
                tgt = sqr[0:1, nt * N5:(nt + 1) * N5] if acc is accA else accB[0:1, nt * N5:(nt + 1) * N5]
                edge(v.tensor_copy(tgt, pln[0:1, :]))
                cx.setmark("rl_plnfree", "rl_vv")
        c1 = float(1.0 / C)
        edge(v.tensor_scalar(sqr[0:1, 0:T], sqr[0:1, 0:T], c1, None, ALU.mult))
        edge(v.tensor_tensor(accA[0:1, :], sqr[0:1, 0:T], sqr[0:1, 0:T], ALU.mult))
        edge(v.scalar_tensor_tensor(accB[0:1, :], accB[0:1, :], c1, accA[0:1, :],
                                    ALU.mult, ALU.subtract))
        edge(v.tensor_scalar(accB[0:1, :], accB[0:1, :], 1e-5, None, ALU.add))
        s.wait_ge(sm.sem("rl_vv"), sm.val("rl_vv"))
        svv = sm.inc(s.activation(accB[0:1, :], accB[0:1, :], AF.Ln), "rl_sv", 1)
        s.wait_ge(sm.sem("rl_sv"), svv)
        svv = sm.inc(s.activation(accB[0:1, :], accB[0:1, :], AF.Exp, scale=-0.5), "rl_sv", 1)
        v.wait_ge(sm.sem("rl_sv"), svv)
        edge(v.tensor_tensor(accA[0:1, :], sqr[0:1, 0:T], accB[0:1, :], ALU.mult))
        t.wait_ge(sm.sem("rl_vv"), sm.val("rl_vv"))

        sy.wait_ge(sm.sem("rl_st0"), sm.val("rl_st0"))
        sy.wait_ge(sm.sem("rl_st1"), sm.val("rl_st1"))
        for nt in range(NT):
            bpar = nt % 2
            cx.wait(t, "rl_bcfree_%d" % bpar)
            i = t.matmul(pbc[:, bpar * N5:(bpar + 1) * N5], cx.ones[0:1, :],
                         accB[0:1, nt * N5:(nt + 1) * N5], start=True, stop=True)
            mmv = sm.inc(i, "rl_mm", 1)
            v.wait_ge(sm.sem("rl_mm"), mmv)
            edge(v.tensor_copy(rb[:, bpar * N5:(bpar + 1) * N5], pbc[:, bpar * N5:(bpar + 1) * N5]))
            t.wait_ge(sm.sem("rl_vv"), sm.val("rl_vv"))
            i = t.matmul(pbc[:, bpar * N5:(bpar + 1) * N5], cx.ones[0:1, :],
                         accA[0:1, nt * N5:(nt + 1) * N5], start=True, stop=True)
            mmv = sm.inc(i, "rl_mm", 1)
            v.wait_ge(sm.sem("rl_mm"), mmv)
            edge(v.tensor_copy(mrb[:, bpar * N5:(bpar + 1) * N5], pbc[:, bpar * N5:(bpar + 1) * N5]))
            cx.setmark("rl_bcfree_%d" % bpar, "rl_vv")
            for ct in range(CT):
                par = ct % 2
                cx.wait(sy, "rl_xffree_%d" % par)
                sm.inc(sy.dma_start(xfrow[:, par * T + nt * N5:par * T + (nt + 1) * N5],
                                    x_out[ct][:, nt * N5:(nt + 1) * N5]), "rl_xf%d" % par, 16)
                v.wait_ge(sm.sem("rl_xf%d" % par), sm.val("rl_xf%d" % par))
                vv = sm.inc(v.tensor_tensor(xf1[:, :], xfrow[:, par * T + nt * N5:par * T + (nt + 1) * N5],
                                            rb[:, bpar * N5:(bpar + 1) * N5], ALU.mult), "rl_vv", 1)
                v.wait_ge(sm.sem("rl_vv"), vv)
                cx.wait(v, "rl_xbfree_%d" % par)
                if xbf_out is not None:
                    vv = sm.inc(v.tensor_tensor(xfrow[:, par * T + nt * N5:par * T + (nt + 1) * N5],
                                                xf1[:, :], mrb[:, bpar * N5:(bpar + 1) * N5],
                                                ALU.subtract), "rl_vv", 1)
                    v.wait_ge(sm.sem("rl_vv"), vv)
                    sy.wait_ge(sm.sem("rl_vv"), vv)
                    sm.inc(sy.dma_start(xbf_out[ct][:, nt * N5:(nt + 1) * N5],
                                        xfrow[:, par * T + nt * N5:par * T + (nt + 1) * N5]),
                           "rl_xbf%d" % par, 16)
                    cx.setmark("rl_xffree_%d" % par, "rl_xbf%d" % par)
                    i = v.tensor_copy(xbrow[:, par * T + nt * N5:par * T + (nt + 1) * N5],
                                      xfrow[:, par * T + nt * N5:par * T + (nt + 1) * N5])
                else:
                    cx.setmark("rl_xffree_%d" % par, "rl_vv")
                    i = v.tensor_tensor(xbrow[:, par * T + nt * N5:par * T + (nt + 1) * N5],
                                        xf1[:, :], mrb[:, bpar * N5:(bpar + 1) * N5], ALU.subtract)
                vv = sm.inc(i, "rl_vv", 1)
                if xb_sb_dst is not None:
                    v.wait_ge(sm.sem("rl_vv"), vv)
                    vv = sm.inc(v.tensor_copy(xb_sb_dst[:, ct * T + nt * N5:ct * T + (nt + 1) * N5],
                                              xbrow[:, par * T + nt * N5:par * T + (nt + 1) * N5]),
                                "rl_vv", 1)
                if sh_xb_dst is not None:
                    sy.wait_ge(sm.sem("rl_vv"), vv)
                    sm.inc(sy.dma_start(sh_xb_dst(ct, nt),
                                        xbrow[:, par * T + nt * N5:par * T + (nt + 1) * N5]),
                           "rl_shxb%d" % par, 16)
                    cx.setmark("rl_xbfree_%d" % par, "rl_shxb%d" % par)
                else:
                    cx.setmark("rl_xbfree_%d" % par, "rl_vv")


def emit_attn(cx, es, units, FF, xb_src3, wqkv_h, wo_h, qkv_h, shp_dst2, tag=""):
    nc, sm, t, v, s, sy, g = cx.nc, cx.sm, cx.t, cx.v, cx.s, cx.sy, cx.g
    OF = FF // 128
    NCH = OF * 128 // N5
    with ExitStack() as ph:
        arena = ph.enter_context(nc.sbuf_tensor("a_arena" + tag, [128, CT * S], BF16))
        wch = ph.enter_context(nc.sbuf_tensor("a_wch" + tag, [128, 2 * CT * N5], BF16))
        qrow = ph.enter_context(nc.sbuf_tensor("a_qrow" + tag, [128, 2 * S], BF16))
        sq = ph.enter_context(nc.sbuf_tensor("a_sq" + tag, [128, 4 * N5], BF16))
        nred = ph.enter_context(nc.sbuf_tensor("a_nred" + tag, [1, 8], F32))

        nmax = ph.enter_context(nc.sbuf_tensor("a_nmax" + tag, [1, 32], F32))
        csb = ph.enter_context(nc.sbuf_tensor("a_c" + tag, [1, 4], F32))
        negc = ph.enter_context(nc.sbuf_tensor("a_negc" + tag, [128, 1], F32))
        pch = ph.enter_context(nc.sbuf_tensor("a_pch" + tag, [128, 3 * N5], BF16))
        linv = ph.enter_context(nc.sbuf_tensor("a_linv" + tag, [1, 2 * N5], F32))
        lb = ph.enter_context(nc.sbuf_tensor("a_lb" + tag, [128, N5], F32))
        stg = ph.enter_context(nc.sbuf_tensor("a_stg" + tag, [128, 4 * N5], F32))
        pS = ph.enter_context(nc.psum_tensor("a_pS" + tag, [128, 3 * N5], F32))
        pAV = ph.enter_context(nc.psum_tensor("a_pAV" + tag, [128, 2 * N5], F32))
        pl = ph.enter_context(nc.psum_tensor("a_pl" + tag, [1, N5], F32))
        pb = ph.enter_context(nc.psum_tensor("a_pb" + tag, [128, N5], F32))
        pvt = ph.enter_context(nc.psum_tensor("a_pvt" + tag, [128, 128], BF16))

        A_AO = 0
        A_KT = ET * S
        A_VT = A_KT + S
        A_VTK = A_VT + S
        A_QT = A_VTK + S
        A_WO = A_QT + S

        qkv3 = qkv_h.rearrange("(of p) t -> of p t", p=128)

        sm.inc(g.dma_start(arena[:, 0:CT * S].rearrange("p (ct t) -> p ct t", ct=CT),
                           xb_src3.rearrange("ct p t -> p ct t")), "a_xbld", 16)
        for eng in (t, v, s):
            eng.wait_ge(sm.sem("a_xbld"), sm.val("a_xbld"))

        for ch in range(NCH):
            par = ch % 2
            wofs = par * CT * N5
            cx.wait(g, "a_wchfree_%d" % par)
            wname = "a_wld%d" % par
            sm.inc(g.dma_start(
                wch[:, wofs:wofs + CT * N5].rearrange("p (ct f) -> p ct f", ct=CT),
                wqkv_h.rearrange("(ct p) f -> p ct f", p=128)[:, :, ch * N5:(ch + 1) * N5]), wname, 16)
            t.wait_ge(sm.sem(wname), sm.val(wname))
            for fo in range(4):
                of = ch * 4 + fo
                qpar = of % 2
                cx.wait(s, "a_qrowfree_%d" % qpar)
                for nt in range(SNT):
                    ppar = nt % 2
                    cx.wait(t, "a_psfree_%d" % ppar)
                    for ct in range(CT):
                        i = t.matmul(
                            pS[:, ppar * N5:(ppar + 1) * N5],
                            wch[:, wofs + ct * N5 + fo * 128: wofs + ct * N5 + fo * 128 + 128],
                            arena[:, ct * S + nt * N5: ct * S + (nt + 1) * N5],
                            start=(ct == 0), stop=(ct == CT - 1))
                    mmv = sm.inc(i, "a_mm", 1)
                    s.wait_ge(sm.sem("a_mm"), mmv)
                    sv = sm.inc(s.activation(qrow[:, qpar * S + nt * N5: qpar * S + (nt + 1) * N5],
                                             pS[:, ppar * N5:(ppar + 1) * N5], AF.Copy), "a_sv", 1)
                    cx.setmark("a_psfree_%d" % ppar, "a_sv")
                    if nt % 4 == 0:
                        cx.wait(s, "a_sqfree")
                    sv2 = sm.inc(s.activation(sq[:, (nt % 4) * N5:((nt % 4) + 1) * N5],
                                              pS[:, ppar * N5:(ppar + 1) * N5], AF.Square), "a_sv", 1)
                    cx.setmark("a_psfree_%d" % ppar, "a_sv")
                    if nt % 4 == 3:
                        t.wait_ge(sm.sem("a_sv"), sv2)
                        half = nt // 4
                        for q4 in range(4):
                            plpar = q4 % 2
                            pno = pl[0:1, 0:N5] if plpar == 0 else pb[0:1, 0:N5]
                            cx.wait(t, "a_pnofree_%d" % plpar)
                            i = t.matmul(pno, cx.onesc[:, :],
                                         sq[:, q4 * N5:(q4 + 1) * N5], start=True, stop=True)
                            mmv = sm.inc(i, "a_mm", 1)
                            cx.setmark("a_sqfree", "a_mm")
                            v.wait_ge(sm.sem("a_mm"), mmv)
                            vv = sm.inc(v.tensor_copy(stg[0:1, q4 * N5:(q4 + 1) * N5],
                                                      pno), "a_vv", 1)
                            cx.setmark("a_pnofree_%d" % plpar, "a_vv")
                        v.wait_ge(sm.sem("a_vv"), sm.val("a_vv"))
                        vv = sm.inc(v.reduce_max(nred[0:1, 0:4],
                                                 stg[0:1, 0:4 * N5].rearrange("o (a b) -> o a b", a=4), axis=AX.X), "a_vv", 1)
                        v.wait_ge(sm.sem("a_vv"), vv)
                        if half == 0:
                            vv = sm.inc(v.reduce_max(nmax[0:1, of:of + 1], nred[0:1, 0:4], axis=AX.X), "a_vv", 1)
                        else:
                            vv = sm.inc(v.reduce_max(nred[0:1, 4:5], nred[0:1, 0:4], axis=AX.X), "a_vv", 1)
                            v.wait_ge(sm.sem("a_vv"), vv)
                            vv = sm.inc(v.tensor_tensor(nmax[0:1, of:of + 1], nmax[0:1, of:of + 1],
                                                        nred[0:1, 4:5], ALU.max), "a_vv", 1)
                        v.wait_ge(sm.sem("a_vv"), vv)
                sy.wait_ge(sm.sem("a_sv"), sm.val("a_sv"))
                sm.inc(sy.dma_start(qkv3[of], qrow[:, qpar * S:(qpar + 1) * S]), "a_qst%d" % qpar, 16)
                cx.setmark("a_qrowfree_%d" % qpar, "a_qst%d" % qpar)
            cx.setmark("a_wchfree_%d" % par, "a_mm")

        sy.wait_ge(sm.sem("a_qst0"), sm.val("a_qst0"))
        sy.wait_ge(sm.sem("a_qst1"), sm.val("a_qst1"))
        g.wait_ge(sm.sem("a_mm"), sm.val("a_mm"))
        sm.inc(g.dma_start(arena[:, A_WO:A_WO + ET * CT * 128].rearrange("p (et ct c) -> p et ct c", et=ET, ct=CT),
                           wo_h.rearrange("(et p) (ct c) -> p et ct c", p=128, c=128)), "a_wold", 16)
        sy.wait_ge(sm.sem("a_mm"), sm.val("a_mm"))
        v.wait_ge(sm.sem("a_mm"), sm.val("a_mm"))
        cx.setmark("a_pbfree", "a_vv")
        plfree = [("a_vv", sm.val("a_vv"))]

        for ui, u in enumerate(units):
            cx.wait(sy, "a_kvfree")
            sm.inc(sy.dma_start(arena[:, A_KT:A_KT + S], qkv3[u["k"]]), "a_kvld", 16)
            sm.inc(sy.dma_start(arena[:, A_VT:A_VT + S], qkv3[u["v"]]), "a_kvld", 16)
            t.wait_ge(sm.sem("a_kvld"), sm.val("a_kvld"))
            for kt in range(NKT):
                cx.wait(t, "a_pvtfree")
                i = t.transpose(pvt[:, :], arena[:, A_VT + kt * 128:A_VT + (kt + 1) * 128], cx.identb[:, :])
                mmv = sm.inc(i, "a_mm", 1)
                v.wait_ge(sm.sem("a_mm"), mmv)
                vv = sm.inc(v.tensor_copy(arena[:, A_VTK + kt * 128:A_VTK + (kt + 1) * 128], pvt[:, :]),
                            "a_vtev", 1)
                cx.setmark("a_pvtfree", "a_vtev")
            t.wait_ge(sm.sem("a_vtev"), sm.val("a_vtev"))
            for qi, qof in enumerate(u["q"]):
                head_et = ui * len(u["q"]) + qi
                cx.wait(sy, "a_qfree")
                qv = sm.inc(sy.dma_start(arena[:, A_QT:A_QT + S], qkv3[qof]), "a_qld", 16)
                t.wait_ge(sm.sem("a_qld"), qv)

                def edge(ins):
                    vv = sm.inc(ins, "a_rt", 1)
                    v.wait_ge(sm.sem("a_rt"), vv)
                edge(v.tensor_tensor(csb[0:1, 0:1], nmax[0:1, qof:qof + 1],
                                     nmax[0:1, u["k"]:u["k"] + 1], ALU.mult))
                s.wait_ge(sm.sem("a_rt"), sm.val("a_rt"))
                sv = sm.inc(s.activation(csb[0:1, 1:2], csb[0:1, 0:1], AF.Sqrt, scale=1.0 / 128.0), "a_sv", 1)
                v.wait_ge(sm.sem("a_sv"), sv)
                edge(v.tensor_scalar(csb[0:1, 2:3], csb[0:1, 1:2], -1.0, None, ALU.mult))
                t.wait_ge(sm.sem("a_rt"), sm.val("a_rt"))
                cx.wait(t, "a_pbfree")
                i = t.matmul(pb[:, 0:1], cx.ones[0:1, :], csb[0:1, 2:3], start=True, stop=True)
                mmv = sm.inc(i, "a_mm", 1)
                v.wait_ge(sm.sem("a_mm"), mmv)
                vv = sm.inc(v.tensor_copy(negc[:, :], pb[:, 0:1]), "a_nc", 1)
                cx.setmark("a_pbfree", "a_nc")
                s.wait_ge(sm.sem("a_nc"), sm.val("a_nc"))
                pending_fin = None

                def emit_fin(fin):
                    qmf, avparf, head_etf, rtv = fin
                    t.wait_ge(sm.sem("a_rt"), rtv)
                    cx.wait(t, "a_pbfree")
                    i = t.matmul(pb[:, 0:N5], cx.ones[0:1, :], linv[0:1, avparf * N5:(avparf + 1) * N5],
                                 start=True, stop=True)
                    mmv = sm.inc(i, "a_mm", 1)
                    v.wait_ge(sm.sem("a_mm"), mmv)
                    vv = sm.inc(v.tensor_copy(lb[:, :], pb[:, 0:N5]), "a_rt", 1)
                    cx.setmark("a_pbfree", "a_rt")
                    v.wait_ge(sm.sem("a_rt"), vv)
                    i = v.tensor_tensor(arena[:, A_AO + head_etf * S + qmf * N5: A_AO + head_etf * S + (qmf + 1) * N5],
                                        pAV[:, avparf * N5:(avparf + 1) * N5], lb[:, :], ALU.mult)
                    sm.inc(i, "a_aoev", 1)
                    cx.setmark("a_pavfree_%d" % avparf, "a_aoev")

                for qm in range(NQM):
                    nkt = 4 * (qm + 1)
                    avpar = qm % 2
                    cx.wait(t, "a_pavfree_%d" % avpar)
                    pending_av = []

                    def emit_av(av):
                        ktf, sparf, ppar2f, svf, firstf, lastf = av
                        t.wait_ge(sm.sem("a_sv"), svf)
                        if firstf:
                            pfn, pfv = plfree[0]
                            t.wait_ge(sm.sem(pfn), pfv)
                        i = t.matmul(pl[0:1, 0:N5], cx.onesc[:, :],
                                     pch[:, ppar2f * N5:(ppar2f + 1) * N5],
                                     start=firstf, stop=lastf)
                        sm.inc(i, "a_mm", 1)
                        i = t.matmul(pAV[:, avpar * N5:(avpar + 1) * N5],
                                     arena[:, A_VTK + ktf * 128:A_VTK + (ktf + 1) * 128],
                                     pch[:, ppar2f * N5:(ppar2f + 1) * N5],
                                     start=firstf, stop=lastf)
                        mmv = sm.inc(i, "a_mm", 1)
                        cx.setmark("a_pchfree_%d" % ppar2f, "a_mm")

                    for kt in range(nkt):
                        spar = kt % 3
                        cx.wait(t, "a_psfree_%d" % spar)
                        i = t.matmul(pS[:, spar * N5:(spar + 1) * N5],
                                     arena[:, A_KT + kt * 128:A_KT + (kt + 1) * 128],
                                     arena[:, A_QT + qm * N5:A_QT + (qm + 1) * N5],
                                     start=True, stop=True)
                        mmv = sm.inc(i, "a_mm", 1)
                        if kt == 0 and pending_fin is not None:
                            emit_fin(pending_fin)
                            pending_fin = None
                        if len(pending_av) >= 2:
                            emit_av(pending_av.pop(0))
                        o = kt * 128 - qm * N5
                        if o >= 0:
                            mi = o // 128
                            v.wait_ge(sm.sem("a_mm"), mmv)
                            vv = sm.inc(v.tensor_tensor(pS[:, spar * N5:(spar + 1) * N5],
                                                        pS[:, spar * N5:(spar + 1) * N5],
                                                        cx.masks[:, mi * N5:(mi + 1) * N5], ALU.add), "a_msk", 1)
                            s.wait_ge(sm.sem("a_msk"), vv)
                        else:
                            s.wait_ge(sm.sem("a_mm"), mmv)
                        ppar2 = kt % 3
                        cx.wait(s, "a_pchfree_%d" % ppar2)
                        sv = sm.inc(s.activation(pch[:, ppar2 * N5:(ppar2 + 1) * N5],
                                                 pS[:, spar * N5:(spar + 1) * N5], AF.Exp,
                                                 bias=negc[:, 0:1], scale=ISQ), "a_sv", 1)
                        cx.setmark("a_psfree_%d" % spar, "a_sv")
                        pending_av.append((kt, spar, ppar2, sv, kt == 0, kt == nkt - 1))
                    for av in pending_av:
                        emit_av(av)
                    v.wait_ge(sm.sem("a_mm"), sm.val("a_mm"))
                    vv = sm.inc(v.tensor_copy(linv[0:1, avpar * N5:(avpar + 1) * N5],
                                              pl[0:1, 0:N5]), "a_rt", 1)
                    plfree[0] = ("a_rt", vv)
                    v.wait_ge(sm.sem("a_rt"), vv)
                    vv = sm.inc(v.reciprocal(linv[0:1, avpar * N5:(avpar + 1) * N5],
                                             linv[0:1, avpar * N5:(avpar + 1) * N5]), "a_rt", 1)
                    pending_fin = (qm, avpar, head_et, vv)
                emit_fin(pending_fin)
                cx.setmark("a_qfree", "a_mm")
            cx.setmark("a_kvfree", "a_mm")

        t.wait_ge(sm.sem("a_aoev"), sm.val("a_aoev"))
        t.wait_ge(sm.sem("a_wold"), sm.val("a_wold"))
        for nt in range(SNT):
            for ch2 in range(8):
                slot = ch2 % 2
                for ct in range(ch2 * 2, ch2 * 2 + 2):
                    ppar = ct % 2
                    cx.wait(t, "a_psfree_%d" % ppar)
                    for et in range(ET):
                        i = t.matmul(pS[:, ppar * N5:(ppar + 1) * N5],
                                     arena[:, A_WO + (et * CT + ct) * 128:A_WO + (et * CT + ct) * 128 + 128],
                                     arena[:, A_AO + et * S + nt * N5:A_AO + et * S + (nt + 1) * N5],
                                     start=(et == 0), stop=(et == ET - 1))
                    mmv = sm.inc(i, "a_mm", 1)
                    v.wait_ge(sm.sem("a_mm"), mmv)
                    if ct % 2 == 0:
                        cx.wait(v, "a_stgfree_%d" % slot)
                    i = v.tensor_copy(stg[:, (slot * 2 + ct % 2) * N5:(slot * 2 + ct % 2 + 1) * N5],
                                      pS[:, ppar * N5:(ppar + 1) * N5])
                    sm.inc(i, "a_yev", 1)
                    cx.setmark("a_psfree_%d" % ppar, "a_yev")
                sy.wait_ge(sm.sem("a_yev"), sm.val("a_yev"))
                i = sy.dma_start(shp_dst2(ch2, nt),
                                 stg[:, slot * 2 * N5:(slot + 1) * 2 * N5].rearrange("p (a t) -> p a t", a=2))
                sm.inc(i, "a_yst", 16)
                cx.setmark("a_stgfree_%d" % slot, "a_yst")
        sy.wait_ge(sm.sem("a_yst"), sm.val("a_yst"))


def emit_ffn(cx, es, xb_sb, xbf_h, wio, y_h, scr, tag=""):
    nc, sm, t, v, s, sy, g = cx.nc, cx.sm, cx.t, cx.v, cx.s, cx.sy, cx.g
    gw_sb, = scr["gw_sb"],
    s1_h, hh_h, sho_h = scr["s1_hbm"], scr["h_hbm"], scr["shout_hbm"]
    shg_h, shu_h, shd_h = wio["shg_T"], wio["shu_T"], wio["shd_T"]
    wg_h, wu_h, wd_h = wio["wg_T"], wio["wu_T"], wio["wd_T"]
    se_sb = scr["se_sb"]
    y3 = y_h.rearrange("(ct p) t -> p ct t", p=128)
    sho3 = sho_h.rearrange("(ct p) t -> p ct t", p=128)
    s1_3 = s1_h.rearrange("(ht p) t -> ht p t", p=128)
    NTT = T // 128

    with ExitStack() as ph:
        scrr = ph.enter_context(nc.sbuf_tensor("f_scr" + tag, [128, NTT * NE], F32))
        m1 = ph.enter_context(nc.sbuf_tensor("f_m1" + tag, [128, NTT], F32))
        m2 = ph.enter_context(nc.sbuf_tensor("f_m2" + tag, [128, NTT], F32))
        tmp8 = ph.enter_context(nc.sbuf_tensor("f_tmp8" + tag, [128, NE], F32))
        ind8 = ph.enter_context(nc.sbuf_tensor("f_ind8" + tag, [128, NE], F32))
        rs = ph.enter_context(nc.sbuf_tensor("f_rs" + tag, [128, NTT], F32))
        comb = ph.enter_context(nc.sbuf_tensor("f_comb" + tag, [128, NTT * NE], F32))
        combT = ph.enter_context(nc.sbuf_tensor("f_combT" + tag, [8, T], F32))

        sgb = ph.enter_context(nc.sbuf_tensor("f_sgb" + tag, [128, N5], F32))
        cball = ph.enter_context(nc.sbuf_tensor("f_cball" + tag, [128, NT * N5], F32))
        wk = ph.enter_context(nc.sbuf_tensor("f_wk" + tag, [128, CT * N5 * 5], BF16))
        wga = wk[:, 0:2 * CT * N5]
        wub = wk[:, 2 * CT * N5:4 * CT * N5]
        hrow = ph.enter_context(nc.sbuf_tensor("f_hrow" + tag, [128, 2 * T], BF16))
        t1 = ph.enter_context(nc.sbuf_tensor("f_t1" + tag, [128, N5], BF16))
        t2 = ph.enter_context(nc.sbuf_tensor("f_t2" + tag, [128, N5], F32))
        arena = wk
        stage = ph.enter_context(nc.sbuf_tensor("f_stage" + tag, [128, 8 * N5], F32))
        shon = ph.enter_context(nc.sbuf_tensor("f_shon" + tag, [128, 8 * N5], BF16))
        pg = ph.enter_context(nc.psum_tensor("f_pg" + tag, [128, 2 * N5], F32))
        pu = ph.enter_context(nc.psum_tensor("f_pu" + tag, [128, 2 * N5], F32))
        pd = ph.enter_context(nc.psum_tensor("f_pd" + tag, [128, 2 * N5], F32))
        pb = ph.enter_context(nc.psum_tensor("f_pb" + tag, [128, N5], F32))
        pt = ph.enter_context(nc.psum_tensor("f_pt" + tag, [128, 128], F32))

        SHD, S1C = 0, CT * N5 * 4
        WD, HNT = 0, CT * N5 * 4

        xbf3 = xbf_h.rearrange("(ct p) t -> ct p t", p=128)
        for ct in range(CT):
            par = ct % 2
            cx.wait(sy, "f_xffree_%d" % par)
            xfn = "f_xfld%d" % par
            sm.inc(sy.dma_start(stage[:, par * T:par * T + T], xbf3[ct]), xfn, 16)
            t.wait_ge(sm.sem(xfn), sm.val(xfn))
            for tt in range(NTT):
                i = t.matmul(pb[:, tt * NE:(tt + 1) * NE],
                             stage[:, par * T + tt * 128: par * T + tt * 128 + 128],
                             gw_sb[:, ct * NE:(ct + 1) * NE],
                             start=(ct == 0 and tt == 0), stop=(ct == CT - 1 and tt == NTT - 1))
            mmv = sm.inc(i, "f_mm", 1)
            cx.setmark("f_xffree_%d" % par, "f_mm")
        v.wait_ge(sm.sem("f_mm"), sm.val("f_mm"))
        ev = sm.inc(v.tensor_copy(scrr[:, 0:NTT * NE], pb[:, 0:NTT * NE]), "f_ev", 1)
        t.wait_ge(sm.sem("f_ev"), ev)
        cx.setmark("f_stagefree", "f_mm")

        for nt in range(NT):
            for ct in range(CT):
                i = t.matmul(pb[0:1, 0:N5], se_sb[:, ct:ct + 1],
                             xb_sb[:, ct * T + nt * N5: ct * T + (nt + 1) * N5],
                             start=(ct == 0), stop=(ct == CT - 1))
            mmv = sm.inc(i, "f_mm", 1)
            s.wait_ge(sm.sem("f_mm"), mmv)
            sv = sm.inc(s.activation(cball[0:1, nt * N5:(nt + 1) * N5], pb[0:1, 0:N5], AF.Sigmoid), "f_sv", 1)
            t.wait_ge(sm.sem("f_sv"), sv)

        v.wait_ge(sm.sem("f_ev"), sm.val("f_ev"))
        for tt in range(NTT):
            sl = scrr[:, tt * NE:(tt + 1) * NE]
            def edge(ins):
                vv = sm.inc(ins, "f_rt", 1)
                v.wait_ge(sm.sem("f_rt"), vv)
            edge(v.reduce_max(m1[:, tt:tt + 1], sl, axis=AX.X))
            edge(v.tensor_scalar(m1[:, tt:tt + 1], m1[:, tt:tt + 1], -1.0, None, ALU.mult))
            edge(v.tensor_scalar(tmp8[:, :], sl, m1[:, tt:tt + 1], None, ALU.add))
            edge(v.tensor_scalar(ind8[:, :], tmp8[:, :], 0.0, None, ALU.is_ge))
            edge(v.scalar_tensor_tensor(t2[:, 0:NE], ind8[:, :], -1e30, tmp8[:, :], ALU.mult, ALU.add))
            edge(v.reduce_max(m2[:, tt:tt + 1], t2[:, 0:NE], axis=AX.X))
            edge(v.tensor_scalar(ind8[:, :], tmp8[:, :], m2[:, tt:tt + 1], None, ALU.is_ge))
            vv = sm.inc(v.tensor_copy(t2[:, 0:NE], tmp8[:, :]), "f_vv", 1)
            s.wait_ge(sm.sem("f_vv"), vv)
            sv = sm.inc(s.activation(tmp8[:, :], t2[:, 0:NE], AF.Exp), "f_sv", 1)
            v.wait_ge(sm.sem("f_sv"), sv)
            edge(v.tensor_tensor(tmp8[:, :], tmp8[:, :], ind8[:, :], ALU.mult))
            edge(v.reduce_sum(rs[:, tt:tt + 1], tmp8[:, :], axis=AX.X))
            edge(v.reciprocal(rs[:, tt:tt + 1], rs[:, tt:tt + 1]))
            i = v.tensor_scalar(comb[:, tt * NE:(tt + 1) * NE], tmp8[:, :], rs[:, tt:tt + 1], None, ALU.mult)
            vv = sm.inc(i, "f_vv", 1)
            t.wait_ge(sm.sem("f_vv"), vv)
            cx.wait(t, "f_ptfree")
            i = t.transpose(pt[0:NE, 0:128], comb[:, tt * NE:(tt + 1) * NE], cx.identf[:, :])
            mmv = sm.inc(i, "f_mm", 1)
            v.wait_ge(sm.sem("f_mm"), mmv)
            sm.inc(v.tensor_copy(combT[0:NE, tt * 128:(tt + 1) * 128], pt[0:NE, 0:128]), "f_ptev", 1)
            cx.setmark("f_ptfree", "f_ptev")

        def gateup(n_htile, wsrc_g, wsrc_u, hout, use_cb):
            nch = n_htile // 4
            for ch in range(nch):
                par = ch % 2
                wofs = par * CT * N5
                cx.wait(g, "f_wbfree_%d" % par)
                wname = "f_wld%d" % par
                sm.inc(g.dma_start(wga[:, wofs:wofs + CT * N5].rearrange("p (ct h) -> p ct h", ct=CT), wsrc_g(ch)), wname, 16)
                sm.inc(g.dma_start(wub[:, wofs:wofs + CT * N5].rearrange("p (ct h) -> p ct h", ct=CT), wsrc_u(ch)), wname, 16)
                t.wait_ge(sm.sem(wname), sm.val(wname))
                for ht in range(4):
                    hpar = (ch * 4 + ht) % 2
                    cx.wait(v, "f_hrfree_%d" % hpar)
                    for nt in range(NT):
                        ppar = nt % 2
                        cx.wait(t, "f_gufree_%d" % ppar)
                        for ct in range(CT):
                            t.matmul(pg[:, ppar * N5:(ppar + 1) * N5],
                                     wga[:, wofs + ct * N5 + ht * 128: wofs + ct * N5 + ht * 128 + 128],
                                     xb_sb[:, ct * T + nt * N5: ct * T + (nt + 1) * N5],
                                     start=(ct == 0), stop=(ct == CT - 1))
                        for ct in range(CT):
                            i = t.matmul(pu[:, ppar * N5:(ppar + 1) * N5],
                                         wub[:, wofs + ct * N5 + ht * 128: wofs + ct * N5 + ht * 128 + 128],
                                         xb_sb[:, ct * T + nt * N5: ct * T + (nt + 1) * N5],
                                         start=(ct == 0), stop=(ct == CT - 1))
                        mmv = sm.inc(i, "f_mm", 1)
                        s.wait_ge(sm.sem("f_mm"), mmv)
                        sv = sm.inc(s.activation(t1[:, :], pg[:, ppar * N5:(ppar + 1) * N5], AF.Silu), "f_sv", 1)
                        v.wait_ge(sm.sem("f_sv"), sv)
                        if not use_cb:
                            i = v.tensor_tensor(hrow[:, hpar * T + nt * N5: hpar * T + (nt + 1) * N5],
                                                t1[:, :], pu[:, ppar * N5:(ppar + 1) * N5], ALU.mult)
                        else:
                            vv = sm.inc(v.tensor_tensor(t2[:, :], t1[:, :], pu[:, ppar * N5:(ppar + 1) * N5], ALU.mult), "f_rt", 1)
                            v.wait_ge(sm.sem("f_rt"), vv)
                            i = v.tensor_tensor(hrow[:, hpar * T + nt * N5: hpar * T + (nt + 1) * N5],
                                                t2[:, :], cball[:, nt * N5:(nt + 1) * N5], ALU.mult)
                        sm.inc(i, "f_guev", 1)
                        cx.setmark("f_gufree_%d" % ppar, "f_guev")
                    sy.wait_ge(sm.sem("f_guev"), sm.val("f_guev"))
                    hname = "f_hst%d" % hpar
                    sm.inc(sy.dma_start(hout(ch * 4 + ht), hrow[:, hpar * T:(hpar + 1) * T]), hname, 16)
                    cx.setmark("f_hrfree_%d" % hpar, hname)
                cx.setmark("f_wbfree_%d" % par, "f_mm")

        gateup(HTS,
               lambda ch: shg_h.rearrange("(ct p) h -> p ct h", p=128)[:, :, ch * N5:(ch + 1) * N5],
               lambda ch: shu_h.rearrange("(ct p) h -> p ct h", p=128)[:, :, ch * N5:(ch + 1) * N5],
               lambda ht: s1_3[ht, :, :], False)

        g.wait_ge(sm.sem("f_mm"), sm.val("f_mm"))
        g.wait_ge(sm.sem("f_hst0"), sm.val("f_hst0"))
        g.wait_ge(sm.sem("f_hst1"), sm.val("f_hst1"))
        sm.inc(g.dma_start(arena[:, SHD:SHD + HTS * CT * 128].rearrange("p (ht ct c) -> p ht ct c", ht=HTS, ct=CT),
                           shd_h.rearrange("(ht p) (ct c) -> p ht ct c", p=128, c=128)), "f_wldA", 16)
        t.wait_ge(sm.sem("f_wldA"), sm.val("f_wldA"))
        sy.wait_ge(sm.sem("f_hst0"), sm.val("f_hst0"))
        sy.wait_ge(sm.sem("f_hst1"), sm.val("f_hst1"))
        for nt in range(NT):
            cx.wait(sy, "f_s1free")
            sm.inc(sy.dma_start(arena[:, S1C:S1C + HTS * N5].rearrange("p (ht t) -> p ht t", ht=HTS),
                                s1_h.rearrange("(ht p) t -> p ht t", p=128)[:, :, nt * N5:(nt + 1) * N5]), "f_s1ld", 16)
            cx.wait(t, "f_pbfree")
            i = t.matmul(pb[:, 0:N5], cx.ones[0:1, :], cball[0:1, nt * N5:(nt + 1) * N5], start=True, stop=True)
            mmv = sm.inc(i, "f_mm", 1)
            v.wait_ge(sm.sem("f_mm"), mmv)
            bcv = sm.inc(v.tensor_copy(sgb[:, :], pb[:, 0:N5]), "f_bc", 1)
            cx.setmark("f_pbfree", "f_bc")
            t.wait_ge(sm.sem("f_s1ld"), sm.val("f_s1ld"))
            for ct in range(CT):
                ppar = ct % 2
                cx.wait(t, "f_dnfree_%d" % ppar)
                if ct == 0:
                    t.wait_ge(sm.sem("f_bc"), bcv)
                for ht in range(HTS):
                    i = t.matmul(pd[:, ppar * N5:(ppar + 1) * N5],
                                 arena[:, SHD + (ht * CT + ct) * 128:SHD + (ht * CT + ct) * 128 + 128],
                                 arena[:, S1C + ht * N5: S1C + (ht + 1) * N5],
                                 start=(ht == 0), stop=(ht == HTS - 1))
                mmv = sm.inc(i, "f_mm", 1)
                v.wait_ge(sm.sem("f_mm"), mmv)
                v.wait_ge(sm.sem("f_bc"), bcv)
                slot2 = (ct // 4) % 2
                if ct % 4 == 0:
                    cx.wait(v, "f_shonfree_%d" % slot2)
                i = v.tensor_tensor(shon[:, (ct % 8) * N5:((ct % 8) + 1) * N5],
                                    pd[:, ppar * N5:(ppar + 1) * N5], sgb[:, :], ALU.mult)
                sm.inc(i, "f_dn", 1)
                cx.setmark("f_dnfree_%d" % ppar, "f_dn")
                if ct % 4 == 3:
                    sy.wait_ge(sm.sem("f_dn"), sm.val("f_dn"))
                    q = ct // 4
                    sname = "f_shst%d" % slot2
                    sm.inc(sy.dma_start(sho3[:, q * 4:(q + 1) * 4, nt * N5:(nt + 1) * N5],
                                        shon[:, (slot2 * 4 % 8) * N5:(slot2 * 4 % 8 + 4) * N5].rearrange("p (ct t) -> p ct t", ct=4)), sname, 16)
                    cx.setmark("f_shonfree_%d" % slot2, sname)
            cx.setmark("f_s1free", "f_dn")
        cx.setmark("f_wbfree_0", "f_mm")
        cx.setmark("f_wbfree_1", "f_mm")
        cx.setmark("f_hrfree_0", "f_mm")
        cx.setmark("f_hrfree_1", "f_mm")
        cx.setmark("f_cballfree", "f_mm")

        for e in range(NE):
            sy.wait_ge(sm.sem("f_ptev"), sm.val("f_ptev"))
            cx.wait(sy, "f_cbrowfree")
            crv = sm.inc(sy.dma_start(stage[0:1, 0:T], combT[e:e + 1, :]), "f_crld", 16)
            t.wait_ge(sm.sem("f_crld"), crv)
            for nt in range(NT):
                cx.wait(t, "f_pbfree")
                i = t.matmul(pb[:, 0:N5], cx.ones[0:1, :], stage[0:1, nt * N5:(nt + 1) * N5], start=True, stop=True)
                mmv = sm.inc(i, "f_mm", 1)
                v.wait_ge(sm.sem("f_mm"), mmv)
                cx.wait(v, "f_cballfree")
                sm.inc(v.tensor_copy(cball[:, nt * N5:(nt + 1) * N5], pb[:, 0:N5]), "f_cbev", 1)
                cx.setmark("f_pbfree", "f_cbev")
            gateup(HT,
                   lambda ch, e=e: wg_h[e].rearrange("(ct p) h -> p ct h", p=128)[:, :, ch * N5:(ch + 1) * N5],
                   lambda ch, e=e: wu_h[e].rearrange("(ct p) h -> p ct h", p=128)[:, :, ch * N5:(ch + 1) * N5],
                   lambda ht, e=e: hh_h[e].rearrange("(ht p) t -> ht p t", p=128)[ht, :, :], True)
            cx.setmark("f_cballfree", "f_guev")
            cx.setmark("f_cbrowfree", "f_mm")

        NG = 8
        EPG = 1
        WDHALF = HT * CT * 128
        sy.wait_ge(sm.sem("f_hst0"), sm.val("f_hst0"))
        sy.wait_ge(sm.sem("f_hst1"), sm.val("f_hst1"))
        g.wait_ge(sm.sem("f_mm"), sm.val("f_mm"))
        for gi in range(NG):
            gpar = gi % 2
            if gi >= 2:
                cx.wait(g, "f_bigfree_%d" % gpar)
            wldn = "f_wldB%d" % gpar
            sm.inc(g.dma_start(
                arena[:, WD + gpar * WDHALF:WD + (gpar + 1) * WDHALF].rearrange("p (ht ct c) -> p ht ct c", ht=HT, ct=CT),
                wd_h[gi].rearrange("(ht p) (ct c) -> p ht ct c", p=128, c=128)), wldn, 16)
            t.wait_ge(sm.sem(wldn), sm.val(wldn))
            for nt in range(NT):
                hpar = (gi * NT + nt) % 2
                idx = gi * NT + nt
                if idx == 0:
                    sm.inc(sy.dma_start(
                        arena[:, HNT: HNT + HT * N5].rearrange("p (ht t) -> p ht t", ht=HT),
                        hh_h[0].rearrange("(ht p) t -> p ht t", p=128)[:, :, 0:N5]), "f_hld0", 16)
                nxt = idx + 1
                if nxt < NG * NT:
                    ngi, nnt = divmod(nxt, NT)
                    npar = nxt % 2
                    cx.wait(sy, "f_hntfree_%d" % npar)
                    sm.inc(sy.dma_start(
                        arena[:, HNT + npar * HT * N5: HNT + (npar + 1) * HT * N5].rearrange("p (ht t) -> p ht t", ht=HT),
                        hh_h[ngi].rearrange("(ht p) t -> p ht t", p=128)[:, :, nnt * N5:(nnt + 1) * N5]),
                        "f_hld%d" % npar, 16)
                t.wait_ge(sm.sem("f_hld%d" % hpar), sm.val("f_hld%d" % hpar))
                if gi == NG - 1:
                    cx.wait(sy, "f_shonfree_0")
                    cx.wait(sy, "f_shonfree_1")
                    sm.inc(sy.dma_start(shon[:, :].rearrange("p (ct t) -> p ct t", ct=8),
                                        sho3[:, 0:8, nt * N5:(nt + 1) * N5]), "f_shold", 16)
                    v.wait_ge(sm.sem("f_shold"), sm.val("f_shold"))
                for ct in range(CT):
                    ppar = ct % 2
                    cx.wait(t, "f_dnfree_%d" % ppar)
                    for ht in range(HT):
                        i = t.matmul(pd[:, ppar * N5:(ppar + 1) * N5],
                                     arena[:, WD + gpar * WDHALF + ht * CT * 128 + ct * 128:WD + gpar * WDHALF + ht * CT * 128 + ct * 128 + 128],
                                     arena[:, HNT + hpar * HT * N5 + ht * N5: HNT + hpar * HT * N5 + (ht + 1) * N5],
                                     start=(ht == 0), stop=(ht == HT - 1))
                    mmv = sm.inc(i, "f_mm", 1)
                    v.wait_ge(sm.sem("f_mm"), mmv)
                    if gi == NG - 1 and ct == 8:
                        sy.wait_ge(sm.sem("f_yev"), sm.val("f_yev"))
                        sm.inc(sy.dma_start(shon[:, :].rearrange("p (ct t) -> p ct t", ct=8),
                                            sho3[:, 8:16, nt * N5:(nt + 1) * N5]), "f_shold", 16)
                        v.wait_ge(sm.sem("f_shold"), sm.val("f_shold"))
                    slot = (ct // 4) % 2
                    if ct % 4 == 0:
                        cx.wait(v, "f_stagefree_%d" % slot)
                    sl_st = stage[:, (ct % 8) * N5:((ct % 8) + 1) * N5]
                    if gi == NG - 1:
                        i = v.scalar_tensor_tensor(sl_st, shon[:, (ct % 8) * N5:((ct % 8) + 1) * N5], 1.0,
                                                   pd[:, ppar * N5:(ppar + 1) * N5], ALU.mult, ALU.add)
                    else:
                        i = v.tensor_copy(sl_st, pd[:, ppar * N5:(ppar + 1) * N5])
                    sm.inc(i, "f_yev", 1)
                    cx.setmark("f_dnfree_%d" % ppar, "f_yev")
                    if ct % 4 == 3:
                        g.wait_ge(sm.sem("f_yev"), sm.val("f_yev"))
                        q = ct // 4
                        yname = "f_yst%d" % slot
                        dst = y3[:, q * 4:(q + 1) * 4, nt * N5:(nt + 1) * N5]
                        src = stage[:, (slot * 4 % 8) * N5:(slot * 4 % 8 + 4) * N5].rearrange("p (ct t) -> p ct t", ct=4)
                        if gi == 0:
                            i = g.dma_start(dst, src)
                        else:
                            i = g.dma_start(dst, src, accum_op=ALU.add)
                        sm.inc(i, yname, 16)
                        cx.setmark("f_stagefree_%d" % slot, yname)
                        if gi == NG - 1 and ct == CT - 1:
                            cx.setmark("f_shonfree_0", yname)
                            cx.setmark("f_shonfree_1", yname)
                cx.setmark("f_hntfree_%d" % hpar, "f_mm")
            cx.setmark("f_bigfree_%d" % gpar, "f_mm")
        g.wait_ge(sm.sem("f_yst0"), sm.val("f_yst0"))
        g.wait_ge(sm.sem("f_yst1"), sm.val("f_yst1"))


def emit_flag_write(cx, sh_flag, fvals_sb, myslot_v, phase):
    sm, sy = cx.sm, cx.sy
    sm.inc(sy.dma_start(sh_flag.ap()[bass.ds(myslot_v, 1), :], fvals_sb[phase - 1:phase, :]),
           "flg", 16)
    sy.wait_ge(sm.sem("flg"), sm.val("flg"))


def emit_poll(cx, sh_flag, flagr_sb, fval_want_reg, paslot_v, regs):
    sm, sy = cx.sm, cx.sy
    it_r, nr_r, fv_r, tmp_r = regs
    sm.nofence.add("po")
    sy.reg_mov(it_r, 0)
    sy.reg_mov(nr_r, 1)
    with sy.While(nr_r):
        sy.dma_start(flagr_sb[0:1, :], sh_flag.ap()[bass.ds(paslot_v, 1), :]).then_inc(sm.sem("po"), 16)
        sy.wait_ge(sm.sem("po"), 16)
        sy.sem_clear(sm.sem("po"))
        sy.reg_load(fv_r, flagr_sb[0:1, 0:1])
        sy.reg_add(it_r, it_r, 1)
        sy.reg_sub(fv_r, fv_r, fval_want_reg)
        sy.reg_alu(nr_r, fv_r, 0, op=ALU.is_lt)
        sy.reg_alu(tmp_r, fv_r, NPH, op=ALU.is_gt)
        sy.reg_alu(nr_r, nr_r, tmp_r, op=ALU.logical_or)
        sy.reg_alu(tmp_r, it_r, POLL_BOUND, op=ALU.is_lt)
        sy.reg_alu(nr_r, nr_r, tmp_r, op=ALU.logical_and)
    sm.inc(sy.nop(), "pollok", 1)
    for eng in (cx.t, cx.v, cx.s, cx.g):
        eng.wait_ge(sm.sem("pollok"), sm.val("pollok"))


def build_program(detect_races=True, upto=9):
    nc = bass.Bass("TRN2", target_bir_lowering=False, detect_race_conditions=detect_races)
    io = {}

    def ein(name, shape, dtype=F32):
        io[name] = nc.dram_tensor(name, shape, dtype, kind="ExternalInput").ap()
        return io[name]

    x_in = ein("x_in", [C, T])
    wqkv_a = ein("wqkv_a", [C, F_A])
    wo_a = ein("wo_a", [1024, C])
    wqkv_b = ein("wqkv_b", [C, F_B])
    wo_b = ein("wo_b", [1024, C])
    gw = ein("gate_wT", [C, NE])
    sew = ein("se_T", [128, CT])
    shg = ein("shg_T", [C, HS])
    shu = ein("shu_T", [C, HS])
    shd = ein("shd_T", [HS, C])
    wg = ein("wg_T", [NE, C, H])
    wu = ein("wu_T", [NE, C, H])
    wd = ein("wd_T", [NE, H, C])
    ones_h = ein("ones128", [1, 128])
    identb_h = ein("identb", [128, 128], BF16)
    identf_h = ein("identf", [128, 128])
    masks_h = ein("masks", [4 * 128, N5], BF16)
    fvals_h = ein("fvals", [NPH, 16], I32)
    route_h = ein("route", [1, 8], I32)

    y_out = nc.dram_tensor("y_out", [C, T], F32, kind="ExternalOutput").ap()

    qkv_hbm = nc.dram_tensor("qkv_hbm", [F_B, S], BF16).ap()
    xbloc_hbm = nc.dram_tensor("xbloc_hbm", [C, T], BF16).ap()
    ypart_hbm = nc.dram_tensor("ypart_hbm", [C, S], F32).ap()
    ppin_hbm = nc.dram_tensor("ppin_hbm", [C, T], F32).ap()
    mpin_hbm = nc.dram_tensor("mpin_hbm", [C, T], F32).ap()
    x_sc = [nc.dram_tensor("x_sc%d" % i, [C, T], F32).ap() for i in range(2)]
    xbf_hbm = nc.dram_tensor("xbf_hbm", [C, T], F32).ap()
    yffn_hbm = nc.dram_tensor("yffn_hbm", [C, T], F32).ap()
    s1_hbm = nc.dram_tensor("s1_hbm", [HS, T], BF16).ap()
    h_hbm = nc.dram_tensor("h_hbm", [NE, H, T], BF16).ap()
    shout_hbm = nc.dram_tensor("shout_hbm", [C, T], BF16).ap()

    sh_xb = nc.dram_tensor("sh_xb", [C, S], BF16, addr_space="Shared")
    sh_part = nc.dram_tensor("sh_part", [2, C, T], F32, addr_space="Shared")
    sh_flag = nc.dram_tensor("sh_flag", [2, 16], I32, addr_space="Shared")

    with ExitStack() as top:
        cx = Ctx(nc)
        sm = cx.sm
        t, v, s, sy, g = cx.t, cx.v, cx.s, cx.sy, cx.g

        cx.ones = top.enter_context(nc.sbuf_tensor("k_ones", [1, 128], F32))
        cx.onesf = top.enter_context(nc.sbuf_tensor("k_onesf", [128, 1], F32))
        cx.onesc = top.enter_context(nc.sbuf_tensor("k_onesc", [128, 1], BF16))
        cx.identb = top.enter_context(nc.sbuf_tensor("k_identb", [128, 128], BF16))
        cx.identf = top.enter_context(nc.sbuf_tensor("k_identf", [128, 128], F32))
        cx.masks = top.enter_context(nc.sbuf_tensor("k_masks", [128, 4 * N5], BF16))
        gw_sb = top.enter_context(nc.sbuf_tensor("k_gw", [128, CT * NE], F32))
        se_sb = top.enter_context(nc.sbuf_tensor("k_se", [128, CT], BF16))
        fvals_sb = top.enter_context(nc.sbuf_tensor("k_fvals", [NPH, 16], I32))
        flagr_sb = top.enter_context(nc.sbuf_tensor("k_flagr", [1, 16], I32))
        route_sb = top.enter_context(nc.sbuf_tensor("k_route", [1, 8], I32))
        blk = top.enter_context(nc.Block())

        @blk.sync
        def _(_sy):
            sm.inc(sy.dma_start(cx.ones[:, :], ones_h), "k_ld", 16)
            sm.inc(g.dma_start(cx.onesf[:, :], ones_h.rearrange("o p -> p o")), "k_ld2", 16)
            sm.inc(g.dma_start(cx.onesc[:, :], ones_h.rearrange("o p -> p o")), "k_ld2", 16)
            sm.inc(sy.dma_start(cx.identb[:, :], identb_h), "k_ld", 16)
            sm.inc(sy.dma_start(cx.identf[:, :], identf_h), "k_ld", 16)
            sm.inc(sy.dma_start(cx.masks[:, :].rearrange("p (m t) -> p m t", m=4),
                                masks_h.rearrange("(m p) t -> p m t", p=128)), "k_ld", 16)
            sm.inc(sy.dma_start(gw_sb[:, :].rearrange("p (ct e) -> p ct e", ct=CT),
                                gw.rearrange("(ct p) e -> ct p e", p=128).rearrange("ct p e -> p ct e")), "k_ld", 16)
            sm.inc(g.dma_start(se_sb[:, :], sew), "k_ld2", 16)
            sm.inc(sy.dma_start(fvals_sb[:, :], fvals_h), "k_ld", 16)
            sm.inc(sy.dma_start(route_sb[:, :], route_h), "k_ld", 16)
            cx.fence()

            my_r = sy.alloc_register("my_r")
            pa_r = sy.alloc_register("pa_r")
            col_r = sy.alloc_register("col_r")
            it_r = sy.alloc_register("it_r")
            nr_r = sy.alloc_register("nr_r")
            fv_r = sy.alloc_register("fv_r")
            tmp_r = sy.alloc_register("tmp_r")
            want_r = sy.alloc_register("want_r")
            sy.reg_load(my_r, route_sb[0:1, 0:1])
            sy.reg_load(pa_r, route_sb[0:1, 1:2])
            sy.reg_load(col_r, route_sb[0:1, 2:3])
            my_v = RuntimeValue(my_r, min_val=0, max_val=1)
            pa_v = RuntimeValue(pa_r, min_val=0, max_val=1)
            col_v = RuntimeValue(col_r, min_val=0, max_val=T)
            gmy_r = g.alloc_register("gmy_r")
            gpa_r = g.alloc_register("gpa_r")
            gcol_r = g.alloc_register("gcol_r")
            gpcol_r = g.alloc_register("gpcol_r")
            g.wait_ge(sm.sem("k_ld"), sm.val("k_ld"))
            g.reg_load(gmy_r, route_sb[0:1, 0:1])
            g.reg_load(gpa_r, route_sb[0:1, 1:2])
            g.reg_load(gcol_r, route_sb[0:1, 2:3])
            g.reg_load(gpcol_r, route_sb[0:1, 3:4])
            gmy_v = RuntimeValue(gmy_r, min_val=0, max_val=1)
            gpa_v = RuntimeValue(gpa_r, min_val=0, max_val=1)
            gcol_v = RuntimeValue(gcol_r, min_val=0, max_val=T)
            gpcol_v = RuntimeValue(gpcol_r, min_val=0, max_val=T)

            x_in3 = x_in.rearrange("(ct p) t -> ct p t", p=128)
            x_sc3 = [x.rearrange("(ct p) t -> ct p t", p=128) for x in x_sc]
            xbf3 = xbf_hbm.rearrange("(ct p) t -> ct p t", p=128)
            yffn3 = yffn_hbm.rearrange("(ct p) t -> ct p t", p=128)
            yout3 = y_out.rearrange("(ct p) t -> ct p t", p=128)
            sh_xb4 = sh_xb.ap().rearrange("(ct p) t -> ct p t", p=128)

            class Rows:
                def __init__(self, fn):
                    self.fn = fn
                def __getitem__(self, ct):
                    return self.fn(ct)

            xbloc3 = xbloc_hbm.rearrange("(ct p) t -> ct p t", p=128)

            def sh_xb_dst(ct, nt):
                return xbloc3[ct][:, nt * N5:(nt + 1) * N5]

            ypart3 = ypart_hbm.rearrange("(ct p) t -> p ct t", p=128)
            ppin3 = ppin_hbm.rearrange("(ct p) t -> ct p t", p=128)
            mpin3 = mpin_hbm.rearrange("(ct p) t -> ct p t", p=128)

            def part_src(w):
                return Rows(lambda ct: (ppin3 if w == 0 else mpin3)[ct])

            def shp_dst2(ch2, nt):
                return ypart3[:, ch2 * 2:(ch2 + 1) * 2, nt * N5:(nt + 1) * N5]

            def share_xb_copy():
                g.wait_ge(sm.sem("rl_shxb0"), sm.val("rl_shxb0"))
                g.wait_ge(sm.sem("rl_shxb1"), sm.val("rl_shxb1"))
                sm.inc(g.dma_start(sh_xb.ap()[:, bass.ds(gcol_v, T)], xbloc_hbm[:, :]), "dyncp", 16)

            def share_ypart_copy():
                g.wait_ge(sm.sem("a_yst"), sm.val("a_yst"))
                sm.inc(g.dma_start(sh_part.ap()[bass.ds(gmy_v, 1), :, :],
                                   ypart_hbm[:, bass.ds(gpcol_v, T)]), "dyncp", 16)

            def fetch_partials():
                sm.inc(g.dma_start(ppin_hbm[:, :], sh_part.ap()[bass.ds(gpa_v, 1), :, :]), "dyncp", 16)
                sm.inc(g.dma_start(mpin_hbm[:, :], ypart_hbm[:, bass.ds(gcol_v, T)]), "dyncp", 16)
                sy.wait_ge(sm.sem("dyncp"), sm.val("dyncp"))

            wio = dict(shg_T=shg, shu_T=shu, shd_T=shd, wg_T=wg, wu_T=wu, wd_T=wd)
            scr = dict(gw_sb=gw_sb, se_sb=se_sb, s1_hbm=s1_hbm, h_hbm=h_hbm, shout_hbm=shout_hbm)

            def poll(phase):
                sy.reg_load(want_r, fvals_sb[phase - 1:phase, 0:1])
                emit_poll(cx, sh_flag, flagr_sb, want_r, pa_v, (it_r, nr_r, fv_r, tmp_r))

            def post(phase):
                cx.fence()
                emit_flag_write(cx, sh_flag, fvals_sb, my_v, phase)

            emit_resid_ln(cx, top, [x_in3], x_sc3[0], None, sh_xb_dst, None, do_ln=True, tag="l1")
            share_xb_copy()
            post(PH_LN1)
            cur_x = x_sc3[0]

            if upto >= 2:
                poll(PH_LN1)
                emit_attn(cx, top, UNITS_A, F_A, sh_xb4, wqkv_a, wo_a, qkv_hbm, shp_dst2, tag="a1")
                share_ypart_copy()
                post(PH_A1)

            if upto >= 3:
                poll(PH_A1)
                fetch_partials()
                with nc.sbuf_tensor("k_xbffn1", [128, CT * T], BF16) as xb_ffn:
                    emit_resid_ln(cx, top, [x_in3, part_src(0), part_src(1)], x_sc3[0],
                                  xb_ffn, None, xbf3, do_ln=True, tag="r1")
                    cx.fence()
                    if upto >= 4:
                        emit_ffn(cx, top, xb_ffn, xbf_hbm, wio, yffn_hbm, scr, tag="f1")
                    cx.fence()

            if upto >= 5:
                emit_resid_ln(cx, top, [x_sc3[0], yffn3], x_sc3[1], None, sh_xb_dst, None,
                              do_ln=True, tag="r2")
                share_xb_copy()
                post(PH_R2)
                cur_x = x_sc3[1]

            if upto >= 6:
                poll(PH_R2)
                emit_attn(cx, top, UNITS_B, F_B, sh_xb4, wqkv_b, wo_b, qkv_hbm, shp_dst2, tag="a2")
                share_ypart_copy()
                post(PH_A2)

            if upto >= 7:
                poll(PH_A2)
                fetch_partials()
                with nc.sbuf_tensor("k_xbffn2", [128, CT * T], BF16) as xb_ffn:
                    emit_resid_ln(cx, top, [x_sc3[1], part_src(0), part_src(1)], x_sc3[0],
                                  xb_ffn, None, xbf3, do_ln=True, tag="r3")
                    cx.fence()
                    cur_x = x_sc3[0]
                    if upto >= 8:
                        emit_ffn(cx, top, xb_ffn, xbf_hbm, wio, yffn_hbm, scr, tag="f2")
                    cx.fence()

            if upto >= 9:
                emit_resid_ln(cx, top, [x_sc3[0], yffn3], yout3, None, None, None,
                              do_ln=False, tag="r4")
            else:
                emit_resid_ln(cx, top, [cur_x], yout3, None, None, None,
                              do_ln=False, tag="r4")
            cx.fence()

    return nc


_NC_CACHE = {}


def kernel(**inputs):
    x = np.asarray(inputs["L0_attn_a_x"])
    wqkv_a_full = np.asarray(inputs["L0_attn_a_weight_qkv"])
    wproj_a = np.asarray(inputs["L0_attn_a_weight_proj"])
    wqkv_b_full = np.asarray(inputs["L0_attn_b_w_qkv"])
    wo_b_full = np.asarray(inputs["L0_attn_b_w_o"])
    gate_w = np.asarray(inputs["L0_ffn_gate_w"])
    w_gate = np.asarray(inputs["L0_ffn_w_gate"])
    w_up = np.asarray(inputs["L0_ffn_w_up"])
    w_down = np.asarray(inputs["L0_ffn_w_down"])
    sh_gate = np.asarray(inputs["L0_ffn_sh_gate"])
    sh_up = np.asarray(inputs["L0_ffn_sh_up"])
    sh_down = np.asarray(inputs["L0_ffn_sh_down"])
    se_gate_w = np.asarray(inputs["L0_ffn_se_gate_w"])

    if "nc" not in _NC_CACHE:
        _NC_CACHE["nc"] = build_program()
    nc = _NC_CACHE["nc"]

    masks = np.zeros((4, 128, N5), dtype=np.float32)
    for mi, o in enumerate((0, 128, 256, 384)):
        for p in range(128):
            masks[mi, p, :] = np.where(np.arange(N5) >= p + o, 0.0, -1e30)
    nonce = int(np.random.randint(1 << 20, 1 << 30))
    fvals = np.zeros((NPH, 16), dtype=np.int32)
    for k in range(NPH):
        fvals[k, 0] = nonce + k + 1

    common = {
        "gate_wT": np.ascontiguousarray(gate_w.T),
        "se_T": np.ascontiguousarray(se_gate_w.reshape(CT, 128).T),
        "shg_T": np.ascontiguousarray(sh_gate.T),
        "shu_T": np.ascontiguousarray(sh_up.T),
        "shd_T": np.ascontiguousarray(sh_down.T),
        "wg_T": np.ascontiguousarray(np.transpose(w_gate, (0, 2, 1))),
        "wu_T": np.ascontiguousarray(np.transpose(w_up, (0, 2, 1))),
        "wd_T": np.ascontiguousarray(np.transpose(w_down, (0, 2, 1))),
        "ones128": np.ones((1, 128), dtype=np.float32),
        "identb": np.eye(128, dtype=ml_dtypes.bfloat16),
        "identf": np.eye(128, dtype=np.float32),
        "masks": masks.reshape(4 * 128, N5).astype(ml_dtypes.bfloat16),
        "fvals": fvals,
    }

    wpa_T = wproj_a.T
    wob_T = wo_b_full.T

    in_maps = []
    for c in range(8):
        b, m = c >> 1, c & 1
        ra = np.r_[2 * m * 768:(2 * m + 2) * 768]
        rb = np.concatenate([np.r_[h * 128:(h + 1) * 128,
                                   2048 + h * 128:2048 + (h + 1) * 128,
                                   4096 + h * 128:4096 + (h + 1) * 128]
                             for h in range(8 * m, 8 * m + 8)])
        im = dict(common)
        im["x_in"] = np.ascontiguousarray(x[b, m * T:(m + 1) * T, :].T)
        im["wqkv_a"] = np.ascontiguousarray(wqkv_a_full[ra].T)
        im["wo_a"] = np.ascontiguousarray(wpa_T[m * 1024:(m + 1) * 1024])
        im["wqkv_b"] = np.ascontiguousarray(wqkv_b_full[rb].T)
        im["wo_b"] = np.ascontiguousarray(wob_T[m * 1024:(m + 1) * 1024])
        im["route"] = np.array([[m, 1 - m, m * T, (1 - m) * T, 0, 0, 0, 0]], dtype=np.int32)
        in_maps.append(im)

    res = run_bass_kernel_spmd(nc, in_maps, core_ids=list(range(8)))

    out = np.empty((B, S, C), dtype=np.float32)
    for c in range(8):
        b, m = c >> 1, c & 1
        out[b, m * T:(m + 1) * T, :] = res.results[c]["y_out"].T
    return out


# version 09
# speedup vs baseline: 1.1520x; 1.1123x over previous
import numpy as np
import ml_dtypes
import concourse.bass as bass
import concourse.mybir as mybir
from concourse.bass_utils import run_bass_kernel_spmd
from concourse.expressions import RuntimeValue
from contextlib import ExitStack

F32 = mybir.dt.float32
BF16 = mybir.dt.bfloat16
I32 = mybir.dt.int32
AF = mybir.ActivationFunctionType
ALU = mybir.AluOpType
AX = mybir.AxisListType

B, S, C = 4, 4096, 2048
T = 2048
HS, H, NE = 2048, 1024, 8
N5 = 512
CT = C // 128
NT = T // N5
SNT = S // N5
NQM = S // N5
NKT = S // 128
HT, HTS = H // 128, HS // 128
ISQ = float(1.0 / np.sqrt(128.0))

UNITS_A = [dict(q=[0, 1, 2, 3], k=4, v=5), dict(q=[6, 7, 8, 9], k=10, v=11)]
F_A = 12 * 128
UNITS_B = [dict(q=[3 * h], k=3 * h + 1, v=3 * h + 2) for h in range(8)]
F_B = 24 * 128
ET = 8

PH_LN1, PH_A1, PH_R1, PH_F1, PH_R2, PH_A2, PH_R3, PH_F2 = 1, 2, 3, 4, 5, 6, 7, 8
NPH = 8
POLL_BOUND = 3000000


class Sems:
    def __init__(self, nc):
        self.nc = nc
        self.h = {}
        self.v = {}
        self.nofence = set()

    def sem(self, name):
        if name not in self.h:
            self.h[name] = self.nc.alloc_semaphore(name=name)
            self.v.setdefault(name, 0)
        return self.h[name]

    def inc(self, ins, name, k):
        ins.then_inc(self.sem(name), k)
        self.v[name] += k
        return self.v[name]

    def val(self, name):
        self.sem(name)
        return self.v[name]


class Ctx:
    def __init__(self, nc):
        self.nc = nc
        self.sm = Sems(nc)
        self.mark = {}
        self.g = nc.gpsimd
        self.sy = nc.sync
        self.t = nc.tensor
        self.v = nc.vector
        self.s = nc.scalar

    def wait(self, eng, key):
        if key in self.mark:
            name, val = self.mark[key]
            if val > 0:
                eng.wait_ge(self.sm.sem(name), val)

    def setmark(self, key, name):
        self.mark[key] = (name, self.sm.val(name))

    def fence(self):
        for eng in (self.t, self.v, self.s, self.sy, self.g):
            for name, val in sorted(self.sm.v.items()):
                if val > 0 and name not in self.sm.nofence:
                    eng.wait_ge(self.sm.sem(name), val)
        self.mark.clear()


def emit_resid_ln(cx, es, srcs, x_out, xb_sb_dst, sh_xb_dst=None, xbf_out=None,
                  do_ln=True, tag=""):
    nc, sm, t, v, s, sy, g = cx.nc, cx.sm, cx.t, cx.v, cx.s, cx.sy, cx.g
    nsrc = len(srcs)
    with ExitStack() as ph:
        row = ph.enter_context(nc.sbuf_tensor("r_row" + tag, [128, 2 * nsrc * T], F32))
        xnew = ph.enter_context(nc.sbuf_tensor("r_xnew" + tag, [128, 2 * T], F32))
        sqr = ph.enter_context(nc.sbuf_tensor("r_sqr" + tag, [128, T], F32))
        accA = ph.enter_context(nc.sbuf_tensor("r_accA" + tag, [128, T], F32))
        accB = ph.enter_context(nc.sbuf_tensor("r_accB" + tag, [128, T], F32))

        rb = ph.enter_context(nc.sbuf_tensor("r_rb" + tag, [128, 2 * N5], F32))
        mrb = ph.enter_context(nc.sbuf_tensor("r_mrb" + tag, [128, 2 * N5], F32))
        xf1 = ph.enter_context(nc.sbuf_tensor("r_xf1" + tag, [128, N5], F32))
        xbrow = ph.enter_context(nc.sbuf_tensor("r_xbrow" + tag, [128, 2 * T], BF16))
        xfrow = ph.enter_context(nc.sbuf_tensor("r_xfrow" + tag, [128, 2 * T], F32))
        pln = ph.enter_context(nc.psum_tensor("r_pln" + tag, [1, N5], F32))
        pbc = ph.enter_context(nc.psum_tensor("r_pbc" + tag, [128, 2 * N5], F32))

        ld = "rl_ld"

        for ct in range(CT):
            par = ct % 2
            cx.wait(sy, "rl_rowfree_%d" % par)
            for si, src in enumerate(srcs):
                sm.inc(sy.dma_start(row[:, (par * nsrc + si) * T:(par * nsrc + si + 1) * T], src[ct]),
                       ld + str(par), 16)
            v.wait_ge(sm.sem(ld + str(par)), sm.val(ld + str(par)))
            cx.wait(v, "rl_xnfree_%d" % par)
            base = par * nsrc * T
            if nsrc == 1:
                i = v.tensor_copy(xnew[:, par * T:(par + 1) * T], row[:, base:base + T])
            elif nsrc == 2:
                i = v.tensor_tensor(xnew[:, par * T:(par + 1) * T], row[:, base:base + T],
                                    row[:, base + T:base + 2 * T], ALU.add)
            else:
                vv = sm.inc(v.tensor_tensor(xnew[:, par * T:(par + 1) * T], row[:, base:base + T],
                                            row[:, base + T:base + 2 * T], ALU.add), "rl_vv", 1)
                v.wait_ge(sm.sem("rl_vv"), vv)
                i = v.tensor_tensor(xnew[:, par * T:(par + 1) * T], xnew[:, par * T:(par + 1) * T],
                                    row[:, base + 2 * T:base + 3 * T], ALU.add)
            vv = sm.inc(i, "rl_vv", 1)
            cx.setmark("rl_rowfree_%d" % par, ld + str(par))
            v.wait_ge(sm.sem("rl_vv"), vv)
            sy.wait_ge(sm.sem("rl_vv"), vv)
            sm.inc(sy.dma_start(x_out[ct], xnew[:, par * T:(par + 1) * T]), "rl_st%d" % par, 16)
            cx.setmark("rl_rowfree_%d" % par, "rl_vv")
            self_mark_val = sm.v["rl_vv"]
            if do_ln:
                s.wait_ge(sm.sem("rl_vv"), vv)
                cx.wait(s, "rl_sqfree")
                svv = sm.inc(s.activation(sqr[:, :], xnew[:, par * T:(par + 1) * T], AF.Square), "rl_sv", 1)
                if ct == 0:
                    i = v.tensor_copy(accA[:, :], xnew[:, par * T:(par + 1) * T])
                else:
                    i = v.tensor_tensor(accA[:, :], accA[:, :], xnew[:, par * T:(par + 1) * T], ALU.add)
                vv2 = sm.inc(i, "rl_vv", 1)
                v.wait_ge(sm.sem("rl_sv"), svv)
                v.wait_ge(sm.sem("rl_vv"), vv2)
                if ct == 0:
                    i = v.tensor_copy(accB[:, :], sqr[:, :])
                else:
                    i = v.tensor_tensor(accB[:, :], accB[:, :], sqr[:, :], ALU.add)
                vv3 = sm.inc(i, "rl_vv", 1)
                cx.setmark("rl_sqfree", "rl_vv")
                v.wait_ge(sm.sem("rl_vv"), vv3)
            cx.setmark("rl_xnfree_%d" % par, "rl_st%d" % par)

        if not do_ln:
            sy.wait_ge(sm.sem("rl_st0"), sm.val("rl_st0"))
            sy.wait_ge(sm.sem("rl_st1"), sm.val("rl_st1"))
            return

        def edge(ins):
            vv = sm.inc(ins, "rl_vv", 1)
            v.wait_ge(sm.sem("rl_vv"), vv)

        for acc, dst in ((accA, sqr), (accB, None)):
            for nt in range(NT):
                cx.wait(t, "rl_plnfree")
                t.wait_ge(sm.sem("rl_vv"), sm.val("rl_vv"))
                t.wait_ge(sm.sem("rl_sv"), sm.val("rl_sv"))
                i = t.matmul(pln[0:1, :], cx.onesf[:, :], acc[:, nt * N5:(nt + 1) * N5],
                             start=True, stop=True)
                mmv = sm.inc(i, "rl_mm", 1)
                v.wait_ge(sm.sem("rl_mm"), mmv)
                tgt = sqr[0:1, nt * N5:(nt + 1) * N5] if acc is accA else accB[0:1, nt * N5:(nt + 1) * N5]
                edge(v.tensor_copy(tgt, pln[0:1, :]))
                cx.setmark("rl_plnfree", "rl_vv")
        c1 = float(1.0 / C)
        edge(v.tensor_scalar(sqr[0:1, 0:T], sqr[0:1, 0:T], c1, None, ALU.mult))
        edge(v.tensor_tensor(accA[0:1, :], sqr[0:1, 0:T], sqr[0:1, 0:T], ALU.mult))
        edge(v.scalar_tensor_tensor(accB[0:1, :], accB[0:1, :], c1, accA[0:1, :],
                                    ALU.mult, ALU.subtract))
        edge(v.tensor_scalar(accB[0:1, :], accB[0:1, :], 1e-5, None, ALU.add))
        s.wait_ge(sm.sem("rl_vv"), sm.val("rl_vv"))
        svv = sm.inc(s.activation(accB[0:1, :], accB[0:1, :], AF.Ln), "rl_sv", 1)
        s.wait_ge(sm.sem("rl_sv"), svv)
        svv = sm.inc(s.activation(accB[0:1, :], accB[0:1, :], AF.Exp, scale=-0.5), "rl_sv", 1)
        v.wait_ge(sm.sem("rl_sv"), svv)
        edge(v.tensor_tensor(accA[0:1, :], sqr[0:1, 0:T], accB[0:1, :], ALU.mult))
        t.wait_ge(sm.sem("rl_vv"), sm.val("rl_vv"))

        sy.wait_ge(sm.sem("rl_st0"), sm.val("rl_st0"))
        sy.wait_ge(sm.sem("rl_st1"), sm.val("rl_st1"))
        for nt in range(NT):
            bpar = nt % 2
            cx.wait(t, "rl_bcfree_%d" % bpar)
            i = t.matmul(pbc[:, bpar * N5:(bpar + 1) * N5], cx.ones[0:1, :],
                         accB[0:1, nt * N5:(nt + 1) * N5], start=True, stop=True)
            mmv = sm.inc(i, "rl_mm", 1)
            v.wait_ge(sm.sem("rl_mm"), mmv)
            edge(v.tensor_copy(rb[:, bpar * N5:(bpar + 1) * N5], pbc[:, bpar * N5:(bpar + 1) * N5]))
            t.wait_ge(sm.sem("rl_vv"), sm.val("rl_vv"))
            i = t.matmul(pbc[:, bpar * N5:(bpar + 1) * N5], cx.ones[0:1, :],
                         accA[0:1, nt * N5:(nt + 1) * N5], start=True, stop=True)
            mmv = sm.inc(i, "rl_mm", 1)
            v.wait_ge(sm.sem("rl_mm"), mmv)
            edge(v.tensor_copy(mrb[:, bpar * N5:(bpar + 1) * N5], pbc[:, bpar * N5:(bpar + 1) * N5]))
            cx.setmark("rl_bcfree_%d" % bpar, "rl_vv")
            for ct in range(CT):
                par = ct % 2
                cx.wait(sy, "rl_xffree_%d" % par)
                sm.inc(sy.dma_start(xfrow[:, par * T + nt * N5:par * T + (nt + 1) * N5],
                                    x_out[ct][:, nt * N5:(nt + 1) * N5]), "rl_xf%d" % par, 16)
                v.wait_ge(sm.sem("rl_xf%d" % par), sm.val("rl_xf%d" % par))
                vv = sm.inc(v.tensor_tensor(xf1[:, :], xfrow[:, par * T + nt * N5:par * T + (nt + 1) * N5],
                                            rb[:, bpar * N5:(bpar + 1) * N5], ALU.mult), "rl_vv", 1)
                v.wait_ge(sm.sem("rl_vv"), vv)
                cx.wait(v, "rl_xbfree_%d" % par)
                if xbf_out is not None:
                    vv = sm.inc(v.tensor_tensor(xfrow[:, par * T + nt * N5:par * T + (nt + 1) * N5],
                                                xf1[:, :], mrb[:, bpar * N5:(bpar + 1) * N5],
                                                ALU.subtract), "rl_vv", 1)
                    v.wait_ge(sm.sem("rl_vv"), vv)
                    sy.wait_ge(sm.sem("rl_vv"), vv)
                    sm.inc(sy.dma_start(xbf_out[ct][:, nt * N5:(nt + 1) * N5],
                                        xfrow[:, par * T + nt * N5:par * T + (nt + 1) * N5]),
                           "rl_xbf%d" % par, 16)
                    cx.setmark("rl_xffree_%d" % par, "rl_xbf%d" % par)
                    i = v.tensor_copy(xbrow[:, par * T + nt * N5:par * T + (nt + 1) * N5],
                                      xfrow[:, par * T + nt * N5:par * T + (nt + 1) * N5])
                else:
                    cx.setmark("rl_xffree_%d" % par, "rl_vv")
                    i = v.tensor_tensor(xbrow[:, par * T + nt * N5:par * T + (nt + 1) * N5],
                                        xf1[:, :], mrb[:, bpar * N5:(bpar + 1) * N5], ALU.subtract)
                vv = sm.inc(i, "rl_vv", 1)
                if xb_sb_dst is not None:
                    v.wait_ge(sm.sem("rl_vv"), vv)
                    vv = sm.inc(v.tensor_copy(xb_sb_dst[:, ct * T + nt * N5:ct * T + (nt + 1) * N5],
                                              xbrow[:, par * T + nt * N5:par * T + (nt + 1) * N5]),
                                "rl_vv", 1)
                if sh_xb_dst is not None:
                    sy.wait_ge(sm.sem("rl_vv"), vv)
                    sm.inc(sy.dma_start(sh_xb_dst(ct, nt),
                                        xbrow[:, par * T + nt * N5:par * T + (nt + 1) * N5]),
                           "rl_shxb%d" % par, 16)
                    cx.setmark("rl_xbfree_%d" % par, "rl_shxb%d" % par)
                else:
                    cx.setmark("rl_xbfree_%d" % par, "rl_vv")


def emit_attn(cx, es, units, FF, xb_src3, wqkv_h, wo_h, qkv_h, shp_dst2, tag=""):
    nc, sm, t, v, s, sy, g = cx.nc, cx.sm, cx.t, cx.v, cx.s, cx.sy, cx.g
    OF = FF // 128
    NCH = OF * 128 // N5
    with ExitStack() as ph:
        arena = ph.enter_context(nc.sbuf_tensor("a_arena" + tag, [128, CT * S], BF16))
        wch = ph.enter_context(nc.sbuf_tensor("a_wch" + tag, [128, 2 * CT * N5], BF16))
        qrow = ph.enter_context(nc.sbuf_tensor("a_qrow" + tag, [128, 2 * S], BF16))
        sq = ph.enter_context(nc.sbuf_tensor("a_sq" + tag, [128, 4 * N5], BF16))
        nred = ph.enter_context(nc.sbuf_tensor("a_nred" + tag, [1, 8], F32))

        nmax = ph.enter_context(nc.sbuf_tensor("a_nmax" + tag, [1, 32], F32))
        csb = ph.enter_context(nc.sbuf_tensor("a_c" + tag, [1, 4], F32))
        negc = ph.enter_context(nc.sbuf_tensor("a_negc" + tag, [128, 1], F32))
        pch = ph.enter_context(nc.sbuf_tensor("a_pch" + tag, [128, 3 * N5], BF16))
        linv = ph.enter_context(nc.sbuf_tensor("a_linv" + tag, [1, 2 * N5], F32))
        lb = ph.enter_context(nc.sbuf_tensor("a_lb" + tag, [128, N5], F32))
        stg = ph.enter_context(nc.sbuf_tensor("a_stg" + tag, [128, 4 * N5], F32))
        pS = ph.enter_context(nc.psum_tensor("a_pS" + tag, [128, 3 * N5], F32))
        pAV = ph.enter_context(nc.psum_tensor("a_pAV" + tag, [128, 2 * N5], F32))
        pl = ph.enter_context(nc.psum_tensor("a_pl" + tag, [1, N5], F32))
        pb = ph.enter_context(nc.psum_tensor("a_pb" + tag, [128, N5], F32))
        pvt = ph.enter_context(nc.psum_tensor("a_pvt" + tag, [128, 128], BF16))

        A_AO = 0
        A_KT = ET * S
        A_VT = A_KT + S
        A_VTK = A_VT + S
        A_QT = A_VTK + S
        A_WO = A_QT + S

        qkv3 = qkv_h.rearrange("(of p) t -> of p t", p=128)

        sm.inc(g.dma_start(arena[:, 0:CT * S].rearrange("p (ct t) -> p ct t", ct=CT),
                           xb_src3.rearrange("ct p t -> p ct t")), "a_xbld", 16)
        for eng in (t, v, s):
            eng.wait_ge(sm.sem("a_xbld"), sm.val("a_xbld"))

        for ch in range(NCH):
            par = ch % 2
            wofs = par * CT * N5
            cx.wait(g, "a_wchfree_%d" % par)
            wname = "a_wld%d" % par
            sm.inc(g.dma_start(
                wch[:, wofs:wofs + CT * N5].rearrange("p (ct f) -> p ct f", ct=CT),
                wqkv_h.rearrange("(ct p) f -> p ct f", p=128)[:, :, ch * N5:(ch + 1) * N5]), wname, 16)
            t.wait_ge(sm.sem(wname), sm.val(wname))
            for fo in range(4):
                of = ch * 4 + fo
                qpar = of % 2
                cx.wait(s, "a_qrowfree_%d" % qpar)
                for nt in range(SNT):
                    ppar = nt % 2
                    cx.wait(t, "a_psfree_%d" % ppar)
                    for ct in range(CT):
                        i = t.matmul(
                            pS[:, ppar * N5:(ppar + 1) * N5],
                            wch[:, wofs + ct * N5 + fo * 128: wofs + ct * N5 + fo * 128 + 128],
                            arena[:, ct * S + nt * N5: ct * S + (nt + 1) * N5],
                            start=(ct == 0), stop=(ct == CT - 1))
                    mmv = sm.inc(i, "a_mm", 1)
                    s.wait_ge(sm.sem("a_mm"), mmv)
                    sv = sm.inc(s.activation(qrow[:, qpar * S + nt * N5: qpar * S + (nt + 1) * N5],
                                             pS[:, ppar * N5:(ppar + 1) * N5], AF.Copy), "a_sv", 1)
                    cx.setmark("a_psfree_%d" % ppar, "a_sv")
                    if nt % 4 == 0:
                        cx.wait(s, "a_sqfree")
                    sv2 = sm.inc(s.activation(sq[:, (nt % 4) * N5:((nt % 4) + 1) * N5],
                                              pS[:, ppar * N5:(ppar + 1) * N5], AF.Square), "a_sv", 1)
                    cx.setmark("a_psfree_%d" % ppar, "a_sv")
                    if nt % 4 == 3:
                        t.wait_ge(sm.sem("a_sv"), sv2)
                        half = nt // 4
                        for q4 in range(4):
                            plpar = q4 % 2
                            pno = pl[0:1, 0:N5] if plpar == 0 else pb[0:1, 0:N5]
                            cx.wait(t, "a_pnofree_%d" % plpar)
                            i = t.matmul(pno, cx.onesc[:, :],
                                         sq[:, q4 * N5:(q4 + 1) * N5], start=True, stop=True)
                            mmv = sm.inc(i, "a_mm", 1)
                            cx.setmark("a_sqfree", "a_mm")
                            v.wait_ge(sm.sem("a_mm"), mmv)
                            vv = sm.inc(v.tensor_copy(stg[0:1, q4 * N5:(q4 + 1) * N5],
                                                      pno), "a_vv", 1)
                            cx.setmark("a_pnofree_%d" % plpar, "a_vv")
                        v.wait_ge(sm.sem("a_vv"), sm.val("a_vv"))
                        vv = sm.inc(v.reduce_max(nred[0:1, 0:4],
                                                 stg[0:1, 0:4 * N5].rearrange("o (a b) -> o a b", a=4), axis=AX.X), "a_vv", 1)
                        v.wait_ge(sm.sem("a_vv"), vv)
                        if half == 0:
                            vv = sm.inc(v.reduce_max(nmax[0:1, of:of + 1], nred[0:1, 0:4], axis=AX.X), "a_vv", 1)
                        else:
                            vv = sm.inc(v.reduce_max(nred[0:1, 4:5], nred[0:1, 0:4], axis=AX.X), "a_vv", 1)
                            v.wait_ge(sm.sem("a_vv"), vv)
                            vv = sm.inc(v.tensor_tensor(nmax[0:1, of:of + 1], nmax[0:1, of:of + 1],
                                                        nred[0:1, 4:5], ALU.max), "a_vv", 1)
                        v.wait_ge(sm.sem("a_vv"), vv)
                sy.wait_ge(sm.sem("a_sv"), sm.val("a_sv"))
                sm.inc(sy.dma_start(qkv3[of], qrow[:, qpar * S:(qpar + 1) * S]), "a_qst%d" % qpar, 16)
                cx.setmark("a_qrowfree_%d" % qpar, "a_qst%d" % qpar)
            cx.setmark("a_wchfree_%d" % par, "a_mm")

        sy.wait_ge(sm.sem("a_qst0"), sm.val("a_qst0"))
        sy.wait_ge(sm.sem("a_qst1"), sm.val("a_qst1"))
        g.wait_ge(sm.sem("a_mm"), sm.val("a_mm"))
        sm.inc(g.dma_start(arena[:, A_WO:A_WO + ET * CT * 128].rearrange("p (et ct c) -> p et ct c", et=ET, ct=CT),
                           wo_h.rearrange("(et p) (ct c) -> p et ct c", p=128, c=128)), "a_wold", 16)
        sy.wait_ge(sm.sem("a_mm"), sm.val("a_mm"))
        v.wait_ge(sm.sem("a_mm"), sm.val("a_mm"))
        cx.setmark("a_pbfree", "a_vv")
        plfree = [("a_vv", sm.val("a_vv"))]

        for ui, u in enumerate(units):
            cx.wait(sy, "a_kvfree")
            sm.inc(sy.dma_start(arena[:, A_KT:A_KT + S], qkv3[u["k"]]), "a_kvld", 16)
            sm.inc(sy.dma_start(arena[:, A_VT:A_VT + S], qkv3[u["v"]]), "a_kvld", 16)
            t.wait_ge(sm.sem("a_kvld"), sm.val("a_kvld"))
            vtev_base = sm.val("a_vtev")

            def emit_vt(kt):
                cx.wait(t, "a_pvtfree")
                i = t.transpose(pvt[:, :], arena[:, A_VT + kt * 128:A_VT + (kt + 1) * 128], cx.identb[:, :])
                mmv = sm.inc(i, "a_mm", 1)
                v.wait_ge(sm.sem("a_mm"), mmv)
                sm.inc(v.tensor_copy(arena[:, A_VTK + kt * 128:A_VTK + (kt + 1) * 128], pvt[:, :]),
                       "a_vtev", 1)
                cx.setmark("a_pvtfree", "a_vtev")

            for kt in range(4):
                emit_vt(kt)
            vt_next = [4]
            for qi, qof in enumerate(u["q"]):
                head_et = ui * len(u["q"]) + qi
                cx.wait(sy, "a_qfree")
                qv = sm.inc(sy.dma_start(arena[:, A_QT:A_QT + S], qkv3[qof]), "a_qld", 16)
                t.wait_ge(sm.sem("a_qld"), qv)
                if qi == 1:
                    t.wait_ge(sm.sem("a_vtev"), vtev_base + NKT)

                def edge(ins):
                    vv = sm.inc(ins, "a_rt", 1)
                    v.wait_ge(sm.sem("a_rt"), vv)
                edge(v.tensor_tensor(csb[0:1, 0:1], nmax[0:1, qof:qof + 1],
                                     nmax[0:1, u["k"]:u["k"] + 1], ALU.mult))
                s.wait_ge(sm.sem("a_rt"), sm.val("a_rt"))
                sv = sm.inc(s.activation(csb[0:1, 1:2], csb[0:1, 0:1], AF.Sqrt, scale=1.0 / 128.0), "a_sv", 1)
                v.wait_ge(sm.sem("a_sv"), sv)
                edge(v.tensor_scalar(csb[0:1, 2:3], csb[0:1, 1:2], -1.0, None, ALU.mult))
                t.wait_ge(sm.sem("a_rt"), sm.val("a_rt"))
                cx.wait(t, "a_pbfree")
                i = t.matmul(pb[:, 0:1], cx.ones[0:1, :], csb[0:1, 2:3], start=True, stop=True)
                mmv = sm.inc(i, "a_mm", 1)
                v.wait_ge(sm.sem("a_mm"), mmv)
                vv = sm.inc(v.tensor_copy(negc[:, :], pb[:, 0:1]), "a_nc", 1)
                cx.setmark("a_pbfree", "a_nc")
                s.wait_ge(sm.sem("a_nc"), sm.val("a_nc"))
                pending_fin = None

                def emit_fin(fin):
                    qmf, avparf, head_etf, rtv = fin
                    t.wait_ge(sm.sem("a_rt"), rtv)
                    cx.wait(t, "a_pbfree")
                    i = t.matmul(pb[:, 0:N5], cx.ones[0:1, :], linv[0:1, avparf * N5:(avparf + 1) * N5],
                                 start=True, stop=True)
                    mmv = sm.inc(i, "a_mm", 1)
                    v.wait_ge(sm.sem("a_mm"), mmv)
                    vv = sm.inc(v.tensor_copy(lb[:, :], pb[:, 0:N5]), "a_rt", 1)
                    cx.setmark("a_pbfree", "a_rt")
                    v.wait_ge(sm.sem("a_rt"), vv)
                    i = v.tensor_tensor(arena[:, A_AO + head_etf * S + qmf * N5: A_AO + head_etf * S + (qmf + 1) * N5],
                                        pAV[:, avparf * N5:(avparf + 1) * N5], lb[:, :], ALU.mult)
                    sm.inc(i, "a_aoev", 1)
                    cx.setmark("a_pavfree_%d" % avparf, "a_aoev")

                for qm in range(NQM):
                    nkt = 4 * (qm + 1)
                    avpar = qm % 2
                    cx.wait(t, "a_pavfree_%d" % avpar)
                    pending_av = []

                    def emit_av(av):
                        ktf, sparf, ppar2f, svf, firstf, lastf = av
                        t.wait_ge(sm.sem("a_sv"), svf)
                        if qi == 0:
                            t.wait_ge(sm.sem("a_vtev"), vtev_base + ktf + 1)
                        if firstf:
                            pfn, pfv = plfree[0]
                            t.wait_ge(sm.sem(pfn), pfv)
                        i = t.matmul(pl[0:1, 0:N5], cx.onesc[:, :],
                                     pch[:, ppar2f * N5:(ppar2f + 1) * N5],
                                     start=firstf, stop=lastf)
                        sm.inc(i, "a_mm", 1)
                        i = t.matmul(pAV[:, avpar * N5:(avpar + 1) * N5],
                                     arena[:, A_VTK + ktf * 128:A_VTK + (ktf + 1) * 128],
                                     pch[:, ppar2f * N5:(ppar2f + 1) * N5],
                                     start=firstf, stop=lastf)
                        mmv = sm.inc(i, "a_mm", 1)
                        cx.setmark("a_pchfree_%d" % ppar2f, "a_mm")

                    for kt in range(nkt):
                        if qi == 0 and vt_next[0] < NKT:
                            emit_vt(vt_next[0])
                            vt_next[0] += 1
                        spar = kt % 3
                        cx.wait(t, "a_psfree_%d" % spar)
                        i = t.matmul(pS[:, spar * N5:(spar + 1) * N5],
                                     arena[:, A_KT + kt * 128:A_KT + (kt + 1) * 128],
                                     arena[:, A_QT + qm * N5:A_QT + (qm + 1) * N5],
                                     start=True, stop=True)
                        mmv = sm.inc(i, "a_mm", 1)
                        if kt == 0 and pending_fin is not None:
                            emit_fin(pending_fin)
                            pending_fin = None
                        if len(pending_av) >= 2:
                            emit_av(pending_av.pop(0))
                        o = kt * 128 - qm * N5
                        if o >= 0:
                            mi = o // 128
                            v.wait_ge(sm.sem("a_mm"), mmv)
                            vv = sm.inc(v.tensor_tensor(pS[:, spar * N5:(spar + 1) * N5],
                                                        pS[:, spar * N5:(spar + 1) * N5],
                                                        cx.masks[:, mi * N5:(mi + 1) * N5], ALU.add), "a_msk", 1)
                            s.wait_ge(sm.sem("a_msk"), vv)
                        else:
                            s.wait_ge(sm.sem("a_mm"), mmv)
                        ppar2 = kt % 3
                        cx.wait(s, "a_pchfree_%d" % ppar2)
                        sv = sm.inc(s.activation(pch[:, ppar2 * N5:(ppar2 + 1) * N5],
                                                 pS[:, spar * N5:(spar + 1) * N5], AF.Exp,
                                                 bias=negc[:, 0:1], scale=ISQ), "a_sv", 1)
                        cx.setmark("a_psfree_%d" % spar, "a_sv")
                        pending_av.append((kt, spar, ppar2, sv, kt == 0, kt == nkt - 1))
                    for av in pending_av:
                        emit_av(av)
                    v.wait_ge(sm.sem("a_mm"), sm.val("a_mm"))
                    vv = sm.inc(v.tensor_copy(linv[0:1, avpar * N5:(avpar + 1) * N5],
                                              pl[0:1, 0:N5]), "a_rt", 1)
                    plfree[0] = ("a_rt", vv)
                    v.wait_ge(sm.sem("a_rt"), vv)
                    vv = sm.inc(v.reciprocal(linv[0:1, avpar * N5:(avpar + 1) * N5],
                                             linv[0:1, avpar * N5:(avpar + 1) * N5]), "a_rt", 1)
                    pending_fin = (qm, avpar, head_et, vv)
                emit_fin(pending_fin)
                if qi == 0:
                    while vt_next[0] < NKT:
                        emit_vt(vt_next[0])
                        vt_next[0] += 1
                    t.wait_ge(sm.sem("a_vtev"), vtev_base + NKT)
                cx.setmark("a_qfree", "a_mm")
            cx.setmark("a_kvfree", "a_mm")

        t.wait_ge(sm.sem("a_aoev"), sm.val("a_aoev"))
        t.wait_ge(sm.sem("a_wold"), sm.val("a_wold"))
        for nt in range(SNT):
            for ch2 in range(8):
                slot = ch2 % 2
                for ct in range(ch2 * 2, ch2 * 2 + 2):
                    ppar = ct % 2
                    cx.wait(t, "a_psfree_%d" % ppar)
                    for et in range(ET):
                        i = t.matmul(pS[:, ppar * N5:(ppar + 1) * N5],
                                     arena[:, A_WO + (et * CT + ct) * 128:A_WO + (et * CT + ct) * 128 + 128],
                                     arena[:, A_AO + et * S + nt * N5:A_AO + et * S + (nt + 1) * N5],
                                     start=(et == 0), stop=(et == ET - 1))
                    mmv = sm.inc(i, "a_mm", 1)
                    v.wait_ge(sm.sem("a_mm"), mmv)
                    if ct % 2 == 0:
                        cx.wait(v, "a_stgfree_%d" % slot)
                    i = v.tensor_copy(stg[:, (slot * 2 + ct % 2) * N5:(slot * 2 + ct % 2 + 1) * N5],
                                      pS[:, ppar * N5:(ppar + 1) * N5])
                    sm.inc(i, "a_yev", 1)
                    cx.setmark("a_psfree_%d" % ppar, "a_yev")
                sy.wait_ge(sm.sem("a_yev"), sm.val("a_yev"))
                i = sy.dma_start(shp_dst2(ch2, nt),
                                 stg[:, slot * 2 * N5:(slot + 1) * 2 * N5].rearrange("p (a t) -> p a t", a=2))
                sm.inc(i, "a_yst", 16)
                cx.setmark("a_stgfree_%d" % slot, "a_yst")
        sy.wait_ge(sm.sem("a_yst"), sm.val("a_yst"))


def emit_ffn(cx, es, xb_sb, xbf_h, wio, y_h, scr, tag=""):
    nc, sm, t, v, s, sy, g = cx.nc, cx.sm, cx.t, cx.v, cx.s, cx.sy, cx.g
    gw_sb, = scr["gw_sb"],
    s1_h, hh_h, sho_h = scr["s1_hbm"], scr["h_hbm"], scr["shout_hbm"]
    shg_h, shu_h, shd_h = wio["shg_T"], wio["shu_T"], wio["shd_T"]
    wg_h, wu_h, wd_h = wio["wg_T"], wio["wu_T"], wio["wd_T"]
    se_sb = scr["se_sb"]
    y3 = y_h.rearrange("(ct p) t -> p ct t", p=128)
    sho3 = sho_h.rearrange("(ct p) t -> p ct t", p=128)
    s1_3 = s1_h.rearrange("(ht p) t -> ht p t", p=128)
    NTT = T // 128

    with ExitStack() as ph:
        scrr = ph.enter_context(nc.sbuf_tensor("f_scr" + tag, [128, NTT * NE], F32))
        m1 = ph.enter_context(nc.sbuf_tensor("f_m1" + tag, [128, NTT], F32))
        m2 = ph.enter_context(nc.sbuf_tensor("f_m2" + tag, [128, NTT], F32))
        tmp8 = ph.enter_context(nc.sbuf_tensor("f_tmp8" + tag, [128, NE], F32))
        ind8 = ph.enter_context(nc.sbuf_tensor("f_ind8" + tag, [128, NE], F32))
        rs = ph.enter_context(nc.sbuf_tensor("f_rs" + tag, [128, NTT], F32))
        comb = ph.enter_context(nc.sbuf_tensor("f_comb" + tag, [128, NTT * NE], F32))
        combT = ph.enter_context(nc.sbuf_tensor("f_combT" + tag, [8, T], F32))

        sgb = ph.enter_context(nc.sbuf_tensor("f_sgb" + tag, [128, N5], F32))
        cball = ph.enter_context(nc.sbuf_tensor("f_cball" + tag, [128, NT * N5], F32))
        wk = ph.enter_context(nc.sbuf_tensor("f_wk" + tag, [128, CT * N5 * 5], BF16))
        wga = wk[:, 0:2 * CT * N5]
        wub = wk[:, 2 * CT * N5:4 * CT * N5]
        hrow = ph.enter_context(nc.sbuf_tensor("f_hrow" + tag, [128, 2 * T], BF16))
        t1 = ph.enter_context(nc.sbuf_tensor("f_t1" + tag, [128, N5], BF16))
        t2 = ph.enter_context(nc.sbuf_tensor("f_t2" + tag, [128, N5], F32))
        arena = wk
        stage = ph.enter_context(nc.sbuf_tensor("f_stage" + tag, [128, 8 * N5], F32))
        shon = ph.enter_context(nc.sbuf_tensor("f_shon" + tag, [128, 8 * N5], BF16))
        pg = ph.enter_context(nc.psum_tensor("f_pg" + tag, [128, 2 * N5], F32))
        pu = ph.enter_context(nc.psum_tensor("f_pu" + tag, [128, 2 * N5], F32))
        pd = ph.enter_context(nc.psum_tensor("f_pd" + tag, [128, 2 * N5], F32))
        pb = ph.enter_context(nc.psum_tensor("f_pb" + tag, [128, N5], F32))
        pt = ph.enter_context(nc.psum_tensor("f_pt" + tag, [128, 128], F32))

        SHD, S1C = 0, CT * N5 * 4
        WD, HNT = 0, CT * N5 * 4

        xbf3 = xbf_h.rearrange("(ct p) t -> ct p t", p=128)
        for ct in range(CT):
            par = ct % 2
            cx.wait(sy, "f_xffree_%d" % par)
            xfn = "f_xfld%d" % par
            sm.inc(sy.dma_start(stage[:, par * T:par * T + T], xbf3[ct]), xfn, 16)
            t.wait_ge(sm.sem(xfn), sm.val(xfn))
            for tt in range(NTT):
                i = t.matmul(pb[:, tt * NE:(tt + 1) * NE],
                             stage[:, par * T + tt * 128: par * T + tt * 128 + 128],
                             gw_sb[:, ct * NE:(ct + 1) * NE],
                             start=(ct == 0 and tt == 0), stop=(ct == CT - 1 and tt == NTT - 1))
            mmv = sm.inc(i, "f_mm", 1)
            cx.setmark("f_xffree_%d" % par, "f_mm")
        v.wait_ge(sm.sem("f_mm"), sm.val("f_mm"))
        ev = sm.inc(v.tensor_copy(scrr[:, 0:NTT * NE], pb[:, 0:NTT * NE]), "f_ev", 1)
        t.wait_ge(sm.sem("f_ev"), ev)
        cx.setmark("f_stagefree", "f_mm")

        for nt in range(NT):
            for ct in range(CT):
                i = t.matmul(pb[0:1, 0:N5], se_sb[:, ct:ct + 1],
                             xb_sb[:, ct * T + nt * N5: ct * T + (nt + 1) * N5],
                             start=(ct == 0), stop=(ct == CT - 1))
            mmv = sm.inc(i, "f_mm", 1)
            s.wait_ge(sm.sem("f_mm"), mmv)
            sv = sm.inc(s.activation(cball[0:1, nt * N5:(nt + 1) * N5], pb[0:1, 0:N5], AF.Sigmoid), "f_sv", 1)
            t.wait_ge(sm.sem("f_sv"), sv)

        v.wait_ge(sm.sem("f_ev"), sm.val("f_ev"))
        for tt in range(NTT):
            sl = scrr[:, tt * NE:(tt + 1) * NE]
            def edge(ins):
                vv = sm.inc(ins, "f_rt", 1)
                v.wait_ge(sm.sem("f_rt"), vv)
            edge(v.reduce_max(m1[:, tt:tt + 1], sl, axis=AX.X))
            edge(v.tensor_scalar(m1[:, tt:tt + 1], m1[:, tt:tt + 1], -1.0, None, ALU.mult))
            edge(v.tensor_scalar(tmp8[:, :], sl, m1[:, tt:tt + 1], None, ALU.add))
            edge(v.tensor_scalar(ind8[:, :], tmp8[:, :], 0.0, None, ALU.is_ge))
            edge(v.scalar_tensor_tensor(t2[:, 0:NE], ind8[:, :], -1e30, tmp8[:, :], ALU.mult, ALU.add))
            edge(v.reduce_max(m2[:, tt:tt + 1], t2[:, 0:NE], axis=AX.X))
            edge(v.tensor_scalar(ind8[:, :], tmp8[:, :], m2[:, tt:tt + 1], None, ALU.is_ge))
            vv = sm.inc(v.tensor_copy(t2[:, 0:NE], tmp8[:, :]), "f_vv", 1)
            s.wait_ge(sm.sem("f_vv"), vv)
            sv = sm.inc(s.activation(tmp8[:, :], t2[:, 0:NE], AF.Exp), "f_sv", 1)
            v.wait_ge(sm.sem("f_sv"), sv)
            edge(v.tensor_tensor(tmp8[:, :], tmp8[:, :], ind8[:, :], ALU.mult))
            edge(v.reduce_sum(rs[:, tt:tt + 1], tmp8[:, :], axis=AX.X))
            edge(v.reciprocal(rs[:, tt:tt + 1], rs[:, tt:tt + 1]))
            i = v.tensor_scalar(comb[:, tt * NE:(tt + 1) * NE], tmp8[:, :], rs[:, tt:tt + 1], None, ALU.mult)
            vv = sm.inc(i, "f_vv", 1)
            t.wait_ge(sm.sem("f_vv"), vv)
            cx.wait(t, "f_ptfree")
            i = t.transpose(pt[0:NE, 0:128], comb[:, tt * NE:(tt + 1) * NE], cx.identf[:, :])
            mmv = sm.inc(i, "f_mm", 1)
            v.wait_ge(sm.sem("f_mm"), mmv)
            sm.inc(v.tensor_copy(combT[0:NE, tt * 128:(tt + 1) * 128], pt[0:NE, 0:128]), "f_ptev", 1)
            cx.setmark("f_ptfree", "f_ptev")

        def gateup(n_htile, wsrc_g, wsrc_u, hout, use_cb):
            nch = n_htile // 4
            for ch in range(nch):
                par = ch % 2
                wofs = par * CT * N5
                cx.wait(g, "f_wbfree_%d" % par)
                wname = "f_wld%d" % par
                sm.inc(g.dma_start(wga[:, wofs:wofs + CT * N5].rearrange("p (ct h) -> p ct h", ct=CT), wsrc_g(ch)), wname, 16)
                sm.inc(g.dma_start(wub[:, wofs:wofs + CT * N5].rearrange("p (ct h) -> p ct h", ct=CT), wsrc_u(ch)), wname, 16)
                t.wait_ge(sm.sem(wname), sm.val(wname))
                for ht in range(4):
                    hpar = (ch * 4 + ht) % 2
                    cx.wait(v, "f_hrfree_%d" % hpar)
                    for nt in range(NT):
                        ppar = nt % 2
                        cx.wait(t, "f_gufree_%d" % ppar)
                        for ct in range(CT):
                            t.matmul(pg[:, ppar * N5:(ppar + 1) * N5],
                                     wga[:, wofs + ct * N5 + ht * 128: wofs + ct * N5 + ht * 128 + 128],
                                     xb_sb[:, ct * T + nt * N5: ct * T + (nt + 1) * N5],
                                     start=(ct == 0), stop=(ct == CT - 1))
                        for ct in range(CT):
                            i = t.matmul(pu[:, ppar * N5:(ppar + 1) * N5],
                                         wub[:, wofs + ct * N5 + ht * 128: wofs + ct * N5 + ht * 128 + 128],
                                         xb_sb[:, ct * T + nt * N5: ct * T + (nt + 1) * N5],
                                         start=(ct == 0), stop=(ct == CT - 1))
                        mmv = sm.inc(i, "f_mm", 1)
                        s.wait_ge(sm.sem("f_mm"), mmv)
                        sv = sm.inc(s.activation(t1[:, :], pg[:, ppar * N5:(ppar + 1) * N5], AF.Silu), "f_sv", 1)
                        v.wait_ge(sm.sem("f_sv"), sv)
                        if not use_cb:
                            i = v.tensor_tensor(hrow[:, hpar * T + nt * N5: hpar * T + (nt + 1) * N5],
                                                t1[:, :], pu[:, ppar * N5:(ppar + 1) * N5], ALU.mult)
                        else:
                            vv = sm.inc(v.tensor_tensor(t2[:, :], t1[:, :], pu[:, ppar * N5:(ppar + 1) * N5], ALU.mult), "f_rt", 1)
                            v.wait_ge(sm.sem("f_rt"), vv)
                            i = v.tensor_tensor(hrow[:, hpar * T + nt * N5: hpar * T + (nt + 1) * N5],
                                                t2[:, :], cball[:, nt * N5:(nt + 1) * N5], ALU.mult)
                        sm.inc(i, "f_guev", 1)
                        cx.setmark("f_gufree_%d" % ppar, "f_guev")
                    sy.wait_ge(sm.sem("f_guev"), sm.val("f_guev"))
                    hname = "f_hst%d" % hpar
                    sm.inc(sy.dma_start(hout(ch * 4 + ht), hrow[:, hpar * T:(hpar + 1) * T]), hname, 16)
                    cx.setmark("f_hrfree_%d" % hpar, hname)
                cx.setmark("f_wbfree_%d" % par, "f_mm")

        gateup(HTS,
               lambda ch: shg_h.rearrange("(ct p) h -> p ct h", p=128)[:, :, ch * N5:(ch + 1) * N5],
               lambda ch: shu_h.rearrange("(ct p) h -> p ct h", p=128)[:, :, ch * N5:(ch + 1) * N5],
               lambda ht: s1_3[ht, :, :], False)

        g.wait_ge(sm.sem("f_mm"), sm.val("f_mm"))
        g.wait_ge(sm.sem("f_hst0"), sm.val("f_hst0"))
        g.wait_ge(sm.sem("f_hst1"), sm.val("f_hst1"))
        sm.inc(g.dma_start(arena[:, SHD:SHD + HTS * CT * 128].rearrange("p (ht ct c) -> p ht ct c", ht=HTS, ct=CT),
                           shd_h.rearrange("(ht p) (ct c) -> p ht ct c", p=128, c=128)), "f_wldA", 16)
        t.wait_ge(sm.sem("f_wldA"), sm.val("f_wldA"))
        sy.wait_ge(sm.sem("f_hst0"), sm.val("f_hst0"))
        sy.wait_ge(sm.sem("f_hst1"), sm.val("f_hst1"))
        for nt in range(NT):
            cx.wait(sy, "f_s1free")
            sm.inc(sy.dma_start(arena[:, S1C:S1C + HTS * N5].rearrange("p (ht t) -> p ht t", ht=HTS),
                                s1_h.rearrange("(ht p) t -> p ht t", p=128)[:, :, nt * N5:(nt + 1) * N5]), "f_s1ld", 16)
            cx.wait(t, "f_pbfree")
            i = t.matmul(pb[:, 0:N5], cx.ones[0:1, :], cball[0:1, nt * N5:(nt + 1) * N5], start=True, stop=True)
            mmv = sm.inc(i, "f_mm", 1)
            v.wait_ge(sm.sem("f_mm"), mmv)
            bcv = sm.inc(v.tensor_copy(sgb[:, :], pb[:, 0:N5]), "f_bc", 1)
            cx.setmark("f_pbfree", "f_bc")
            t.wait_ge(sm.sem("f_s1ld"), sm.val("f_s1ld"))
            for ct in range(CT):
                ppar = ct % 2
                cx.wait(t, "f_dnfree_%d" % ppar)
                if ct == 0:
                    t.wait_ge(sm.sem("f_bc"), bcv)
                for ht in range(HTS):
                    i = t.matmul(pd[:, ppar * N5:(ppar + 1) * N5],
                                 arena[:, SHD + (ht * CT + ct) * 128:SHD + (ht * CT + ct) * 128 + 128],
                                 arena[:, S1C + ht * N5: S1C + (ht + 1) * N5],
                                 start=(ht == 0), stop=(ht == HTS - 1))
                mmv = sm.inc(i, "f_mm", 1)
                v.wait_ge(sm.sem("f_mm"), mmv)
                v.wait_ge(sm.sem("f_bc"), bcv)
                slot2 = (ct // 4) % 2
                if ct % 4 == 0:
                    cx.wait(v, "f_shonfree_%d" % slot2)
                i = v.tensor_tensor(shon[:, (ct % 8) * N5:((ct % 8) + 1) * N5],
                                    pd[:, ppar * N5:(ppar + 1) * N5], sgb[:, :], ALU.mult)
                sm.inc(i, "f_dn", 1)
                cx.setmark("f_dnfree_%d" % ppar, "f_dn")
                if ct % 4 == 3:
                    sy.wait_ge(sm.sem("f_dn"), sm.val("f_dn"))
                    q = ct // 4
                    sname = "f_shst%d" % slot2
                    sm.inc(sy.dma_start(sho3[:, q * 4:(q + 1) * 4, nt * N5:(nt + 1) * N5],
                                        shon[:, (slot2 * 4 % 8) * N5:(slot2 * 4 % 8 + 4) * N5].rearrange("p (ct t) -> p ct t", ct=4)), sname, 16)
                    cx.setmark("f_shonfree_%d" % slot2, sname)
            cx.setmark("f_s1free", "f_dn")
        cx.setmark("f_wbfree_0", "f_mm")
        cx.setmark("f_wbfree_1", "f_mm")
        cx.setmark("f_hrfree_0", "f_mm")
        cx.setmark("f_hrfree_1", "f_mm")
        cx.setmark("f_cballfree", "f_mm")

        for e in range(NE):
            sy.wait_ge(sm.sem("f_ptev"), sm.val("f_ptev"))
            cx.wait(sy, "f_cbrowfree")
            crv = sm.inc(sy.dma_start(stage[0:1, 0:T], combT[e:e + 1, :]), "f_crld", 16)
            t.wait_ge(sm.sem("f_crld"), crv)
            for nt in range(NT):
                cx.wait(t, "f_pbfree")
                i = t.matmul(pb[:, 0:N5], cx.ones[0:1, :], stage[0:1, nt * N5:(nt + 1) * N5], start=True, stop=True)
                mmv = sm.inc(i, "f_mm", 1)
                v.wait_ge(sm.sem("f_mm"), mmv)
                cx.wait(v, "f_cballfree")
                sm.inc(v.tensor_copy(cball[:, nt * N5:(nt + 1) * N5], pb[:, 0:N5]), "f_cbev", 1)
                cx.setmark("f_pbfree", "f_cbev")
            gateup(HT,
                   lambda ch, e=e: wg_h[e].rearrange("(ct p) h -> p ct h", p=128)[:, :, ch * N5:(ch + 1) * N5],
                   lambda ch, e=e: wu_h[e].rearrange("(ct p) h -> p ct h", p=128)[:, :, ch * N5:(ch + 1) * N5],
                   lambda ht, e=e: hh_h[e].rearrange("(ht p) t -> ht p t", p=128)[ht, :, :], True)
            cx.setmark("f_cballfree", "f_guev")
            cx.setmark("f_cbrowfree", "f_mm")

        NG = 8
        EPG = 1
        WDHALF = HT * CT * 128
        sy.wait_ge(sm.sem("f_hst0"), sm.val("f_hst0"))
        sy.wait_ge(sm.sem("f_hst1"), sm.val("f_hst1"))
        g.wait_ge(sm.sem("f_mm"), sm.val("f_mm"))
        for gi in range(NG):
            gpar = gi % 2
            if gi >= 2:
                cx.wait(g, "f_bigfree_%d" % gpar)
            wldn = "f_wldB%d" % gpar
            sm.inc(g.dma_start(
                arena[:, WD + gpar * WDHALF:WD + (gpar + 1) * WDHALF].rearrange("p (ht ct c) -> p ht ct c", ht=HT, ct=CT),
                wd_h[gi].rearrange("(ht p) (ct c) -> p ht ct c", p=128, c=128)), wldn, 16)
            t.wait_ge(sm.sem(wldn), sm.val(wldn))
            for nt in range(NT):
                hpar = (gi * NT + nt) % 2
                idx = gi * NT + nt
                if idx == 0:
                    sm.inc(sy.dma_start(
                        arena[:, HNT: HNT + HT * N5].rearrange("p (ht t) -> p ht t", ht=HT),
                        hh_h[0].rearrange("(ht p) t -> p ht t", p=128)[:, :, 0:N5]), "f_hld0", 16)
                nxt = idx + 1
                if nxt < NG * NT:
                    ngi, nnt = divmod(nxt, NT)
                    npar = nxt % 2
                    cx.wait(sy, "f_hntfree_%d" % npar)
                    sm.inc(sy.dma_start(
                        arena[:, HNT + npar * HT * N5: HNT + (npar + 1) * HT * N5].rearrange("p (ht t) -> p ht t", ht=HT),
                        hh_h[ngi].rearrange("(ht p) t -> p ht t", p=128)[:, :, nnt * N5:(nnt + 1) * N5]),
                        "f_hld%d" % npar, 16)
                t.wait_ge(sm.sem("f_hld%d" % hpar), sm.val("f_hld%d" % hpar))
                if gi == NG - 1:
                    cx.wait(sy, "f_shonfree_0")
                    cx.wait(sy, "f_shonfree_1")
                    sm.inc(sy.dma_start(shon[:, :].rearrange("p (ct t) -> p ct t", ct=8),
                                        sho3[:, 0:8, nt * N5:(nt + 1) * N5]), "f_shold", 16)
                    v.wait_ge(sm.sem("f_shold"), sm.val("f_shold"))
                for ct in range(CT):
                    ppar = ct % 2
                    cx.wait(t, "f_dnfree_%d" % ppar)
                    for ht in range(HT):
                        i = t.matmul(pd[:, ppar * N5:(ppar + 1) * N5],
                                     arena[:, WD + gpar * WDHALF + ht * CT * 128 + ct * 128:WD + gpar * WDHALF + ht * CT * 128 + ct * 128 + 128],
                                     arena[:, HNT + hpar * HT * N5 + ht * N5: HNT + hpar * HT * N5 + (ht + 1) * N5],
                                     start=(ht == 0), stop=(ht == HT - 1))
                    mmv = sm.inc(i, "f_mm", 1)
                    v.wait_ge(sm.sem("f_mm"), mmv)
                    if gi == NG - 1 and ct == 8:
                        sy.wait_ge(sm.sem("f_yev"), sm.val("f_yev"))
                        sm.inc(sy.dma_start(shon[:, :].rearrange("p (ct t) -> p ct t", ct=8),
                                            sho3[:, 8:16, nt * N5:(nt + 1) * N5]), "f_shold", 16)
                        v.wait_ge(sm.sem("f_shold"), sm.val("f_shold"))
                    slot = (ct // 4) % 2
                    if ct % 4 == 0:
                        cx.wait(v, "f_stagefree_%d" % slot)
                    sl_st = stage[:, (ct % 8) * N5:((ct % 8) + 1) * N5]
                    if gi == NG - 1:
                        i = v.scalar_tensor_tensor(sl_st, shon[:, (ct % 8) * N5:((ct % 8) + 1) * N5], 1.0,
                                                   pd[:, ppar * N5:(ppar + 1) * N5], ALU.mult, ALU.add)
                    else:
                        i = v.tensor_copy(sl_st, pd[:, ppar * N5:(ppar + 1) * N5])
                    sm.inc(i, "f_yev", 1)
                    cx.setmark("f_dnfree_%d" % ppar, "f_yev")
                    if ct % 4 == 3:
                        g.wait_ge(sm.sem("f_yev"), sm.val("f_yev"))
                        q = ct // 4
                        yname = "f_yst%d" % slot
                        dst = y3[:, q * 4:(q + 1) * 4, nt * N5:(nt + 1) * N5]
                        src = stage[:, (slot * 4 % 8) * N5:(slot * 4 % 8 + 4) * N5].rearrange("p (ct t) -> p ct t", ct=4)
                        if gi == 0:
                            i = g.dma_start(dst, src)
                        else:
                            i = g.dma_start(dst, src, accum_op=ALU.add)
                        sm.inc(i, yname, 16)
                        cx.setmark("f_stagefree_%d" % slot, yname)
                        if gi == NG - 1 and ct == CT - 1:
                            cx.setmark("f_shonfree_0", yname)
                            cx.setmark("f_shonfree_1", yname)
                cx.setmark("f_hntfree_%d" % hpar, "f_mm")
            cx.setmark("f_bigfree_%d" % gpar, "f_mm")
        g.wait_ge(sm.sem("f_yst0"), sm.val("f_yst0"))
        g.wait_ge(sm.sem("f_yst1"), sm.val("f_yst1"))


def emit_flag_write(cx, sh_flag, fvals_sb, myslot_v, phase):
    sm, sy = cx.sm, cx.sy
    sm.inc(sy.dma_start(sh_flag.ap()[bass.ds(myslot_v, 1), :], fvals_sb[phase - 1:phase, :]),
           "flg", 16)
    sy.wait_ge(sm.sem("flg"), sm.val("flg"))


def emit_poll(cx, sh_flag, flagr_sb, fval_want_reg, paslot_v, regs):
    sm, sy = cx.sm, cx.sy
    it_r, nr_r, fv_r, tmp_r = regs
    sm.nofence.add("po")
    sy.reg_mov(it_r, 0)
    sy.reg_mov(nr_r, 1)
    with sy.While(nr_r):
        sy.dma_start(flagr_sb[0:1, :], sh_flag.ap()[bass.ds(paslot_v, 1), :]).then_inc(sm.sem("po"), 16)
        sy.wait_ge(sm.sem("po"), 16)
        sy.sem_clear(sm.sem("po"))
        sy.reg_load(fv_r, flagr_sb[0:1, 0:1])
        sy.reg_add(it_r, it_r, 1)
        sy.reg_sub(fv_r, fv_r, fval_want_reg)
        sy.reg_alu(nr_r, fv_r, 0, op=ALU.is_lt)
        sy.reg_alu(tmp_r, fv_r, NPH, op=ALU.is_gt)
        sy.reg_alu(nr_r, nr_r, tmp_r, op=ALU.logical_or)
        sy.reg_alu(tmp_r, it_r, POLL_BOUND, op=ALU.is_lt)
        sy.reg_alu(nr_r, nr_r, tmp_r, op=ALU.logical_and)
    sm.inc(sy.nop(), "pollok", 1)
    for eng in (cx.t, cx.v, cx.s, cx.g):
        eng.wait_ge(sm.sem("pollok"), sm.val("pollok"))


def build_program(detect_races=True, upto=9):
    nc = bass.Bass("TRN2", target_bir_lowering=False, detect_race_conditions=detect_races)
    io = {}

    def ein(name, shape, dtype=F32):
        io[name] = nc.dram_tensor(name, shape, dtype, kind="ExternalInput").ap()
        return io[name]

    x_in = ein("x_in", [C, T])
    wqkv_a = ein("wqkv_a", [C, F_A])
    wo_a = ein("wo_a", [1024, C])
    wqkv_b = ein("wqkv_b", [C, F_B])
    wo_b = ein("wo_b", [1024, C])
    gw = ein("gate_wT", [C, NE])
    sew = ein("se_T", [128, CT])
    shg = ein("shg_T", [C, HS])
    shu = ein("shu_T", [C, HS])
    shd = ein("shd_T", [HS, C])
    wg = ein("wg_T", [NE, C, H])
    wu = ein("wu_T", [NE, C, H])
    wd = ein("wd_T", [NE, H, C])
    ones_h = ein("ones128", [1, 128])
    identb_h = ein("identb", [128, 128], BF16)
    identf_h = ein("identf", [128, 128])
    masks_h = ein("masks", [4 * 128, N5], BF16)
    fvals_h = ein("fvals", [NPH, 16], I32)
    route_h = ein("route", [1, 8], I32)

    y_out = nc.dram_tensor("y_out", [C, T], F32, kind="ExternalOutput").ap()

    qkv_hbm = nc.dram_tensor("qkv_hbm", [F_B, S], BF16).ap()
    xbloc_hbm = nc.dram_tensor("xbloc_hbm", [C, T], BF16).ap()
    ypart_hbm = nc.dram_tensor("ypart_hbm", [C, S], F32).ap()
    ppin_hbm = nc.dram_tensor("ppin_hbm", [C, T], F32).ap()
    mpin_hbm = nc.dram_tensor("mpin_hbm", [C, T], F32).ap()
    x_sc = [nc.dram_tensor("x_sc%d" % i, [C, T], F32).ap() for i in range(2)]
    xbf_hbm = nc.dram_tensor("xbf_hbm", [C, T], F32).ap()
    yffn_hbm = nc.dram_tensor("yffn_hbm", [C, T], F32).ap()
    s1_hbm = nc.dram_tensor("s1_hbm", [HS, T], BF16).ap()
    h_hbm = nc.dram_tensor("h_hbm", [NE, H, T], BF16).ap()
    shout_hbm = nc.dram_tensor("shout_hbm", [C, T], BF16).ap()

    sh_xb = nc.dram_tensor("sh_xb", [C, S], BF16, addr_space="Shared")
    sh_part = nc.dram_tensor("sh_part", [2, C, T], F32, addr_space="Shared")
    sh_flag = nc.dram_tensor("sh_flag", [2, 16], I32, addr_space="Shared")

    with ExitStack() as top:
        cx = Ctx(nc)
        sm = cx.sm
        t, v, s, sy, g = cx.t, cx.v, cx.s, cx.sy, cx.g

        cx.ones = top.enter_context(nc.sbuf_tensor("k_ones", [1, 128], F32))
        cx.onesf = top.enter_context(nc.sbuf_tensor("k_onesf", [128, 1], F32))
        cx.onesc = top.enter_context(nc.sbuf_tensor("k_onesc", [128, 1], BF16))
        cx.identb = top.enter_context(nc.sbuf_tensor("k_identb", [128, 128], BF16))
        cx.identf = top.enter_context(nc.sbuf_tensor("k_identf", [128, 128], F32))
        cx.masks = top.enter_context(nc.sbuf_tensor("k_masks", [128, 4 * N5], BF16))
        gw_sb = top.enter_context(nc.sbuf_tensor("k_gw", [128, CT * NE], F32))
        se_sb = top.enter_context(nc.sbuf_tensor("k_se", [128, CT], BF16))
        fvals_sb = top.enter_context(nc.sbuf_tensor("k_fvals", [NPH, 16], I32))
        flagr_sb = top.enter_context(nc.sbuf_tensor("k_flagr", [1, 16], I32))
        route_sb = top.enter_context(nc.sbuf_tensor("k_route", [1, 8], I32))
        blk = top.enter_context(nc.Block())

        @blk.sync
        def _(_sy):
            sm.inc(sy.dma_start(cx.ones[:, :], ones_h), "k_ld", 16)
            sm.inc(g.dma_start(cx.onesf[:, :], ones_h.rearrange("o p -> p o")), "k_ld2", 16)
            sm.inc(g.dma_start(cx.onesc[:, :], ones_h.rearrange("o p -> p o")), "k_ld2", 16)
            sm.inc(sy.dma_start(cx.identb[:, :], identb_h), "k_ld", 16)
            sm.inc(sy.dma_start(cx.identf[:, :], identf_h), "k_ld", 16)
            sm.inc(sy.dma_start(cx.masks[:, :].rearrange("p (m t) -> p m t", m=4),
                                masks_h.rearrange("(m p) t -> p m t", p=128)), "k_ld", 16)
            sm.inc(sy.dma_start(gw_sb[:, :].rearrange("p (ct e) -> p ct e", ct=CT),
                                gw.rearrange("(ct p) e -> ct p e", p=128).rearrange("ct p e -> p ct e")), "k_ld", 16)
            sm.inc(g.dma_start(se_sb[:, :], sew), "k_ld2", 16)
            sm.inc(sy.dma_start(fvals_sb[:, :], fvals_h), "k_ld", 16)
            sm.inc(sy.dma_start(route_sb[:, :], route_h), "k_ld", 16)
            cx.fence()

            my_r = sy.alloc_register("my_r")
            pa_r = sy.alloc_register("pa_r")
            col_r = sy.alloc_register("col_r")
            it_r = sy.alloc_register("it_r")
            nr_r = sy.alloc_register("nr_r")
            fv_r = sy.alloc_register("fv_r")
            tmp_r = sy.alloc_register("tmp_r")
            want_r = sy.alloc_register("want_r")
            sy.reg_load(my_r, route_sb[0:1, 0:1])
            sy.reg_load(pa_r, route_sb[0:1, 1:2])
            sy.reg_load(col_r, route_sb[0:1, 2:3])
            my_v = RuntimeValue(my_r, min_val=0, max_val=1)
            pa_v = RuntimeValue(pa_r, min_val=0, max_val=1)
            col_v = RuntimeValue(col_r, min_val=0, max_val=T)
            gmy_r = g.alloc_register("gmy_r")
            gpa_r = g.alloc_register("gpa_r")
            gcol_r = g.alloc_register("gcol_r")
            gpcol_r = g.alloc_register("gpcol_r")
            g.wait_ge(sm.sem("k_ld"), sm.val("k_ld"))
            g.reg_load(gmy_r, route_sb[0:1, 0:1])
            g.reg_load(gpa_r, route_sb[0:1, 1:2])
            g.reg_load(gcol_r, route_sb[0:1, 2:3])
            g.reg_load(gpcol_r, route_sb[0:1, 3:4])
            gmy_v = RuntimeValue(gmy_r, min_val=0, max_val=1)
            gpa_v = RuntimeValue(gpa_r, min_val=0, max_val=1)
            gcol_v = RuntimeValue(gcol_r, min_val=0, max_val=T)
            gpcol_v = RuntimeValue(gpcol_r, min_val=0, max_val=T)

            x_in3 = x_in.rearrange("(ct p) t -> ct p t", p=128)
            x_sc3 = [x.rearrange("(ct p) t -> ct p t", p=128) for x in x_sc]
            xbf3 = xbf_hbm.rearrange("(ct p) t -> ct p t", p=128)
            yffn3 = yffn_hbm.rearrange("(ct p) t -> ct p t", p=128)
            yout3 = y_out.rearrange("(ct p) t -> ct p t", p=128)
            sh_xb4 = sh_xb.ap().rearrange("(ct p) t -> ct p t", p=128)

            class Rows:
                def __init__(self, fn):
                    self.fn = fn
                def __getitem__(self, ct):
                    return self.fn(ct)

            xbloc3 = xbloc_hbm.rearrange("(ct p) t -> ct p t", p=128)

            def sh_xb_dst(ct, nt):
                return xbloc3[ct][:, nt * N5:(nt + 1) * N5]

            ypart3 = ypart_hbm.rearrange("(ct p) t -> p ct t", p=128)
            ppin3 = ppin_hbm.rearrange("(ct p) t -> ct p t", p=128)
            mpin3 = mpin_hbm.rearrange("(ct p) t -> ct p t", p=128)

            def part_src(w):
                return Rows(lambda ct: (ppin3 if w == 0 else mpin3)[ct])

            def shp_dst2(ch2, nt):
                return ypart3[:, ch2 * 2:(ch2 + 1) * 2, nt * N5:(nt + 1) * N5]

            def share_xb_copy():
                g.wait_ge(sm.sem("rl_shxb0"), sm.val("rl_shxb0"))
                g.wait_ge(sm.sem("rl_shxb1"), sm.val("rl_shxb1"))
                sm.inc(g.dma_start(sh_xb.ap()[:, bass.ds(gcol_v, T)], xbloc_hbm[:, :]), "dyncp", 16)

            def share_ypart_copy():
                g.wait_ge(sm.sem("a_yst"), sm.val("a_yst"))
                sm.inc(g.dma_start(sh_part.ap()[bass.ds(gmy_v, 1), :, :],
                                   ypart_hbm[:, bass.ds(gpcol_v, T)]), "dyncp", 16)

            def fetch_partials():
                sm.inc(g.dma_start(ppin_hbm[:, :], sh_part.ap()[bass.ds(gpa_v, 1), :, :]), "dyncp", 16)
                sm.inc(g.dma_start(mpin_hbm[:, :], ypart_hbm[:, bass.ds(gcol_v, T)]), "dyncp", 16)
                sy.wait_ge(sm.sem("dyncp"), sm.val("dyncp"))

            wio = dict(shg_T=shg, shu_T=shu, shd_T=shd, wg_T=wg, wu_T=wu, wd_T=wd)
            scr = dict(gw_sb=gw_sb, se_sb=se_sb, s1_hbm=s1_hbm, h_hbm=h_hbm, shout_hbm=shout_hbm)

            def poll(phase):
                sy.reg_load(want_r, fvals_sb[phase - 1:phase, 0:1])
                emit_poll(cx, sh_flag, flagr_sb, want_r, pa_v, (it_r, nr_r, fv_r, tmp_r))

            def post(phase):
                cx.fence()
                emit_flag_write(cx, sh_flag, fvals_sb, my_v, phase)

            emit_resid_ln(cx, top, [x_in3], x_sc3[0], None, sh_xb_dst, None, do_ln=True, tag="l1")
            share_xb_copy()
            post(PH_LN1)
            cur_x = x_sc3[0]

            if upto >= 2:
                poll(PH_LN1)
                emit_attn(cx, top, UNITS_A, F_A, sh_xb4, wqkv_a, wo_a, qkv_hbm, shp_dst2, tag="a1")
                share_ypart_copy()
                post(PH_A1)

            if upto >= 3:
                poll(PH_A1)
                fetch_partials()
                with nc.sbuf_tensor("k_xbffn1", [128, CT * T], BF16) as xb_ffn:
                    emit_resid_ln(cx, top, [x_in3, part_src(0), part_src(1)], x_sc3[0],
                                  xb_ffn, None, xbf3, do_ln=True, tag="r1")
                    cx.fence()
                    if upto >= 4:
                        emit_ffn(cx, top, xb_ffn, xbf_hbm, wio, yffn_hbm, scr, tag="f1")
                    cx.fence()

            if upto >= 5:
                emit_resid_ln(cx, top, [x_sc3[0], yffn3], x_sc3[1], None, sh_xb_dst, None,
                              do_ln=True, tag="r2")
                share_xb_copy()
                post(PH_R2)
                cur_x = x_sc3[1]

            if upto >= 6:
                poll(PH_R2)
                emit_attn(cx, top, UNITS_B, F_B, sh_xb4, wqkv_b, wo_b, qkv_hbm, shp_dst2, tag="a2")
                share_ypart_copy()
                post(PH_A2)

            if upto >= 7:
                poll(PH_A2)
                fetch_partials()
                with nc.sbuf_tensor("k_xbffn2", [128, CT * T], BF16) as xb_ffn:
                    emit_resid_ln(cx, top, [x_sc3[1], part_src(0), part_src(1)], x_sc3[0],
                                  xb_ffn, None, xbf3, do_ln=True, tag="r3")
                    cx.fence()
                    cur_x = x_sc3[0]
                    if upto >= 8:
                        emit_ffn(cx, top, xb_ffn, xbf_hbm, wio, yffn_hbm, scr, tag="f2")
                    cx.fence()

            if upto >= 9:
                emit_resid_ln(cx, top, [x_sc3[0], yffn3], yout3, None, None, None,
                              do_ln=False, tag="r4")
            else:
                emit_resid_ln(cx, top, [cur_x], yout3, None, None, None,
                              do_ln=False, tag="r4")
            cx.fence()

    return nc


_NC_CACHE = {}


def kernel(**inputs):
    x = np.asarray(inputs["L0_attn_a_x"])
    wqkv_a_full = np.asarray(inputs["L0_attn_a_weight_qkv"])
    wproj_a = np.asarray(inputs["L0_attn_a_weight_proj"])
    wqkv_b_full = np.asarray(inputs["L0_attn_b_w_qkv"])
    wo_b_full = np.asarray(inputs["L0_attn_b_w_o"])
    gate_w = np.asarray(inputs["L0_ffn_gate_w"])
    w_gate = np.asarray(inputs["L0_ffn_w_gate"])
    w_up = np.asarray(inputs["L0_ffn_w_up"])
    w_down = np.asarray(inputs["L0_ffn_w_down"])
    sh_gate = np.asarray(inputs["L0_ffn_sh_gate"])
    sh_up = np.asarray(inputs["L0_ffn_sh_up"])
    sh_down = np.asarray(inputs["L0_ffn_sh_down"])
    se_gate_w = np.asarray(inputs["L0_ffn_se_gate_w"])

    if "nc" not in _NC_CACHE:
        _NC_CACHE["nc"] = build_program()
    nc = _NC_CACHE["nc"]

    masks = np.zeros((4, 128, N5), dtype=np.float32)
    for mi, o in enumerate((0, 128, 256, 384)):
        for p in range(128):
            masks[mi, p, :] = np.where(np.arange(N5) >= p + o, 0.0, -1e30)
    nonce = int(np.random.randint(1 << 20, 1 << 30))
    fvals = np.zeros((NPH, 16), dtype=np.int32)
    for k in range(NPH):
        fvals[k, 0] = nonce + k + 1

    common = {
        "gate_wT": np.ascontiguousarray(gate_w.T),
        "se_T": np.ascontiguousarray(se_gate_w.reshape(CT, 128).T),
        "shg_T": np.ascontiguousarray(sh_gate.T),
        "shu_T": np.ascontiguousarray(sh_up.T),
        "shd_T": np.ascontiguousarray(sh_down.T),
        "wg_T": np.ascontiguousarray(np.transpose(w_gate, (0, 2, 1))),
        "wu_T": np.ascontiguousarray(np.transpose(w_up, (0, 2, 1))),
        "wd_T": np.ascontiguousarray(np.transpose(w_down, (0, 2, 1))),
        "ones128": np.ones((1, 128), dtype=np.float32),
        "identb": np.eye(128, dtype=ml_dtypes.bfloat16),
        "identf": np.eye(128, dtype=np.float32),
        "masks": masks.reshape(4 * 128, N5).astype(ml_dtypes.bfloat16),
        "fvals": fvals,
    }

    wpa_T = wproj_a.T
    wob_T = wo_b_full.T

    in_maps = []
    for c in range(8):
        b, m = c >> 1, c & 1
        ra = np.r_[2 * m * 768:(2 * m + 2) * 768]
        rb = np.concatenate([np.r_[h * 128:(h + 1) * 128,
                                   2048 + h * 128:2048 + (h + 1) * 128,
                                   4096 + h * 128:4096 + (h + 1) * 128]
                             for h in range(8 * m, 8 * m + 8)])
        im = dict(common)
        im["x_in"] = np.ascontiguousarray(x[b, m * T:(m + 1) * T, :].T)
        im["wqkv_a"] = np.ascontiguousarray(wqkv_a_full[ra].T)
        im["wo_a"] = np.ascontiguousarray(wpa_T[m * 1024:(m + 1) * 1024])
        im["wqkv_b"] = np.ascontiguousarray(wqkv_b_full[rb].T)
        im["wo_b"] = np.ascontiguousarray(wob_T[m * 1024:(m + 1) * 1024])
        im["route"] = np.array([[m, 1 - m, m * T, (1 - m) * T, 0, 0, 0, 0]], dtype=np.int32)
        in_maps.append(im)

    res = run_bass_kernel_spmd(nc, in_maps, core_ids=list(range(8)))

    out = np.empty((B, S, C), dtype=np.float32)
    for c in range(8):
        b, m = c >> 1, c & 1
        out[b, m * T:(m + 1) * T, :] = res.results[c]["y_out"].T
    return out
